# version 05
# baseline (speedup 1.0000x reference)
import numpy as np
import concourse.bass as bass
import concourse.mybir as mybir
from concourse.bass_utils import run_bass_kernel_spmd

F32 = mybir.dt.float32
F32R = mybir.dt.float32r
BF16 = mybir.dt.bfloat16
I32 = mybir.dt.int32
AF = mybir.ActivationFunctionType
ALU = mybir.AluOpType
AX = mybir.AxisListType

MAGIC = float(np.float32(12582912.0))
MASK11 = -4096
EXPMASK = 0x7F800000

ENGINES = ("pe", "dve", "act", "pool", "sp")



class TV:
    __slots__ = ("ap", "key", "p0", "p1", "b0", "b1")

    def __init__(self, ap, key, p0, p1, b0, b1):
        self.ap, self.key = ap, key
        self.p0, self.p1, self.b0, self.b1 = p0, p1, b0, b1

    def overlaps(self, o):
        return (self.key == o.key and self.p0 < o.p1 and o.p0 < self.p1
                and self.b0 < o.b1 and o.b0 < self.b1)

    def covers(self, o):
        return (self.key == o.key and self.p0 <= o.p0 and self.p1 >= o.p1
                and self.b0 <= o.b0 and self.b1 >= o.b1)

    def r32(self):
        return TV(self.ap.bitcast(F32R), self.key, self.p0, self.p1,
                  self.b0, self.b1)

    def i32(self):
        return TV(self.ap.bitcast(I32), self.key, self.p0, self.p1,
                  self.b0, self.b1)


def _srange(idx, n):
    if isinstance(idx, slice):
        start, stop, step = idx.indices(n)
        assert step == 1
        return start, stop
    i = int(idx)
    return i, i + 1


class Buf:

    def __init__(self, base_ap, key, shape, dtype, space):
        self.base = base_ap
        self.key = key
        self.shape = tuple(shape)
        self.dtype = dtype
        self.space = space
        self.esize = mybir.dt.size(dtype)
        strides = []
        acc = 1
        for s in reversed(self.shape[1:]):
            strides.append(acc)
            acc *= s
        self.free_strides = tuple(reversed(strides))
        self.free_total = acc

    def cast(self, dtype, shape=None):
        if shape is None:
            f = self.esize // mybir.dt.size(dtype)
            if f >= 1:
                shape = self.shape[:-1] + (self.shape[-1] * f,)
            else:
                k = mybir.dt.size(dtype) // self.esize
                assert self.shape[-1] % k == 0
                shape = self.shape[:-1] + (self.shape[-1] // k,)
        ap = self.base.bitcast(dtype)
        b = Buf(ap, self.key, shape, dtype, self.space)
        return b

    def reshape(self, shape):
        total = 1
        for s in shape[1:]:
            total *= s
        assert shape[0] == self.shape[0] and total == self.free_total, \
            (shape, self.shape)
        assert len(self.shape) == 2, "reshape only from 2D"
        if len(shape) == 2:
            return self
        names = [f"b{i}" for i in range(len(shape) - 1)]
        pat = "p (" + " ".join(names) + ") -> p " + " ".join(names)
        kw = {n: s for n, s in zip(names, shape[1:])}
        ap = self.base.rearrange(pat, **kw)
        return Buf(ap, self.key, shape, self.dtype, self.space)

    def __getitem__(self, idx):
        if not isinstance(idx, tuple):
            idx = (idx,)
        idx = idx + (slice(None),) * (len(self.shape) - len(idx))
        p0, p1 = _srange(idx[0], self.shape[0])
        b_lo, b_hi = 0, 1
        for i, ix in enumerate(idx[1:]):
            s0, s1 = _srange(ix, self.shape[1 + i])
            b_lo += s0 * self.free_strides[i]
            b_hi += (s1 - 1) * self.free_strides[i]
        ap = self.base[idx]
        if self.space == "dram":
            lo = (p0 * self.free_total + b_lo) * self.esize
            hi = ((p1 - 1) * self.free_total + b_hi) * self.esize
            return TV(ap, self.key, 0, 1, lo, hi)
        return TV(ap, self.key, p0, p1, b_lo * self.esize, b_hi * self.esize)

    def full(self):
        return self[(slice(None),) * len(self.shape)]


class Op:
    __slots__ = ("eng", "fn", "reads", "writes", "idx", "inc", "waits",
                 "tag", "n_inst")

    def __init__(self, eng, fn, reads, writes, idx, inc, tag, n_inst=1):
        self.eng, self.fn = eng, fn
        self.reads, self.writes = reads, writes
        self.idx, self.inc = idx, inc
        self.waits = []
        self.tag = tag
        self.n_inst = n_inst


class Prog:
    ANNOTATE = False
    """Records ops with dependency tracking; emits a raw-Bass Block."""

    def __init__(self, nc):
        self.nc = nc
        self.ops = {e: [] for e in ENGINES}
        self.sems = {}
        for e in ENGINES:
            self.sems["e_" + e] = nc.alloc_semaphore(name="e_" + e)
        for q in ("sp", "act", "pool"):
            self.sems["q_" + q] = nc.alloc_semaphore(name="q_" + q)
        self.counts = {e: 0 for e in ENGINES}
        self.qcounts = {q: 0 for q in ("sp", "act", "pool")}
        self.mem = {}
        self.sbuf_bytes = 0
        self.n_inst = 0
        self._fconsts = {}
        self._fconst_buf = None
        self.qbounds = {q: [] for q in ("sp", "act", "pool")}
        self.qbound_starts = {q: set() for q in ("sp", "act", "pool")}
        self.pe_meta = []

    def float_const(self, val):
        if val in self._fconsts:
            return self._fconsts[val]
        if self._fconst_buf is None:
            self._fconst_buf = self.sbuf("fconst", (128, 8), F32)
        i = len(self._fconsts)
        assert i < 8
        tv = self._fconst_buf[:, i:i + 1]
        self.memset("pool", tv, val, tag=f"fconst {val}")
        self.nc.const_aps.register(F32, val, tv.ap)
        self._fconsts[val] = tv
        return tv

    def sbuf(self, name, shape, dtype):
        h = self.nc.alloc_sbuf_tensor(name, list(shape), dtype)
        b = Buf(h.ap(), name, shape, dtype, "sbuf")
        n = 1
        for s in shape[1:]:
            n *= s
        self.sbuf_bytes += n * mybir.dt.size(dtype)
        return b

    def psum(self, name, shape=(128, 512), dtype=F32):
        h = self.nc.alloc_psum_tensor(name, list(shape), dtype)
        return Buf(h.ap(), name, shape, dtype, "psum")

    def dram(self, name, shape, dtype, kind="Internal"):
        h = self.nc.dram_tensor(name, list(shape), dtype, kind=kind)
        return Buf(h.ap(), name, shape, dtype, "dram")

    def _record(self, eng, fn, reads, writes, queue=None, tag="", n_inst=1):
        reads = [r for r in reads if r is not None]
        writes = [w for w in writes if w is not None]
        if queue is None:
            src = ("e", eng, self.counts[eng])
            self.counts[eng] += 1
            inc = ("e_" + eng, 1)
        else:
            src = ("q", queue, self.qcounts[queue])
            self.qcounts[queue] += 1
            inc = ("q_" + queue, 16)
        if eng in ("act", "dve"):
            promoted = [r for r in reads if r.key.startswith("pb")]
            if promoted:
                writes = writes + promoted
        op = Op(eng, fn, reads, writes, src[2], inc, tag, n_inst)
        self.n_inst += n_inst

        deps = []
        for r in reads:
            lst = self.mem.get(r.key)
            if lst:
                deps += [s for tv, s in lst["w"] if tv.overlaps(r)]
        for w in writes:
            lst = self.mem.get(w.key)
            if lst:
                deps += [s for tv, s in lst["w"] if tv.overlaps(w)]
                deps += [s for tv, s in lst["r"] if tv.overlaps(w)]
        waits = {}
        for d in deps:
            if d == src:
                continue
            if (d[0] == "e" and src[0] == "e" and d[1] == src[1] == "pe"):
                continue
            sem = ("e_" if d[0] == "e" else "q_") + d[1]
            if d[0] == "e":
                val = d[2] + 1
            else:
                q, j = d[1], d[2]
                b = None
                for t in self.qbounds[q]:
                    if t >= j + 1:
                        b = t
                        break
                if b is None:
                    b = self.qcounts[q]
                    if src == ("q", q, b):
                        self.qbound_starts[q].add(b)
                        self.qbounds[q].append(b)
                    else:
                        self.qbounds[q].append(b)
                        self.qbound_starts[q].add(b)
                val = 16 * b
            waits[sem] = max(waits.get(sem, 0), val)
        op.waits = sorted(waits.items())

        for w in writes:
            lst = self.mem.setdefault(w.key, {"w": [], "r": []})
            lst["w"] = [(tv, s) for (tv, s) in lst["w"] if not w.covers(tv)]
            lst["r"] = [(tv, s) for (tv, s) in lst["r"] if not w.covers(tv)]
            lst["w"].append((w, src))
        for r in reads:
            lst = self.mem.setdefault(r.key, {"w": [], "r": []})
            lst["r"].append((r, src))
        self.ops[eng].append(op)
        return op

    def dma(self, eng, out_tv, in_tv, tag=""):
        def fn(e):
            return e.dma_start(out_tv.ap, in_tv.ap)
        return self._record(eng, fn, [in_tv], [out_tv], queue=eng, tag=tag)

    def memset(self, eng, out_tv, val, tag=""):
        def fn(e):
            return e.memset(out_tv.ap, val)
        return self._record(eng, fn, [], [out_tv], tag=tag)

    def mm_group(self, pairs, out_tv, tag="", start=True, stop=True):
        n = len(pairs)

        def fn(e):
            last = None
            for i, (l, r) in enumerate(pairs):
                last = e.matmul(out_tv.ap, l.ap, r.ap,
                                start=(start and i == 0),
                                stop=(stop and i == n - 1))
            return last
        reads = []
        for l, r in pairs:
            reads += [l, r]
        self.pe_meta.append([("M", tuple(l.ap.shape), tuple(r.ap.shape))
                             for l, r in pairs])
        return self._record("pe", fn, reads, [out_tv], tag=tag, n_inst=n)

    def transpose(self, out_psum_tv, in_tv, ident_tv, tag=""):
        def fn(e):
            return e.transpose(out_psum_tv.ap, in_tv.ap, ident_tv.ap)
        self.pe_meta.append([("T", tuple(in_tv.ap.shape))])
        return self._record("pe", fn, [in_tv, ident_tv], [out_psum_tv], tag=tag)

    def act(self, out_tv, in_tv, func, bias=0.0, scale=1.0, tag="",
            extra=()):
        extra = list(extra)
        kb = bias
        if isinstance(bias, TV):
            kb = bias.ap
            extra.append(bias)
        elif float(bias) != 0.0:
            extra.append(self.float_const(float(bias)))
        ks = scale
        if isinstance(scale, TV):
            ks = scale.ap
            extra.append(scale)
        elif float(scale) != 1.0:
            extra.append(self.float_const(float(scale)))

        def fn(e):
            return e.activation(out_tv.ap, in_tv.ap, func, bias=kb, scale=ks)
        return self._record("act", fn, [in_tv] + extra, [out_tv], tag=tag)

    def tscalar(self, eng, out_tv, in_tv, s1, s2, op0, op1=None, tag="",
                extra=()):
        extra = list(extra)
        a1 = s1
        if isinstance(s1, TV):
            a1 = s1.ap
            extra.append(s1)
        a2 = s2
        if isinstance(s2, TV):
            a2 = s2.ap
            extra.append(s2)

        def fn(e):
            if op1 is None:
                return e.tensor_scalar(out_tv.ap, in_tv.ap, a1, a2, op0)
            return e.tensor_scalar(out_tv.ap, in_tv.ap, a1, a2, op0, op1)
        return self._record(eng, fn, [in_tv] + extra, [out_tv], tag=tag)

    def tt(self, eng, out_tv, in0_tv, in1_tv, op, tag=""):
        def fn(e):
            return e.tensor_tensor(out_tv.ap, in0_tv.ap, in1_tv.ap, op)
        return self._record(eng, fn, [in0_tv, in1_tv], [out_tv], tag=tag)

    def stt(self, eng, out_tv, in0_tv, scalar, in1_tv, op0, op1, tag=""):
        extra = []
        sc = scalar
        if isinstance(scalar, TV):
            sc = scalar.ap
            extra.append(scalar)

        def fn(e):
            return e.scalar_tensor_tensor(out_tv.ap, in0_tv.ap, sc, in1_tv.ap,
                                          op0, op1)
        return self._record(eng, fn, [in0_tv, in1_tv] + extra, [out_tv],
                            tag=tag)

    def reduce(self, eng, out_tv, in_tv, op, axis=AX.X, absval=None,
               negate=None, tag=""):
        def fn(e):
            return e.tensor_reduce(out_tv.ap, in_tv.ap, axis, op,
                                   apply_absolute_value=absval, negate=negate)
        return self._record(eng, fn, [in_tv], [out_tv], tag=tag)

    def scan(self, eng, out_tv, d0_tv, d1_tv, init, op0, op1, tag=""):
        def fn(e):
            return e.tensor_tensor_scan(out_tv.ap, d0_tv.ap, d1_tv.ap, init,
                                        op0, op1)
        return self._record(eng, fn, [d0_tv, d1_tv], [out_tv], tag=tag)

    def recip(self, out_tv, in_tv, tag=""):
        def fn(e):
            return e.reciprocal(out_tv.ap, in_tv.ap)
        return self._record("dve", fn, [in_tv], [out_tv], tag=tag)

    def copy(self, eng, out_tv, in_tv, tag="", extra=()):
        if eng == "act":
            def fn(e):
                return e.copy(out_tv.ap, in_tv.ap)
        else:
            def fn(e):
                return e.tensor_copy(out_tv.ap, in_tv.ap)
        return self._record(eng, fn, [in_tv] + list(extra), [out_tv],
                            tag=tag)

    def select(self, out_tv, mask_tv, t_tv, f_tv, tag=""):
        def fn(e):
            return e.select(out_tv.ap, mask_tv.ap, t_tv.ap, f_tv.ap)
        return self._record("dve", fn, [mask_tv, t_tv, f_tv], [out_tv],
                            tag=tag, n_inst=2)

    def emit(self):
        nc = self.nc
        sem_handles = self.sems

        def replay(eng_name):
            qb = self.qbound_starts.get(eng_name, set())

            def body(e):
                waited = {}
                for op in self.ops[eng_name]:
                    if op.inc[0].startswith("q_") and op.idx in qb:
                        sem = "q_" + eng_name
                        if waited.get(sem, 0) < 16 * op.idx:
                            e.wait_ge(sem_handles[sem], 16 * op.idx)
                            waited[sem] = 16 * op.idx
                    for sem, val in op.waits:
                        if waited.get(sem, 0) < val:
                            e.wait_ge(sem_handles[sem], val)
                            waited[sem] = val
                    inst = op.fn(e)
                    if Prog.ANNOTATE:
                        inst.annotate(op.tag)
                    sem, amt = op.inc
                    inst.then_inc(sem_handles[sem], amt)
                if eng_name == "sp":
                    for q, cnt in self.qcounts.items():
                        if cnt:
                            e.wait_ge(sem_handles["q_" + q], 16 * cnt)
            return body

        with nc.Block() as block:
            block.tensor(replay("pe"))
            block.vector(replay("dve"))
            block.scalar(replay("act"))
            block.gpsimd(replay("pool"))
            block.sync(replay("sp"))



class Ctx:

    def __init__(self, P: Prog, host_consts: dict):
        self.P = P
        self.consts = {}
        self.const_arrays = dict(host_consts)
        for name, arr in host_consts.items():
            dt = {np.dtype(np.float32): F32,
                  np.dtype(np.int32): I32}[arr.dtype]
            self.consts[name] = P.dram("c_" + name, arr.shape, dt,
                                       kind="ExternalInput")
        self.pb = [P.psum(f"pb{i}") for i in range(8)]
        self.S = [P.sbuf(f"ar_s{i}", (128, 16), F32) for i in range(8)]
        self.R = [P.sbuf(f"ar_r{i}", (1, 128 if i < 2 else 512), F32)
                  for i in range(4)]
        self.reg = Reg(P)
        self._sbuf_consts = {}

    SCRATCH_MB = {"s_g1": 72, "s_g2": 72, "s_t1": 72}

    def scratch(self, name, shape, dtype):
        if not hasattr(self, "_scratch"):
            self._scratch = {}
        n = 1
        for s in shape:
            n *= s
        nbytes = n * mybir.dt.size(dtype)
        if name not in self._scratch:
            mb = self.SCRATCH_MB.get(name, 20)
            h = self.P.dram("zz_" + name, (mb * 1024 * 1024,),
                            mybir.dt.uint8)
            self._scratch[name] = h
        h = self._scratch[name]
        assert nbytes <= h.shape[0], (name, shape, dtype)
        ap = h.base.bitcast(dtype)[0:n]
        names = [f"b{i}" for i in range(len(shape))]
        pat = "(" + " ".join(names) + ") -> " + " ".join(names)
        kw = {nm: s for nm, s in zip(names, shape)}
        ap = ap.rearrange(pat, **kw)
        return Buf(ap, h.key, shape, dtype, "dram")

    def sconst(self, name, rows, cols, dtype=None):
        if name in self._sbuf_consts:
            return self._sbuf_consts[name]
        src = self.consts[name]
        dt = dtype or src.dtype
        b = self.P.sbuf("sc_" + name, (rows, cols), dt)
        q = "sp" if dt == src.dtype else "pool"
        self.P.dma(q, b.full(), src.full(), tag=f"const {name}")
        self._sbuf_consts[name] = b
        return b


def std_consts(T=2048):
    c = {}
    c["ident"] = np.eye(128, dtype=np.float32)
    c["ident_bf"] = np.eye(128, dtype=np.float32)
    iu = np.triu(np.ones((128, 128), np.float32), 0)
    c["mask_triu"] = iu
    c["mask_sl"] = np.tril(np.ones((128, 128), np.float32), -1)
    c["ones_row"] = np.ones((1, 512), np.float32)
    dec = np.exp(-np.arange(T, dtype=np.float64) / T).astype(np.float32)
    c["decay"] = np.ascontiguousarray(dec.reshape(T // 128, 128).T)
    inv_freq = (1.0 / 10000.0 ** (np.arange(0, 128, 2, dtype=np.float32)
                                  / 128.0))
    ang = np.arange(T, dtype=np.float32)[:, None] * inv_freq[None, :]
    c["ropec"] = np.cos(ang).astype(np.float32)
    c["ropes"] = np.sin(ang).astype(np.float32)
    return c




def kt4_shape(T, K):
    return (T // 128, 128, K // 128, 128)


def wslab_shape(K, N, nslab=512):
    return (N // nslab, 128, K // 128, nslab)


def host_wslab(w, nslab=512):
    K, N = w.shape
    return np.ascontiguousarray(
        w.reshape(K // 128, 128, N // nslab, nslab).transpose(2, 1, 0, 3))


RESERVE_KB = {
    "gwR": 61, "gwB": 16,
    "gxw": 32, "gxR0": 2, "gxR1": 2, "gxB": 24,
    "stg": 8,
    "lnx": 16,
    "tsa": 2, "tsb": 2, "tsc": 2, "tsd": 2,
    "rex0": 2, "rex1": 2, "reo0": 2, "reo1": 2,
    "st0": 2, "st1": 2,
    "msc0": 8, "msc1": 8,
}

GW_BASE = 11392


class Reg:

    def __init__(self, P):
        self.P = P
        self.raw = {}

    def _raw(self, name):
        if name not in self.raw:
            kb = RESERVE_KB[name]
            self.raw[name] = self.P.sbuf("rg_" + name, (128, kb * 256), F32)
        return self.raw[name]

    def get(self, name, shape, dtype=F32, off=0):
        b = self._raw(name)
        n = 1
        for s in shape[1:]:
            n *= s
        es = mybir.dt.size(dtype)
        cap = b.free_total * 4 // es
        assert off + n <= cap, (name, shape, dtype, off, cap)
        ap = b.base.bitcast(dtype)[0:shape[0], off:off + n]
        v = Buf(ap, b.key, (shape[0], n), dtype, "sbuf")
        if len(shape) > 2:
            v = v.reshape(shape)
        return _OffsetBuf(v, off * es)


class _OffsetBuf:

    def __init__(self, buf, byte_off):
        self._b = buf
        self._o = byte_off
        self.shape = buf.shape
        self.dtype = buf.dtype
        self.free_total = buf.free_total

    def __getitem__(self, idx):
        tv = self._b[idx]
        return TV(tv.ap, tv.key, tv.p0, tv.p1, tv.b0 + self._o,
                  tv.b1 + self._o)

    def full(self):
        return self[(slice(None),) * len(self.shape)]




def kt4_block(buf, tt, k0=0, nk=None):
    nkall = buf.shape[2]
    if nk is None:
        nk = nkall
    ap = buf.base[tt].rearrange("p c t -> p (c t)")
    if k0 or nk != nkall:
        ap = ap[:, k0 * 128:(k0 + nk) * 128]
    blk = buf.free_total
    es = buf.esize
    base_off = tt * 128 * nkall * 128 * es
    return TV(ap, buf.key, 0, 1, base_off + k0 * 128 * es,
              base_off + (nkall * 127 + nk) * 128 * es + 128 * es)


def wslab_block(buf, ns, c0=0, cn=None):
    _, _, kall, nslab = buf.shape
    if cn is None:
        cn = kall
    ap = buf.base[ns].rearrange("p c s -> p (c s)")
    if c0 or cn != kall:
        ap = ap[:, c0 * nslab:(c0 + cn) * nslab]
    es = buf.esize
    base = ns * 128 * kall * nslab * es
    return TV(ap, buf.key, 0, 1, base, base + 128 * kall * nslab * es)


def emit_gemm(ctx, x_src, w_src, out_cb, T, K, N, *, wprep, nslab=512,
              banks=(2, 3, 4, 5, 6, 7), wcg=2, extra_pairs=None,
              kgroup=None, tag="g"):
    P, reg = ctx.P, ctx.reg
    nt, nsl = T // 128, N // nslab
    kall = w_src.shape[2]
    k0, nk = kgroup if kgroup is not None else (0, kall)
    assert nk * 128 <= 2048 * 128 and nk <= 16
    G = len(banks)
    xbf = (x_src.dtype == BF16)
    ncg = (nk + wcg - 1) // wcg
    f32rW = wprep == "f32r_split"
    intW = wprep == "int_f32r"
    assert f32rW or intW or wprep in ("bf16", "raw")
    assert xbf == (wprep in ("bf16", "raw"))
    wq_ct, ws_ct, xs_ct = [0], [0], [0]
    WQ = ("act", "sp", "pool")

    for g0 in range(0, nt, G):
        tts = list(range(g0, min(g0 + G, nt)))
        xs = []
        for ti, tt in enumerate(tts):
            if xbf:
                xb = reg.get("gxB", (128, nk, 128), BF16, off=ti * 2048)
            elif ti < 4:
                xb = reg.get("gxw", (128, nk, 128), F32, off=ti * 2048)
            else:
                xb = reg.get("gxB", (128, nk, 128), F32,
                             off=(ti - 4) * 2048)
            dst = xb.full()
            dst2 = TV(dst.ap.rearrange("p c t -> p (c t)"), dst.key,
                      dst.p0, dst.p1, dst.b0, dst.b1)
            P.dma("sp", dst2, kt4_block(x_src, tt, k0, nk),
                  tag=f"{tag} x{tt}")
            xs.append(xb)
        for ns in range(nsl):
            for ci in range(ncg):
                c0, cn = ci * wcg, min(wcg, nk - ci * wcg)
                sl = ws_ct[0] % 2
                ws_ct[0] += 1
                wsrc = wslab_block(w_src, ns, k0 + c0, cn)
                qn = WQ[wq_ct[0] % 3]
                wq_ct[0] += 1
                if wprep == "raw":
                    wb = reg.get("gwB", (128, cn, nslab), BF16,
                                 off=6144 + sl * wcg * nslab)
                    wtv = wb.full()
                    wtv2 = TV(wtv.ap.rearrange("p c s -> p (c s)"), wtv.key,
                              wtv.p0, wtv.p1, wtv.b0, wtv.b1)
                    P.dma(qn, wtv2, wsrc, tag=f"{tag} W{ns}.{ci}")
                    wparts = [wb]
                else:
                    sdt = I32 if intW else F32
                    st = reg.get("stg", (128, cn * nslab), sdt,
                                 off=sl * wcg * nslab)
                    P.dma(qn, st.full(), wsrc, tag=f"{tag} W{ns}.{ci}")
                    st3 = reg.get("stg", (128, cn, nslab), sdt,
                                  off=sl * wcg * nslab)
                    if f32rW:
                        w1 = reg.get("gwR", (128, cn, nslab), F32R,
                                     off=GW_BASE + sl * 2 * wcg * nslab)
                        w2 = reg.get("gwR", (128, cn, nslab), F32R,
                                     off=GW_BASE + sl * 2 * wcg * nslab
                                     + wcg * nslab)
                        P.copy("act", w1.full(), st.full(),
                               tag=f"{tag} ws1")
                        P.tt("pool", w2.full(), st.full(), w1.full(),
                             ALU.subtract, tag=f"{tag} ws2")
                        wparts = [w1, w2]
                    elif intW:
                        wc = reg.get("gwR", (128, cn, nslab), F32R,
                                     off=GW_BASE + sl * wcg * nslab)
                        P.tscalar("dve", wc.full(), st.full(), 128.0, None,
                                  ALU.subtract, tag=f"{tag} wi")
                        wparts = [wc]
                    else:
                        wb = reg.get("gwB", (128, cn, nslab), BF16,
                                     off=6144 + sl * wcg * nslab)
                        P.copy("dve", wb.full(), st.full(),
                               tag=f"{tag} wc")
                        wparts = [wb]
                for ti, tt in enumerate(tts):
                    ps = ctx.pb[banks[ti]][:, 0:nslab]
                    pairs = []
                    if xbf:
                        for c in range(cn):
                            pairs.append((xs[ti][:, c0 + c, :],
                                          wparts[0][:, c, :]))
                    else:
                        xsl = xs_ct[0] % 2
                        par = xs_ct[0] % 2
                        xs_ct[0] += 1
                        xraw = xs[ti][:, c0:c0 + cn, :]
                        xraw = TV(xraw.ap.rearrange("p c t -> p (c t)"),
                                  xraw.key, xraw.p0, xraw.p1, xraw.b0,
                                  xraw.b1)
                        x1t = reg.get("gxR0", (128, cn, 128), F32R,
                                      off=xsl * wcg * 128)
                        P.copy(("act", "dve")[par], x1t.full(), xraw,
                               tag=f"{tag} xs1")
                        x2t = None
                        if f32rW or intW:
                            x2t = reg.get("gxR1", (128, cn, 128), F32R,
                                          off=xsl * wcg * 128)
                            P.tt(("dve", "pool")[par], x2t.full(), xraw,
                                 x1t.full(), ALU.subtract,
                                 tag=f"{tag} xs2")
                        for c in range(cn):
                            if f32rW:
                                pairs += [(x1t[:, c, :], wparts[0][:, c, :]),
                                          (x2t[:, c, :], wparts[0][:, c, :]),
                                          (x1t[:, c, :], wparts[1][:, c, :])]
                            elif intW:
                                pairs += [(x1t[:, c, :], wparts[0][:, c, :]),
                                          (x2t[:, c, :], wparts[0][:, c, :])]
                            else:
                                pairs.append((x1t[:, c, :],
                                              wparts[0][:, c, :]))
                    stop = (ci == ncg - 1)
                    if stop and extra_pairs is not None:
                        pairs = pairs + list(extra_pairs(tt, ns))
                    P.mm_group(pairs, ps, start=(ci == 0), stop=stop,
                               tag=f"{tag} mm{tt}.{ns}")
                    if stop:
                        out_cb(ps, tt, ns)


def _fm_outs(ctx, pv, tt, j0, jn, dst_split, dst_f32, dst_bf16,
             dst_bf163, tag):
    P = ctx.P
    if dst_split is not None:
        d1, d2 = dst_split
        o1 = ctx.reg.get("tsa", (128, jn * 128), F32R)
        o2 = ctx.reg.get("tsb", (128, jn * 128), F32R)
        P.copy("dve", o1.full(), pv, tag=f"{tag} s1")
        P.tt("dve", o2.full(), pv, o1.full(), ALU.subtract, tag=f"{tag} s2")
        P.dma("pool", kt4_block(d1, tt, j0, jn), o1.full(), tag=f"{tag} w1")
        P.dma("pool", kt4_block(d2, tt, j0, jn), o2.full(), tag=f"{tag} w2")
    prev = []
    if dst_split is not None:
        prev = [ctx.reg.get("tsb", (128, jn * 128), F32R).full()]
    if dst_f32 is not None:
        o3 = ctx.reg.get("tsc", (128, jn * 128), F32)
        P.copy("act", o3.full(), pv, tag=f"{tag} cf", extra=prev)
        prev = [o3.full()]
        P.dma("act", kt4_block(dst_f32, tt, j0, jn), o3.full(),
              tag=f"{tag} wf")
    if dst_bf16 is not None:
        o4 = ctx.reg.get("tsd", (128, jn * 128), BF16)
        P.copy("act", o4.full(), pv, tag=f"{tag} cb", extra=prev)
        P.dma("act", kt4_block(dst_bf16, tt, j0, jn), o4.full(),
              tag=f"{tag} wb")
    if dst_bf163 is not None:
        dh, dm, dl = dst_bf163
        hi = ctx.reg.get("tsa", (128, jn * 128), BF16)
        r = ctx.reg.get("tsb", (128, jn * 128), F32)
        mid = ctx.reg.get("tsc", (128, jn * 128), BF16)
        lo = ctx.reg.get("tsd", (128, jn * 128), BF16)
        P.copy("dve", hi.full(), pv, tag=f"{tag} b3h")
        P.tt("dve", r.full(), pv, hi.full(), ALU.subtract, tag=f"{tag} b3r")
        P.copy("dve", mid.full(), r.full(), tag=f"{tag} b3m")
        P.tt("dve", lo.full(), r.full(), mid.full(), ALU.subtract,
             tag=f"{tag} b3l")
        P.dma("pool", kt4_block(dh, tt, j0, jn), hi.full(), tag=f"{tag} w3h")
        P.dma("pool", kt4_block(dm, tt, j0, jn), mid.full(), tag=f"{tag} w3m")
        P.dma("pool", kt4_block(dl, tt, j0, jn), lo.full(), tag=f"{tag} w3l")


def emit_ln(ctx, src, T, D, dst_split=None, dst_f32=None, dst_bf16=None,
            dst_bf163=None, dst_tm=None, eps=1e-5, tag="ln"):
    P = ctx.P
    nt, nk = T // 128, D // 128
    ident = ctx.sconst("ident", 128, 128)
    want_fm = (dst_split is not None or dst_f32 is not None
               or dst_bf16 is not None or dst_bf163 is not None)
    for tt in range(nt):
        xb = ctx.reg.get("lnx", (128, D), F32, off=0)
        ln = ctx.reg.get("lnx", (128, D), F32, off=D)
        sq = ctx.reg.get("lnx", (128, D), F32, off=0)
        xt = xb.full()
        P.dma("sp", xt, src[tt * 128:(tt + 1) * 128, 0:D], tag=f"{tag} x{tt}")
        s = ctx.S
        mean, var, tmp, rstd, t2 = (s[0][:, 0:1], s[1][:, 0:1], s[2][:, 0:1],
                                    s[3][:, 0:1], s[4][:, 0:1])
        P.reduce("dve", mean, xt, ALU.add, tag=f"{tag} sum")
        P.tscalar("dve", mean, mean, 1.0 / D, None, ALU.mult,
                  tag=f"{tag} mean")
        P.tscalar("dve", ln.full(), xt, mean, None, ALU.subtract,
                  tag=f"{tag} ctr")
        P.act(sq.full(), ln.full(), AF.Square, tag=f"{tag} sq")
        P.reduce("dve", var, sq.full(), ALU.add, tag=f"{tag} var")
        P.tscalar("dve", var, var, 1.0 / D, eps, ALU.mult, ALU.add,
                  tag=f"{tag} var2")
        P.act(tmp, var, AF.Sqrt, tag=f"{tag} sqrt")
        P.recip(rstd, tmp, tag=f"{tag} rs")
        P.tt("dve", t2, rstd, rstd, ALU.mult, tag=f"{tag} nw1")
        P.tt("dve", t2, t2, var, ALU.mult, tag=f"{tag} nw2")
        P.tscalar("dve", t2, t2, -0.5, 1.5, ALU.mult, ALU.add,
                  tag=f"{tag} nw3")
        P.tt("dve", rstd, rstd, t2, ALU.mult, tag=f"{tag} nw4")
        P.tscalar("dve", ln.full(), ln.full(), rstd, None, ALU.mult,
                  tag=f"{tag} norm")
        if dst_tm is not None:
            P.dma("pool", dst_tm[tt * 128:(tt + 1) * 128, 0:D], ln.full(),
                  tag=f"{tag} tm{tt}")
        if want_fm:
            for j0 in range(0, nk, 4):
                jn = min(4, nk - j0)
                ps = ctx.pb[(j0 // 4) % 2]
                lt = ln.full()
                for j in range(j0, j0 + jn):
                    col = TV(lt.ap[:, j * 128:(j + 1) * 128], lt.key, lt.p0,
                             lt.p1, lt.b0 + j * 512, lt.b0 + (j + 1) * 512)
                    P.transpose(ps[:, (j - j0) * 128:(j - j0 + 1) * 128],
                                col, ident.full(), tag=f"{tag} T{tt}.{j}")
                _fm_outs(ctx, ps[:, 0:jn * 128], tt, j0, jn, dst_split,
                         dst_f32, dst_bf16, dst_bf163, tag)


def emit_transpose_split(ctx, src_tm, T, D, dst_split=None, dst_f32=None,
                         dst_bf16=None, dst_bf163=None, src_cb=None,
                         tag="ts"):
    P = ctx.P
    nt, nk = T // 128, D // 128
    ident = ctx.sconst("ident", 128, 128)
    for tt in range(nt):
        if src_cb is not None:
            xt = src_cb(tt)
        else:
            xb = ctx.reg.get("lnx", (128, D), F32, off=(tt % 2) * D)
            P.dma("sp", xb.full(), src_tm[tt * 128:(tt + 1) * 128, 0:D],
                  tag=f"{tag} r{tt}")
            xt = xb.full()
        for j0 in range(0, nk, 4):
            jn = min(4, nk - j0)
            ps = ctx.pb[(j0 // 4) % 2]
            for j in range(j0, j0 + jn):
                col = TV(xt.ap[:, j * 128:(j + 1) * 128], xt.key, xt.p0,
                         xt.p1, xt.b0 + j * 512, xt.b0 + (j + 1) * 512)
                P.transpose(ps[:, (j - j0) * 128:(j - j0 + 1) * 128], col,
                            ident.full(), tag=f"{tag} T{tt}.{j}")
            _fm_outs(ctx, ps[:, 0:jn * 128], tt, j0, jn, dst_split, dst_f32,
                     dst_bf16, dst_bf163, tag)


def res_add_cb(ctx, xres, out_dram, T, D, nslab=512, post=None, tag="res"):
    P = ctx.P

    def cb(ps, tt, ns):
        xt = ctx.reg.get(f"rex{tt % 2}", (128, nslab), F32)
        ot = ctx.reg.get(f"reo{tt % 2}", (128, nslab), F32)
        P.dma("act", xt.full(),
              xres[tt * 128:(tt + 1) * 128, ns * nslab:(ns + 1) * nslab],
              tag=f"{tag} rx")
        src = ps
        if post is not None:
            src = post(ps, tt, ns)
        P.tt("dve", ot.full(), src, xt.full(), ALU.add, tag=f"{tag} add")
        P.dma("pool", out_dram[tt * 128:(tt + 1) * 128,
                               ns * nslab:(ns + 1) * nslab], ot.full(),
              tag=f"{tag} wr")
    return cb



def emit_l0_attn(ctx, xres, out, w, T=2048, D=2048, H=16, nslab=512,
                 tag="l0a"):
    P, reg = ctx.P, ctx.reg
    nt, nk, nh = T // 128, D // 128, H
    assert D // H == 128
    sc = ctx.scratch

    xln = sc("s_x1", kt4_shape(T, D), F32)
    emit_ln(ctx, xres, T, D, dst_f32=xln, tag=tag + ".ln")

    qtm = sc("s_t1", (T, D), F32)
    ktm = sc("s_t2", (T, D), F32)
    v1tm = sc("s_t3", (T, D), F32R)
    v2tm = sc("s_t4", (T, D), F32R)

    def plain_cb(dst):
        def cb(ps, tt, ns):
            o = reg.get(f"reo{tt % 2}", (128, nslab), F32)
            P.copy("dve", o.full(), ps, tag=f"{tag} qkv evac")
            P.dma("pool", dst[tt * 128:(tt + 1) * 128,
                              ns * nslab:(ns + 1) * nslab], o.full(),
                  tag=f"{tag} qkv wr")
        return cb

    def vsplit_cb(ps, tt, ns):
        o1 = reg.get("reo0", (128, nslab), F32R)
        o2 = reg.get("reo1", (128, nslab), F32R)
        P.copy("dve", o1.full(), ps, tag=f"{tag} v s1")
        P.tt("dve", o2.full(), ps, o1.full(), ALU.subtract, tag=f"{tag} v s2")
        P.dma("pool", v1tm[tt * 128:(tt + 1) * 128,
                           ns * nslab:(ns + 1) * nslab], o1.full(),
              tag=f"{tag} v w1")
        P.dma("pool", v2tm[tt * 128:(tt + 1) * 128,
                           ns * nslab:(ns + 1) * nslab], o2.full(),
              tag=f"{tag} v w2")

    g3 = dict(wprep="f32r_split", nslab=nslab)
    emit_gemm(ctx, xln, w["wq"], plain_cb(qtm), T, D, D, tag=tag + ".q",
              **g3)
    emit_gemm(ctx, xln, w["wk"], plain_cb(ktm), T, D, D, tag=tag + ".k",
              **g3)
    emit_gemm(ctx, xln, w["wv"], vsplit_cb, T, D, D, tag=tag + ".v",
              **g3)

    q1 = sc("s_q1", kt4_shape(T, D), F32R)
    q2 = sc("s_q2", kt4_shape(T, D), F32R)
    k1 = sc("s_k1", kt4_shape(T, D), F32R)
    k2 = sc("s_k2", kt4_shape(T, D), F32R)
    emit_transpose_split(ctx, qtm, T, D, dst_split=(q1, q2), tag=tag + ".tq")
    emit_transpose_split(ctx, ktm, T, D, dst_split=(k1, k2), tag=tag + ".tk")

    ao = sc("s_a1", kt4_shape(T, D), F32)
    mtri = ctx.sconst("mask_triu", 128, 128)
    ones_row = ctx.sconst("ones_row", 1, 512)
    onesc = reg.get("gwR", (128, 1), F32R, off=11264)
    _o1f = reg.get("msc1", (128, 1), F32, off=2047)
    P.memset("pool", _o1f.full(), 1.0, tag=f"{tag} ones0")
    P.copy("dve", onesc.full(), _o1f.full(), tag=f"{tag} ones")
    inv_sqrt_T = 1.0 / float(np.sqrt(T))
    nts = T // 512

    for h in range(nh):
        kb1 = reg.get("gwR", (128, nt, 128), F32R, off=0)
        kb2 = reg.get("gwR", (128, nt, 128), F32R, off=nt * 128)
        vb1 = reg.get("gwR", (128, nt, 128), F32R, off=2 * nt * 128)
        vb2 = reg.get("gwR", (128, nt, 128), F32R, off=3 * nt * 128)
        P.dma("sp", kb1.full(),
              TV(k1.base[:, :, h, :].rearrange("t p w -> p t w"), k1.key,
                 0, 1, 0, k1.free_total * nt * 4), tag=f"{tag} kld1.{h}")
        P.dma("sp", kb2.full(),
              TV(k2.base[:, :, h, :].rearrange("t p w -> p t w"), k2.key,
                 0, 1, 0, k2.free_total * nt * 4), tag=f"{tag} kld2.{h}")
        P.dma("sp", vb1.full(),
              TV(v1tm.base.rearrange("(t p) (c w) -> c p t w", p=128, w=128)
                 [h], v1tm.key, 0, 1, 0, v1tm.free_total * T * 4),
              tag=f"{tag} vld1.{h}")
        P.dma("sp", vb2.full(),
              TV(v2tm.base.rearrange("(t p) (c w) -> c p t w", p=128, w=128)
                 [h], v2tm.key, 0, 1, 0, v2tm.free_total * T * 4),
              tag=f"{tag} vld2.{h}")

        vs_ps = ctx.pb[6][0:1, 0:128]
        pairs = []
        for tk in range(nt):
            pairs.append((onesc.full(), vb1[:, tk, :]))
            pairs.append((onesc.full(), vb2[:, tk, :]))
        P.mm_group(pairs, vs_ps, tag=f"{tag} vsum.{h}")
        vs_row = reg.get("msc1", (1, 128), F32, off=0)
        P.copy("dve", vs_row.full(), vs_ps, tag=f"{tag} vsumc.{h}")

        for ts in range(nts):
            qb1 = reg.get("gwR", (128, 4, 128), F32R, off=8192)
            qb2 = reg.get("gwR", (128, 4, 128), F32R, off=8704)
            P.dma("sp", qb1.full(),
                  TV(q1.base[4 * ts:4 * ts + 4, :, h, :]
                     .rearrange("t p w -> p t w"), q1.key, 0, 1, 0,
                     q1.free_total * nt * 4), tag=f"{tag} qld1.{h}.{ts}")
            P.dma("sp", qb2.full(),
                  TV(q2.base[4 * ts:4 * ts + 4, :, h, :]
                     .rearrange("t p w -> p t w"), q2.key, 0, 1, 0,
                     q2.free_total * nt * 4), tag=f"{tag} qld2.{h}.{ts}")
            q1s = TV(qb1.full().ap.rearrange("p t w -> p (t w)"), "rg_gwR",
                     0, 128, 8192 * 4, 8192 * 4 + 2048)
            q2s = TV(qb2.full().ap.rearrange("p t w -> p (t w)"), "rg_gwR",
                     0, 128, 8704 * 4, 8704 * 4 + 2048)
            nps = ctx.pb[2][:, 0:512]
            dps = ctx.pb[3][0:1, 0:512]
            ntk = 4 * ts + 4

            def emit_scores(tk):
                sps = ctx.pb[4 + tk % 3][:, 0:512]
                P.mm_group([(kb1[:, tk, :], q1s),
                            (kb2[:, tk, :], q1s),
                            (kb1[:, tk, :], q2s)], sps,
                           tag=f"{tag} sc.{h}.{ts}.{tk}")
                dcol = tk - 4 * ts
                if dcol >= 0:
                    dtv = TV(sps.ap[:, dcol * 128:(dcol + 1) * 128],
                             sps.key, 0, 128, sps.b0 + dcol * 512,
                             sps.b0 + (dcol + 1) * 512)
                    P.tt("dve", dtv, dtv, mtri.full(), ALU.mult,
                         tag=f"{tag} msk")
                    if dcol > 0:
                        P.memset("dve", TV(sps.ap[:, 0:dcol * 128], sps.key,
                                           0, 128, sps.b0,
                                           sps.b0 + dcol * 512), 0.0,
                                 tag=f"{tag} z")
                base = (tk % 2) * 1024
                eb = reg.get("lnx", (128, 512), F32, off=base)
                ef = reg.get("lnx", (128, 512), F32, off=base + 512)
                e2 = reg.get("gwR", (128, 512), F32R,
                             off=9216 + (tk % 2) * 1024)
                e1 = reg.get("gwR", (128, 512), F32R,
                             off=9216 + (tk % 2) * 1024 + 512)
                P.act(eb.full(), sps, AF.Exp, scale=inv_sqrt_T,
                      tag=f"{tag} exp")
                P.act(ef.full(), eb.full(), AF.Identity, bias=-1.0,
                      tag=f"{tag} em1")
                P.copy("dve", e2.full(), ef.full(), tag=f"{tag} es1")
                P.tt("pool", e1.full(), ef.full(), e2.full(), ALU.subtract,
                     tag=f"{tag} es2")
                return e1, e2

            epipe = [emit_scores(0)]
            for tk in range(ntk):
                if tk + 1 < ntk:
                    epipe.append(emit_scores(tk + 1))
                e1, e2 = epipe[tk]
                P.mm_group([(vb1[:, tk, :], e2.full()),
                            (vb2[:, tk, :], e2.full()),
                            (vb1[:, tk, :], e1.full())], nps,
                           start=(tk == 0), stop=False,
                           tag=f"{tag} av.{h}.{ts}.{tk}")
                P.mm_group([(onesc.full(), e2.full()),
                            (onesc.full(), e1.full())], dps,
                           start=(tk == 0), stop=(tk == ntk - 1),
                           tag=f"{tag} dn.{h}.{ts}.{tk}")
            P.mm_group([(vs_row.full(), ones_row.full())], nps,
                       start=False, stop=True, tag=f"{tag} vadd.{h}.{ts}")
            dn = reg.get("msc1", (1, 512), F32, off=256)
            P.tscalar("dve", dn.full(), dps, float(T), None, ALU.add,
                      tag=f"{tag} dT")
            P.recip(dn.full(), dn.full(), tag=f"{tag} rec")
            rb_ps = ctx.pb[7][:, 0:512]
            ones_r1 = reg.get("msc1", (1, 128), F32, off=1024)
            P.memset("pool", ones_r1.full(), 1.0, tag=f"{tag} o1")
            P.mm_group([(ones_r1.full(), dn.full())], rb_ps,
                       tag=f"{tag} rb.{h}.{ts}")
            rb = reg.get("lnx", (128, 512), F32, off=2048)
            P.copy("act", rb.full(), rb_ps, tag=f"{tag} rbc")
            att = reg.get("lnx", (128, 512), F32, off=2560)
            P.tt("dve", att.full(), nps, rb.full(), ALU.mult,
                 tag=f"{tag} norm")
            P.dma("pool",
                  TV(ao.base[4 * ts:4 * ts + 4, :, h, :]
                     .rearrange("t p w -> p t w"), ao.key, 0, 1, 0,
                     ao.free_total * nt * 4),
                  TV(att.full().ap.rearrange("p (t w) -> p t w", w=128),
                     att.full().key, 0, 128, att.full().b0, att.full().b1),
                  tag=f"{tag} aw1")

    emit_gemm(ctx, ao, w["wo"],
              res_add_cb(ctx, xres, out, T, D, nslab=nslab),
              T, D, D, wprep="f32r_split", nslab=nslab, tag=tag + ".o")



def build_l0a(T=2048, D=2048, H=16):
    nc = bass.Bass("TRN2", target_bir_lowering=False)
    P = Prog(nc)
    x_in = P.dram("x", (T, D), F32, kind="ExternalInput")
    y_out = P.dram("y", (T, D), F32, kind="ExternalOutput")
    ns = min(512, D)
    w = {k: P.dram(k, wslab_shape(D, D, ns), F32, kind="ExternalInput")
         for k in ("wq", "wk", "wv", "wo")}
    ctx = Ctx(P, std_consts(T))
    emit_l0_attn(ctx, x_in, y_out, w, T, D, H, nslab=ns)
    P.emit()
    return nc, ctx, P



def _bcast16(tv, groups):
    ap = tv.ap.unsqueeze(2).broadcast_to([tv.ap.shape[0], groups, 16])
    return TV(ap, tv.key, tv.p0, tv.p1, tv.b0, tv.b1)


def emit_e4m3fn_round(ctx, eng, out_tv, in_tv, tmp1, tmp2, tag):
    P = ctx.P
    P.tscalar(eng, out_tv, in_tv, 448.0, None, ALU.min, tag=tag + " clip")
    P.tscalar(eng, tmp1.i32(), out_tv.i32(), EXPMASK, None, ALU.bitwise_and,
              tag=tag + " eb")
    P.tscalar(eng, tmp1, tmp1, 0.125, float(2.0 ** -9), ALU.mult, ALU.max,
              tag=tag + " st")
    P.recip(tmp2, tmp1, tag=tag + " rc")
    P.tt(eng, tmp2, out_tv, tmp2, ALU.mult, tag=tag + " dv")
    P.tscalar(eng, tmp2, tmp2, MAGIC, MAGIC, ALU.add, ALU.subtract,
              tag=tag + " rn")
    P.tt(eng, out_tv, tmp2, tmp1, ALU.mult, tag=tag + " ml")


def emit_e2m1_round(ctx, out_tv, in_tv, t, tag):
    P = ctx.P
    a, rlo, rmid, c2 = t[0], t[1], t[2], t[3]
    sgn = out_tv
    P.tscalar("dve", a.i32(), in_tv.i32(), 0x7FFFFFFF, None,
              ALU.bitwise_and, tag=tag + " ab")
    P.tscalar("dve", sgn.i32(), in_tv.i32(), -2147483648, None,
              ALU.bitwise_and, tag=tag + " sg")
    P.tscalar("pool", a, a, 6.0, None, ALU.min, tag=tag + " cl")
    P.tscalar("dve", rlo, a, 2.0, MAGIC, ALU.mult, ALU.add, tag=tag + " l1")
    P.tscalar("dve", rlo, rlo, MAGIC, 0.5, ALU.subtract, ALU.mult,
              tag=tag + " l2")
    P.tscalar("pool", rmid, a, MAGIC, MAGIC, ALU.add, ALU.subtract,
              tag=tag + " m1")
    P.tscalar("dve", c2, a, 2.0, None, ALU.is_ge, tag=tag + " c2")
    P.tt("dve", rmid, rmid, rlo, ALU.subtract, tag=tag + " d1")
    P.tt("dve", rmid, rmid, c2, ALU.mult, tag=tag + " d2")
    P.tt("dve", rlo, rlo, rmid, ALU.add, tag=tag + " d3")
    P.tscalar("pool", rmid, a, 0.5, MAGIC, ALU.mult, ALU.add, tag=tag + " h1")
    P.tscalar("pool", rmid, rmid, MAGIC, 2.0, ALU.subtract, ALU.mult,
              tag=tag + " h2")
    P.tscalar("dve", c2, a, 4.0, None, ALU.is_ge, tag=tag + " c4")
    P.tt("dve", rmid, rmid, rlo, ALU.subtract, tag=tag + " e1")
    P.tt("dve", rmid, rmid, c2, ALU.mult, tag=tag + " e2")
    P.tt("dve", rlo, rlo, rmid, ALU.add, tag=tag + " e3")
    P.tt("dve", out_tv.i32(), rlo.i32(), sgn.i32(), ALU.bitwise_or,
         tag=tag + " or")


def emit_nvfp4_quant_chunk(ctx, dq_out_bf16, src_tv, K, recip6ts_col,
                           recip_ts_col, tag):
    P, reg = ctx.P, ctx.reg
    G = K // 16
    am = reg.get("msc0", (128, G), F32, off=0)
    t1 = reg.get("msc0", (128, G), F32, off=G)
    t2 = reg.get("msc0", (128, G), F32, off=2 * G)
    P.reduce("dve", am.full(), TV(src_tv.ap.rearrange("p (g k) -> p g k",
                                                      k=16), src_tv.key,
                                  src_tv.p0, src_tv.p1, src_tv.b0,
                                  src_tv.b1),
             ALU.max, absval=True, tag=tag + " am")
    P.tscalar("dve", am.full(), am.full(), recip6ts_col, None, ALU.mult,
              tag=tag + " bs")
    emit_e4m3fn_round(ctx, "dve", am.full(), am.full(), t1.full(), t2.full(),
                      tag + " e4")
    P.tscalar("dve", t1.full(), am.full(), 0.0, None, ALU.is_equal,
              tag=tag + " eq")
    P.tt("dve", t1.full(), t1.full(), am.full(), ALU.add, tag=tag + " de")
    P.recip(t1.full(), t1.full(), tag=tag + " rd")
    P.tscalar("dve", t1.full(), t1.full(), recip_ts_col, None, ALU.mult,
              tag=tag + " rt")
    y = reg.get("msc1", (128, K), F32, off=0)
    yv = TV(y.full().ap.rearrange("p (g k) -> p g k", k=16), y.full().key,
            0, 128, y.full().b0, y.full().b1)
    sv = TV(src_tv.ap.rearrange("p (g k) -> p g k", k=16), src_tv.key,
            src_tv.p0, src_tv.p1, src_tv.b0, src_tv.b1)
    P.tt("dve", yv, sv, _bcast16(t1.full(), G), ALU.mult, tag=tag + " sc")
    scr = [TV(ctx.reg.get("gxB", (128, K), F32, off=i * K).full().ap,
              "rg_gxB", 0, 128, i * K * 4, (i + 1) * K * 4)
           for i in range(3)]
    scr.append(TV(ctx.reg.get("lnx", (128, K), F32, off=0).full().ap,
                  "rg_lnx", 0, 128, 0, K * 4))
    emit_e2m1_round(ctx, y.full(), y.full(), scr, tag + " e2")
    ov = TV(dq_out_bf16.ap.rearrange("p (g k) -> p g k", k=16),
            dq_out_bf16.key, dq_out_bf16.p0, dq_out_bf16.p1,
            dq_out_bf16.b0, dq_out_bf16.b1)
    P.tt("dve", ov, yv, _bcast16(am.full(), G), ALU.mult, tag=tag + " dq")

def emit_l0_ffn(ctx, xres, out, wdram, bias_dram, xts_dram, T=2048, D=2048,
                nslab=512, tag="l0f"):
    P, reg = ctx.P, ctx.reg
    N = K = D
    nt, nk = T // 128, K // 128
    sc = ctx.scratch
    xln_tm = sc("s_t1", (T, D), F32)
    emit_ln(ctx, xres, T, D, dst_tm=xln_tm, tag=tag + ".ln")

    S = ctx.S
    wmax = S[5][:, 0:1]
    P.memset("dve", wmax, 0.0, tag=f"{tag} wm0")
    for c in range(N // 128):
        wt = reg.get("msc1", (128, K), F32, off=0)
        P.dma("sp", wt.full(), wdram[c * 128:(c + 1) * 128, :],
              tag=f"{tag} WA{c}")
        cm = S[6][:, 0:1]
        P.reduce("dve", cm, wt.full(), ALU.max, absval=True,
                 tag=f"{tag} wmr{c}")
        P.tt("dve", wmax, wmax, cm, ALU.max, tag=f"{tag} wmm{c}")
    ident = ctx.sconst("ident", 128, 128)
    pad = reg.get("lnx", (128, 128), F32, off=0)
    P.memset("dve", pad.full(), 0.0, tag=f"{tag} pz")
    P.copy("dve", pad[:, 0:1], wmax, tag=f"{tag} pc")
    tps = ctx.pb[0][:, 0:128]
    P.transpose(tps, pad.full(), ident.full(), tag=f"{tag} pt")
    row = ctx.R[0][0:1, 0:128]
    P.copy("dve", row, TV(tps.ap[0:1, :], tps.key, 0, 1, tps.b0,
                          tps.b0 + 512), tag=f"{tag} rc")
    wts = S[5][0:1, 1:2]
    P.reduce("dve", wts, row, ALU.max, tag=f"{tag} wts")
    P.tscalar("dve", wts, wts, 1.0 / 2688.0, None, ALU.mult,
              tag=f"{tag} ts")
    onesr = ctx.R[1][0:1, 0:128]
    P.memset("dve", onesr, 1.0, tag=f"{tag} o")
    s6 = S[5][0:1, 2:3]
    P.tscalar("dve", s6, wts, 6.0, None, ALU.mult, tag=f"{tag} s6")
    P.recip(s6, s6, tag=f"{tag} s6r")
    rts = S[5][0:1, 3:4]
    P.recip(rts, wts, tag=f"{tag} rts")
    P.dma("sp", S[6][0:1, 0:1], xts_dram[0:1, 0:1], tag=f"{tag} xts")
    x6 = S[6][0:1, 1:2]
    P.tscalar("dve", x6, S[6][0:1, 0:1], 6.0, None, ALU.mult,
              tag=f"{tag} x6")
    P.recip(x6, x6, tag=f"{tag} x6r")
    xrts = S[6][0:1, 2:3]
    P.recip(xrts, S[6][0:1, 0:1], tag=f"{tag} xrts")
    osc = S[6][0:1, 3:4]
    P.tt("dve", osc, wts, S[6][0:1, 0:1], ALU.mult, tag=f"{tag} osc")

    def bcast_col(dst_col, src11, bank, tg):
        ps = ctx.pb[bank][:, 0:1]
        P.mm_group([(onesr, src11)], ps, tag=tg)
        P.copy("dve", dst_col, ps, tag=tg + "c")
    rw6c = S[7][:, 0:1]
    rwtc = S[7][:, 1:2]
    rx6c = S[7][:, 2:3]
    rxtc = S[7][:, 3:4]
    oscc = S[7][:, 4:5]
    bcast_col(rw6c, s6, 1, f"{tag} b1")
    bcast_col(rwtc, rts, 1, f"{tag} b2")
    bcast_col(rx6c, x6, 1, f"{tag} b3")
    bcast_col(rxtc, xrts, 1, f"{tag} b4")
    bcast_col(oscc, osc, 1, f"{tag} b5")

    bias_tvs = []
    for i in range(N // nslab):
        if i < 2:
            tvb = ctx.R[2 + i][0:1, 0:nslab]
        else:
            tvb = reg.get("msc0", (1, nslab), F32,
                          off=512 * (i - 1)).full()
        P.dma("sp", tvb, bias_dram[i * nslab:(i + 1) * nslab],
              tag=f"{tag} bld{i}")
        bias_tvs.append(tvb)

    def bias_row_tv(ns):
        return bias_tvs[ns]

    identb = ctx.sconst("ident_bf", 128, 128, dtype=BF16)
    wdqT = sc("s_wq", wslab_shape(K, N, nslab), BF16)
    for c in range(N // 128):
        wt = reg.get("msc1", (128, K), F32, off=0)
        P.dma("sp", wt.full(), wdram[c * 128:(c + 1) * 128, :],
              tag=f"{tag} WB{c}")
        dq = reg.get("gwB", (128, K), BF16, off=0)
        emit_nvfp4_quant_chunk(ctx, dq.full(), wt.full(), K, rw6c, rwtc,
                               f"{tag} wq{c}")
        for j0 in range(0, nk, 4):
            jn = min(4, nk - j0)
            psb = ctx.pb[2 + (j0 // 4) % 2].cast(BF16)
            for j in range(j0, j0 + jn):
                P.transpose(psb[:, (j - j0) * 128:(j - j0 + 1) * 128],
                            TV(dq.full().ap[:, j * 128:(j + 1) * 128],
                               "rg_gwB", 0, 128, j * 256, (j + 1) * 256),
                            identb.full(), tag=f"{tag} wT{c}.{j}")
            ob = reg.get("gwB", (128, jn * 128), BF16, off=4096)
            P.copy("dve", ob.full(), psb[:, 0:jn * 128], tag=f"{tag} wE")
            dst = TV(wdqT.base[c // 4, :, j0:j0 + jn,
                               (c % 4) * 128:(c % 4 + 1) * 128],
                     wdqT.key, 0, 1, 0, wdqT.free_total * 8)
            src = TV(ob.full().ap.rearrange("p (c t) -> p c t", t=128),
                     ob.full().key, 0, 128, ob.full().b0, ob.full().b1)
            P.dma("pool", dst, src, tag=f"{tag} wW{c}.{j0}")

    xdqT = sc("s_xq", kt4_shape(T, K), BF16)
    for tt in range(nt):
        xt = reg.get("msc1", (128, D), F32, off=0)
        P.dma("sp", xt.full(), xln_tm[tt * 128:(tt + 1) * 128, :],
              tag=f"{tag} xr{tt}")
        dq = reg.get("gwB", (128, K), BF16, off=0)
        emit_nvfp4_quant_chunk(ctx, dq.full(), xt.full(), K, rx6c, rxtc,
                               f"{tag} xq{tt}")
        for j0 in range(0, nk, 4):
            jn = min(4, nk - j0)
            psb = ctx.pb[2 + (j0 // 4) % 2].cast(BF16)
            for j in range(j0, j0 + jn):
                P.transpose(psb[:, (j - j0) * 128:(j - j0 + 1) * 128],
                            TV(dq.full().ap[:, j * 128:(j + 1) * 128],
                               "rg_gwB", 0, 128, j * 256, (j + 1) * 256),
                            identb.full(), tag=f"{tag} xT{tt}.{j}")
            ob = reg.get("gwB", (128, jn * 128), BF16, off=4096)
            P.copy("dve", ob.full(), psb[:, 0:jn * 128], tag=f"{tag} xE")
            P.dma("pool", kt4_block(xdqT, tt, j0, jn), ob.full(),
                  tag=f"{tag} xW{tt}.{j0}")

    def post(ps, tt, ns):
        o = reg.get("tsa", (128, nslab), F32)
        bb = ctx.pb[(tt % 2)][:, 0:nslab]
        P.mm_group([(onesr, bias_row_tv(ns))], bb, tag=f"{tag} bb{tt}.{ns}")
        bs = reg.get("tsb", (128, nslab), F32)
        P.copy("act", bs.full(), bb, tag=f"{tag} bbc")
        P.stt("dve", o.full(), ps, oscc, bs.full(), ALU.mult, ALU.add,
              tag=f"{tag} ep")
        return o.full()

    emit_gemm(ctx, xdqT, wdqT,
              res_add_cb(ctx, xres, out, T, D, nslab=nslab, post=post),
              T, K, N, wprep="raw", nslab=nslab, tag=tag + ".mm")



def emit_l1_attn(ctx, xres, out, w, T=2048, D=2048, H=16, nslab=512,
                 tag="l1a"):
    P, reg = ctx.P, ctx.reg
    nt, nh = T // 128, H
    D_ = D
    eps = 1e-6
    sc = ctx.scratch
    xln = sc("s_x1", kt4_shape(T, D), F32)
    emit_ln(ctx, xres, T, D, dst_f32=xln, tag=tag + ".ln")

    dec = reg.get("msc0", (128, nt), F32, off=1024)
    P.dma("sp", dec.full(), ctx.consts["decay"].full(),
          tag=f"{tag} dec")
    onesr = ctx.R[1][0:1, 0:128]
    P.memset("dve", onesr, 1.0, tag=f"{tag} o")

    bias_tv = {}

    def load_bias(nm):
        tvs = []
        for i in range(D // nslab):
            rb = reg.get("msc1", (1, nslab), F32, off=i * nslab)
            P.dma("sp", rb.full(), w[nm][i * nslab:(i + 1) * nslab],
                  tag=f"{tag} b{nm}{i}")
            tvs.append(rb.full())
        bias_tv[nm] = tvs

    qtm = sc("s_t1", (T, D), F32)
    k1tm = sc("s_t3", (T, D), F32R)
    k2tm = sc("s_t4", (T, D), F32R)
    v1tm = sc("s_t5", (T, D), F32R)
    v2tm = sc("s_t6", (T, D), F32R)

    def bias_pairs(nm):
        def ep(tt, ns):
            return [(onesr, bias_tv[nm][ns])]
        return ep

    def q_cb(ps, tt, ns):
        o = reg.get(f"reo{tt % 2}", (128, nslab), F32)
        P.act(o.full(), ps, AF.Relu, tag=f"{tag} qrelu")
        P.tscalar("dve", o.full(), o.full(), dec[:, tt:tt + 1], None,
                  ALU.mult, tag=f"{tag} qdec")
        P.dma("pool", qtm[tt * 128:(tt + 1) * 128,
                          ns * nslab:(ns + 1) * nslab], o.full(),
              tag=f"{tag} qw")

    def kv_cb(d1tm, d2tm, relu):
        def cb(ps, tt, ns):
            o = reg.get("tsc", (128, nslab), F32)
            if relu:
                P.act(o.full(), ps, AF.Relu, tag=f"{tag} krelu")
                P.tscalar("dve", o.full(), o.full(), dec[:, tt:tt + 1],
                          None, ALU.mult, tag=f"{tag} kdec")
            else:
                P.copy("act", o.full(), ps, tag=f"{tag} vev")
            o1 = reg.get("reo0", (128, nslab), F32R)
            o2 = reg.get("reo1", (128, nslab), F32R)
            P.copy("dve", o1.full(), o.full(), tag=f"{tag} s1")
            P.tt("dve", o2.full(), o.full(), o1.full(), ALU.subtract,
                 tag=f"{tag} s2")
            P.dma("pool", d1tm[tt * 128:(tt + 1) * 128,
                               ns * nslab:(ns + 1) * nslab], o1.full(),
                  tag=f"{tag} w1")
            P.dma("pool", d2tm[tt * 128:(tt + 1) * 128,
                               ns * nslab:(ns + 1) * nslab], o2.full(),
                  tag=f"{tag} w2")
        return cb

    g3 = dict(wprep="f32r_split", nslab=nslab)
    load_bias("bq")
    emit_gemm(ctx, xln, w["wq"], q_cb, T, D, D, tag=tag + ".q",
              extra_pairs=bias_pairs("bq"), **g3)
    load_bias("bk")
    emit_gemm(ctx, xln, w["wk"], kv_cb(k1tm, k2tm, True), T, D, D,
              tag=tag + ".k", extra_pairs=bias_pairs("bk"), **g3)
    load_bias("bv")
    emit_gemm(ctx, xln, w["wv"], kv_cb(v1tm, v2tm, False), T, D, D,
              tag=tag + ".v", extra_pairs=bias_pairs("bv"), **g3)

    q1 = sc("s_q1", kt4_shape(T, D), F32R)
    q2 = sc("s_q2", kt4_shape(T, D), F32R)
    emit_transpose_split(ctx, qtm, T, D, dst_split=(q1, q2), tag=tag + ".tq")

    ao = sc("s_t2", (T, D), F32)
    onesc = reg.get("gwR", (128, 2), F32R, off=9100)
    _o1f = ctx.S[6][:, 8:10]
    P.memset("dve", _o1f, 1.0, tag=f"{tag} oc0")
    P.copy("dve", onesc.full(), _o1f, tag=f"{tag} oc")
    for h in range(nh):
        kb1 = reg.get("gwR", (128, nt, 128), F32R, off=0)
        kb2 = reg.get("gwR", (128, nt, 128), F32R, off=nt * 128)
        vb1 = reg.get("gwR", (128, nt, 128), F32R, off=2 * nt * 128)
        vb2 = reg.get("gwR", (128, nt, 128), F32R, off=3 * nt * 128)
        for (b, tm) in ((kb1, k1tm), (kb2, k2tm), (vb1, v1tm), (vb2, v2tm)):
            P.dma("sp", b.full(),
                  TV(tm.base.rearrange("(t p) (c w) -> c p t w", p=128,
                                       w=128)[h], tm.key, 0, 1, 0,
                     tm.free_total * T * 4), tag=f"{tag} ld{h}")
        kv_p = ctx.pb[2][:, 0:128]
        ks_p = ctx.pb[3][:, 0:2]
        pairs_kv = []
        pairs_ks = []
        for tk in range(nt):
            pairs_kv += [(kb1[:, tk, :], vb1[:, tk, :]),
                         (kb2[:, tk, :], vb1[:, tk, :]),
                         (kb1[:, tk, :], vb2[:, tk, :])]
            pairs_ks += [(kb1[:, tk, :], onesc.full()),
                         (kb2[:, tk, :], onesc.full())]
        P.mm_group(pairs_kv, kv_p, tag=f"{tag} kv.{h}")
        P.mm_group(pairs_ks, ks_p, tag=f"{tag} ks.{h}")
        r1 = reg.get("gwR", (128, 130), F32R, off=4 * nt * 128)
        r2 = reg.get("gwR", (128, 130), F32R, off=4 * nt * 128 + 130)
        t0 = reg.get("lnx", (128, 130), F32, off=0)
        P.memset("dve", t0[:, 128:130], 0.0, tag=f"{tag} kvz")
        P.copy("dve", t0[:, 0:128], kv_p, tag=f"{tag} kve")
        P.copy("dve", t0[:, 128:129],
               TV(ks_p.ap[:, 0:1], ks_p.key, 0, 128, ks_p.b0, ks_p.b0 + 4),
               tag=f"{tag} kse")
        P.copy("dve", r1.full(), t0.full(), tag=f"{tag} kvs1")
        P.tt("dve", r2.full(), t0.full(), r1.full(), ALU.subtract,
             tag=f"{tag} kvs2")
        for tt in range(nt):
            qb1 = reg.get("gwR", (128, 128), F32R,
                          off=4 * nt * 128 + 300 + (tt % 2) * 256)
            qb2 = reg.get("gwR", (128, 128), F32R,
                          off=4 * nt * 128 + 300 + (tt % 2) * 256 + 128)
            P.dma("sp", qb1.full(), kt4_block(q1, tt, h, 1),
                  tag=f"{tag} q1.{h}.{tt}")
            P.dma("sp", qb2.full(), kt4_block(q2, tt, h, 1),
                  tag=f"{tag} q2.{h}.{tt}")
            ops_ = ctx.pb[4 + tt % 2][:, 0:130]
            P.mm_group([(qb1.full(), r1.full()),
                        (qb2.full(), r1.full()),
                        (qb1.full(), r2.full())], ops_,
                       tag=f"{tag} out.{h}.{tt}")
            z = ctx.S[6][:, 9:10]
            P.tscalar("dve", z, TV(ops_.ap[:, 128:129], ops_.key, 0, 128,
                                   ops_.b0 + 512, ops_.b0 + 516),
                      eps, None, ALU.add, tag=f"{tag} z")
            P.recip(z, z, tag=f"{tag} zr")
            o = reg.get("lnx", (128, 128), F32, off=256 + (tt % 2) * 128)
            P.tscalar("dve", o.full(), TV(ops_.ap[:, 0:128], ops_.key, 0,
                                          128, ops_.b0, ops_.b0 + 512),
                      z, None, ALU.mult, tag=f"{tag} on")
            P.dma("pool", ao[tt * 128:(tt + 1) * 128,
                            h * 128:(h + 1) * 128], o.full(),
                  tag=f"{tag} ow.{h}.{tt}")

    aoT = sc("s_a1", kt4_shape(T, D), F32)
    emit_transpose_split(ctx, ao, T, D, dst_f32=aoT, tag=tag + ".ta")
    load_bias("bo")
    emit_gemm(ctx, aoT, w["wo"],
              res_add_cb(ctx, xres, out, T, D, nslab=nslab),
              T, D, D, wprep="f32r_split",
              nslab=nslab, extra_pairs=bias_pairs("bo"), tag=tag + ".o")



def emit_l1_ffn(ctx, xres, out, w, T=2048, D=2048, E=8, FFN=1024,
                nslab=512, tag="l1f"):
    P, reg = ctx.P, ctx.reg
    nt, nk = T // 128, D // 128
    EF = E * FFN
    sc = ctx.scratch
    xln = sc("s_x1", kt4_shape(T, D), F32)
    emit_ln(ctx, xres, T, D, dst_f32=xln, tag=tag + ".ln")

    onesr = ctx.R[1][0:1, 0:128]
    P.memset("dve", onesr, 1.0, tag=f"{tag} o")
    brow = ctx.R[2][0:1, 0:E]
    P.dma("sp", brow, w["bias"][0:E], tag=f"{tag} bias")
    bps = ctx.pb[0][:, 0:E]
    P.mm_group([(onesr, brow)], bps, tag=f"{tag} bb")
    bbc = ctx.S[6][:, 0:E]
    P.copy("dve", bbc, bps, tag=f"{tag} bbc")

    wn = reg.get("msc0", (128, nt * E), F32, off=1024)

    def r_cb(ps, tt, ns):
        s_ = ctx.S
        r = s_[5][:, 0:E]
        P.act(r, ps, AF.Sigmoid, tag=f"{tag} sg")
        sco = s_[5][:, 8:8 + E]
        P.tt("dve", sco, r, bbc, ALU.add, tag=f"{tag} sb")
        m1 = s_[6][:, 8:9]
        P.reduce("dve", m1, sco, ALU.max, tag=f"{tag} m1")
        eq = s_[7][:, 0:E]
        P.tscalar("dve", eq, sco, m1, None, ALU.is_ge, tag=f"{tag} eq")
        s2 = s_[7][:, 8:8 + E]
        P.stt("dve", s2, eq, -1e30, sco, ALU.mult, ALU.add, tag=f"{tag} s2")
        m2 = s_[6][:, 9:10]
        P.reduce("dve", m2, s2, ALU.max, tag=f"{tag} m2")
        mm_ = s_[7][:, 0:E]
        P.tscalar("dve", mm_, sco, m2, None, ALU.is_ge, tag=f"{tag} mtop")
        rm = s_[5][:, 8:8 + E]
        P.tt("dve", rm, r, mm_, ALU.mult, tag=f"{tag} rm")
        ds = s_[6][:, 8:9]
        P.reduce("dve", ds, rm, ALU.add, tag=f"{tag} ds")
        P.recip(ds, ds, tag=f"{tag} dr")
        P.tscalar("dve", wn[:, tt * E:(tt + 1) * E], rm, ds, None, ALU.mult,
                  tag=f"{tag} wn")

    emit_gemm(ctx, xln, w["wg"], r_cb, T, D, E,
              wprep="f32r_split", nslab=E, tag=tag + ".r")

    s1tm = sc("s_t1", (T, EF), F32)
    ident = ctx.sconst("ident", 128, 128)

    def h1_cb(ps, tt, ns):
        o = reg.get(f"reo{tt % 2}", (128, nslab), F32)
        P.act(o.full(), ps, AF.Silu, tag=f"{tag} silu")
        P.dma("pool", s1tm[tt * 128:(tt + 1) * 128,
                           ns * nslab:(ns + 1) * nslab], o.full(),
              tag=f"{tag} s1w")

    emit_gemm(ctx, xln, w["w1"], h1_cb, T, D, EF,
              wprep="f32r_split", nslab=nslab, tag=tag + ".h1")

    gkt = sc("s_g1", kt4_shape(T, EF), F32)
    nsl_per_e = FFN // nslab

    def h3_cb(ps, tt, ns):
        e = ns // nsl_per_e
        sl = reg.get(f"rex{tt % 2}", (128, nslab), F32)
        P.dma("act", sl.full(), s1tm[tt * 128:(tt + 1) * 128,
                                     ns * nslab:(ns + 1) * nslab],
              tag=f"{tag} s1r")
        g = reg.get("tsc", (128, nslab), F32)
        P.stt("dve", g.full(), ps, wn[:, tt * E + e:tt * E + e + 1],
              sl.full(), ALU.mult, ALU.mult, tag=f"{tag} g")
        psT = ctx.pb[(tt % 2)]
        for j in range(nslab // 128):
            P.transpose(psT[:, j * 128:(j + 1) * 128],
                        TV(g.full().ap[:, j * 128:(j + 1) * 128],
                           "rg_tsc", 0, 128, j * 512, (j + 1) * 512),
                        ident.full(), tag=f"{tag} gT")
        o1 = reg.get("tsa", (128, nslab), F32)
        pv = psT[:, 0:nslab]
        P.copy("dve", o1.full(), pv, tag=f"{tag} gs1")
        P.dma("pool", kt4_block(gkt, tt, ns * (nslab // 128), nslab // 128),
              o1.full(), tag=f"{tag} gw1")

    emit_gemm(ctx, xln, w["w3"], h3_cb, T, D, EF,
              wprep="f32r_split", nslab=nslab, tag=tag + ".h3")

    acc = sc("s_t2", (T, D), F32)
    nkg = EF // 2048
    for kg in range(nkg):
        first = (kg == 0)
        last = (kg == nkg - 1)

        if last:
            cb = res_add_acc_cb(ctx, acc, xres, out, T, D, nslab, tag)
        elif first:
            cb = store_cb(ctx, acc, T, D, nslab, tag)
        else:
            cb = accum_cb(ctx, acc, T, D, nslab, tag)
        emit_gemm(ctx, gkt, w["w2"], cb, T, 2048, D,
                  wprep="f32r_split", nslab=nslab, kgroup=(kg * 16, 16),
                  tag=tag + f".d{kg}")


def store_cb(ctx, dst, T, D, nslab, tag):
    P = ctx.P

    def cb(ps, tt, ns):
        o = ctx.reg.get(f"reo{tt % 2}", (128, nslab), F32)
        P.copy("dve", o.full(), ps, tag=f"{tag} st")
        P.dma("pool", dst[tt * 128:(tt + 1) * 128,
                          ns * nslab:(ns + 1) * nslab], o.full(),
              tag=f"{tag} stw")
    return cb


def accum_cb(ctx, acc, T, D, nslab, tag):
    P = ctx.P

    def cb(ps, tt, ns):
        a = ctx.reg.get(f"rex{tt % 2}", (128, nslab), F32)
        P.dma("act", a.full(), acc[tt * 128:(tt + 1) * 128,
                                   ns * nslab:(ns + 1) * nslab],
              tag=f"{tag} accr")
        o = ctx.reg.get(f"reo{tt % 2}", (128, nslab), F32)
        P.tt("dve", o.full(), ps, a.full(), ALU.add, tag=f"{tag} acca")
        P.dma("pool", acc[tt * 128:(tt + 1) * 128,
                          ns * nslab:(ns + 1) * nslab], o.full(),
              tag=f"{tag} accw")
    return cb


def res_add_acc_cb(ctx, acc, xres, out, T, D, nslab, tag):
    P = ctx.P

    def cb(ps, tt, ns):
        a = ctx.reg.get(f"rex{tt % 2}", (128, nslab), F32)
        P.dma("act", a.full(), acc[tt * 128:(tt + 1) * 128,
                                   ns * nslab:(ns + 1) * nslab],
              tag=f"{tag} fr1")
        xv = ctx.reg.get("tsd", (128, nslab), F32)
        P.dma("act", xv.full(), xres[tt * 128:(tt + 1) * 128,
                                     ns * nslab:(ns + 1) * nslab],
              tag=f"{tag} fr2")
        o = ctx.reg.get(f"reo{tt % 2}", (128, nslab), F32)
        P.tt("dve", o.full(), ps, a.full(), ALU.add, tag=f"{tag} fa1")
        P.tt("dve", o.full(), o.full(), xv.full(), ALU.add, tag=f"{tag} fa2")
        P.dma("pool", out[tt * 128:(tt + 1) * 128,
                          ns * nslab:(ns + 1) * nslab], o.full(),
              tag=f"{tag} fw")
    return cb



def emit_rope_tile(ctx, o_buf, nheads, cosb, sinb, tag):
    P = ctx.P
    W = nheads * 64

    def half(i):
        ap = o_buf.full().ap.rearrange("p (h two d) -> p h two d", two=2,
                                       d=64)[:, :, i, :]
        return TV(ap, o_buf.full().key, 0, 128, o_buf.full().b0,
                  o_buf.full().b1)

    def cbc(ctv):
        ap = ctv.ap.unsqueeze(1).broadcast_to([128, nheads, 64])
        return TV(ap, ctv.key, ctv.p0, ctv.p1, ctv.b0, ctv.b1)

    q1, q2 = half(0), half(1)
    a = ctx.reg.get("tsc", (128, W), F32)
    d = ctx.reg.get("tsd", (128, W), F32)

    def v3(buf):
        ap = buf.full().ap.rearrange("p (h d) -> p h d", d=64)
        return TV(ap, buf.full().key, 0, 128, buf.full().b0, buf.full().b1)

    P.tt("dve", v3(a), q1, cbc(cosb), ALU.mult, tag=tag + " r1")
    P.tt("pool", v3(d), q1, cbc(sinb), ALU.mult, tag=tag + " r2")
    t2 = ctx.reg.get("rex0", (128, W), F32)
    P.tt("dve", v3(t2), q2, cbc(sinb), ALU.mult, tag=tag + " r3")
    P.tt("dve", v3(a), v3(a), v3(t2), ALU.subtract, tag=tag + " r4")
    P.tt("pool", v3(t2), q2, cbc(cosb), ALU.mult, tag=tag + " r5")
    P.tt("pool", v3(d), v3(t2), v3(d), ALU.add, tag=tag + " r6")
    P.copy("dve", q1, v3(a), tag=tag + " r7")
    P.copy("pool", q2, v3(d), tag=tag + " r8")


def emit_l2_attn(ctx, xres, out, w, T=2048, D=2048, H=16, KV=4, WIN=512,
                 nslab=512, tag="l2a"):
    P, reg = ctx.P, ctx.reg
    nt, nh = T // 128, H
    dh = 128
    Dkv = KV * dh
    nrep = H // KV
    sc = ctx.scratch
    xln = sc("s_x1", kt4_shape(T, D), F32)
    emit_ln(ctx, xres, T, D, dst_f32=xln, tag=tag + ".ln")

    qtm = sc("s_t1", (T, D), F32)
    ktm = sc("s_t2", (T, Dkv), F32)
    v1tm = sc("s_t3", (T, Dkv), F32R)
    v2tm = sc("s_t4", (T, Dkv), F32R)

    def rope_cb(dst, nhd, nsl):
        def cb(ps, tt, ns):
            o = reg.get(f"reo{tt % 2}", (128, nsl), F32)
            P.copy("dve", o.full(), ps, tag=f"{tag} ev")
            cosb = reg.get("msc0", (128, 64), F32, off=0)
            sinb = reg.get("msc0", (128, 64), F32, off=64)
            P.dma("sp", cosb.full(),
                  ctx.consts["ropec"][tt * 128:(tt + 1) * 128, :],
                  tag=f"{tag} cs")
            P.dma("sp", sinb.full(),
                  ctx.consts["ropes"][tt * 128:(tt + 1) * 128, :],
                  tag=f"{tag} sn")
            emit_rope_tile(ctx, o, nsl // 128, cosb.full(),
                           sinb.full(), f"{tag} rp{tt}.{ns}")
            P.dma("pool", dst[tt * 128:(tt + 1) * 128,
                              ns * nsl:(ns + 1) * nsl], o.full(),
                  tag=f"{tag} qw")
        return cb

    emit_gemm(ctx, xln, w["wq"], rope_cb(qtm, H, nslab), T, D, D,
              tag=tag + ".q", wprep="f32r_split", nslab=nslab)
    emit_gemm(ctx, xln, w["wk"], rope_cb(ktm, KV, min(nslab, Dkv)),
              T, D, Dkv, tag=tag + ".k", nslab=min(nslab, Dkv),
              wprep="f32r_split")

    def vsplit_cb(ps, tt, ns):
        o1 = reg.get("reo0", (128, min(nslab, Dkv)), F32R)
        o2 = reg.get("reo1", (128, min(nslab, Dkv)), F32R)
        P.copy("dve", o1.full(), ps, tag=f"{tag} v s1")
        P.tt("dve", o2.full(), ps, o1.full(), ALU.subtract,
             tag=f"{tag} v s2")
        for d_, o_ in ((v1tm, o1), (v2tm, o2)):
            P.dma("pool", d_[tt * 128:(tt + 1) * 128,
                            ns * min(nslab, Dkv):(ns + 1) * min(nslab, Dkv)],
                  o_.full(), tag=f"{tag} vw")

    emit_gemm(ctx, xln, w["wv"], vsplit_cb, T, D, Dkv,
              nslab=min(nslab, Dkv), wprep="f32r_split", tag=tag + ".v")

    q1 = sc("s_q1", kt4_shape(T, D), F32R)
    q2 = sc("s_q2", kt4_shape(T, D), F32R)
    k1 = sc("s_k1", kt4_shape(T, Dkv), F32R)
    k2 = sc("s_k2", kt4_shape(T, Dkv), F32R)
    emit_transpose_split(ctx, qtm, T, D, dst_split=(q1, q2), tag=tag + ".tq")
    emit_transpose_split(ctx, ktm, T, Dkv, dst_split=(k1, k2),
                         tag=tag + ".tk")

    ao = sc("s_a1", kt4_shape(T, D), F32)
    mtri = ctx.sconst("mask_triu", 128, 128)
    msl = ctx.sconst("mask_sl", 128, 128)
    ones_row = ctx.sconst("ones_row", 1, 512)
    onesc = reg.get("gwR", (128, 2), F32R, off=11264)
    _o1f = ctx.S[6][:, 8:10]
    P.memset("dve", _o1f, 1.0, tag=f"{tag} oc0")
    P.copy("dve", onesc.full(), _o1f, tag=f"{tag} oc")
    scale = 1.0 / float(np.sqrt(dh))
    nts = T // 512
    wtiles = WIN // 128

    for h in range(nh):
        g = h // nrep
        kb1 = reg.get("gwR", (128, nt, 128), F32R, off=0)
        kb2 = reg.get("gwR", (128, nt, 128), F32R, off=nt * 128)
        vb1 = reg.get("gwR", (128, nt, 128), F32R, off=2 * nt * 128)
        vb2 = reg.get("gwR", (128, nt, 128), F32R, off=3 * nt * 128)
        if h % nrep == 0:
            for (b, src, iskt4) in ((kb1, k1, True), (kb2, k2, True),
                                    (vb1, v1tm, False), (vb2, v2tm, False)):
                if iskt4:
                    P.dma("sp", b.full(),
                          TV(src.base[:, :, g, :].rearrange(
                              "t p w -> p t w"), src.key, 0, 1, 0,
                             src.free_total * nt * 4),
                          tag=f"{tag} kld.{h}")
                else:
                    P.dma("sp", b.full(),
                          TV(src.base.rearrange(
                              "(t p) (c w) -> c p t w", p=128, w=128)[g],
                             src.key, 0, 1, 0, src.free_total * T * 4),
                          tag=f"{tag} vld.{h}")
        for ts in range(nts):
            qb1 = reg.get("gwR", (128, 4, 128), F32R, off=8192)
            qb2 = reg.get("gwR", (128, 4, 128), F32R, off=8704)
            P.dma("sp", qb1.full(),
                  TV(q1.base[4 * ts:4 * ts + 4, :, h, :]
                     .rearrange("t p w -> p t w"), q1.key, 0, 1, 0,
                     q1.free_total * nt * 4), tag=f"{tag} qld1.{h}.{ts}")
            P.dma("sp", qb2.full(),
                  TV(q2.base[4 * ts:4 * ts + 4, :, h, :]
                     .rearrange("t p w -> p t w"), q2.key, 0, 1, 0,
                     q2.free_total * nt * 4), tag=f"{tag} qld2.{h}.{ts}")
            q1s = TV(qb1.full().ap.rearrange("p t w -> p (t w)"), "rg_gwR",
                     0, 128, 8192 * 4, 8192 * 4 + 2048)
            q2s = TV(qb2.full().ap.rearrange("p t w -> p (t w)"), "rg_gwR",
                     0, 128, 8704 * 4, 8704 * 4 + 2048)
            nps = ctx.pb[2][:, 0:512]
            dps = ctx.pb[3][0:2, 0:512]
            tks = [tk for tk in range(max(0, 4 * ts - wtiles),
                                      4 * ts + 4)]

            def emit_sc(tk):
                sps = ctx.pb[4 + tk % 3][:, 0:512]
                P.mm_group([(kb1[:, tk, :], q1s),
                            (kb2[:, tk, :], q1s),
                            (kb1[:, tk, :], q2s)], sps,
                           tag=f"{tag} sc.{h}.{ts}.{tk}")
                base = (tk % 2) * 1024
                eb = reg.get("lnx", (128, 512), F32, off=base)
                e2 = reg.get("gwR", (128, 512), F32R,
                             off=9216 + (tk % 2) * 1024)
                e1 = reg.get("gwR", (128, 512), F32R,
                             off=9216 + (tk % 2) * 1024 + 512)
                P.act(eb.full(), sps, AF.Exp, scale=scale,
                      tag=f"{tag} exp")
                for qq in range(4):
                    tq = 4 * ts + qq
                    dcol = tq - tk
                    sl = TV(eb.full().ap[:, qq * 128:(qq + 1) * 128],
                            "rg_lnx", 0, 128, eb.full().b0 + qq * 512,
                            eb.full().b0 + (qq + 1) * 512)
                    if dcol == 0:
                        P.tt("dve", sl, sl, mtri.full(), ALU.mult,
                             tag=f"{tag} mk")
                    elif dcol < 0:
                        P.memset("dve", sl, 0.0, tag=f"{tag} z")
                    elif dcol == wtiles:
                        P.tt("dve", sl, sl, msl.full(), ALU.mult,
                             tag=f"{tag} mw")
                    elif dcol > wtiles:
                        P.memset("dve", sl, 0.0, tag=f"{tag} z2")
                P.copy("dve", e2.full(), eb.full(), tag=f"{tag} es1")
                P.tt("pool", e1.full(), eb.full(), e2.full(), ALU.subtract,
                     tag=f"{tag} es2")
                return e1, e2

            epipe = {tks[0]: emit_sc(tks[0])}
            for i, tk in enumerate(tks):
                if i + 1 < len(tks):
                    epipe[tks[i + 1]] = emit_sc(tks[i + 1])
                e1, e2 = epipe[tk]
                P.mm_group([(vb1[:, tk, :], e2.full()),
                            (vb2[:, tk, :], e2.full()),
                            (vb1[:, tk, :], e1.full())], nps,
                           start=(i == 0), stop=(i == len(tks) - 1),
                           tag=f"{tag} av.{h}.{ts}.{tk}")
                P.mm_group([(onesc.full(), e2.full()),
                            (onesc.full(), e1.full())], dps,
                           start=(i == 0), stop=(i == len(tks) - 1),
                           tag=f"{tag} dn.{h}.{ts}.{tk}")
            dn = reg.get("msc1", (1, 512), F32, off=256)
            P.copy("dve", dn.full(), TV(dps.ap[0:1, :], dps.key, 0, 1,
                                        dps.b0, dps.b0 + 2048),
                   tag=f"{tag} dnc")
            P.recip(dn.full(), dn.full(), tag=f"{tag} rec")
            rb_ps = ctx.pb[7][:, 0:512]
            ones_r1 = reg.get("msc1", (1, 128), F32, off=1024)
            P.memset("pool", ones_r1.full(), 1.0, tag=f"{tag} o1")
            P.mm_group([(ones_r1.full(), dn.full())], rb_ps,
                       tag=f"{tag} rb.{h}.{ts}")
            rb = reg.get("lnx", (128, 512), F32, off=2048)
            P.copy("act", rb.full(), rb_ps, tag=f"{tag} rbc")
            att = reg.get("lnx", (128, 512), F32, off=2560)
            P.tt("dve", att.full(), nps, rb.full(), ALU.mult,
                 tag=f"{tag} norm")
            P.dma("pool",
                  TV(ao.base[4 * ts:4 * ts + 4, :, h, :]
                     .rearrange("t p w -> p t w"), ao.key, 0, 1, 0,
                     ao.free_total * nt * 4),
                  TV(att.full().ap.rearrange("p (t w) -> p t w", w=128),
                     att.full().key, 0, 128, att.full().b0,
                     att.full().b1),
                  tag=f"{tag} aw")

    emit_gemm(ctx, ao, w["wo"],
              res_add_cb(ctx, xres, out, T, D, nslab=nslab),
              T, D, D, wprep="f32r_split",
              nslab=nslab, tag=tag + ".o")



def emit_l2_ffn(ctx, xres, out, w, T=2048, D=2048, I=8192, nslab=512,
                tag="l2f"):
    P, reg = ctx.P, ctx.reg
    nt = T // 128
    sc = ctx.scratch
    xln = sc("s_x1", kt4_shape(T, D), F32)
    emit_ln(ctx, xres, T, D, dst_f32=xln, tag=tag + ".ln")

    onesr = ctx.R[1][0:1, 0:128]
    P.memset("dve", onesr, 1.0, tag=f"{tag} o")
    ident = ctx.sconst("ident", 128, 128)
    sgtm = sc("s_t1", (T, I), F32)
    hkt = sc("s_g1", kt4_shape(T, I), F32)

    def bcast_scale(slot, src_dram, ns, nm):
        bt = reg.get("msc0", (128, nslab), F32, off=slot * 512)
        stv = src_dram[ns * nslab:(ns + 1) * nslab]
        bsrc = TV(stv.ap.partition_broadcast(128), stv.key, stv.p0,
                  stv.p1, stv.b0, stv.b1)
        P.dma("sp", bt.full(), bsrc, tag=f"{tag} {nm}r{ns}")
        return bt

    gb_ref = {}

    def gate_cb(ps, tt, ns):
        if tt % 6 == 0:
            gb_ref[ns] = bcast_scale(0, w["gs"], ns, "gs")
        gb = gb_ref[ns]
        o = reg.get(f"reo{tt % 2}", (128, nslab), F32)
        P.tt("dve", o.full(), ps, gb.full(), ALU.mult, tag=f"{tag} gsc")
        P.act(o.full(), o.full(), AF.Silu, tag=f"{tag} silu")
        P.dma("pool", sgtm[tt * 128:(tt + 1) * 128,
                           ns * nslab:(ns + 1) * nslab], o.full(),
              tag=f"{tag} sgw")

    emit_gemm(ctx, xln, w["wg"], gate_cb, T, D, I,
              wprep="int_f32r", nslab=nslab, tag=tag + ".g")

    ub_ref = {}

    def up_cb(ps, tt, ns):
        if tt % 6 == 0:
            ub_ref[ns] = bcast_scale(1, w["us"], ns, "us")
        ub = ub_ref[ns]
        t_ = reg.get(f"reo{tt % 2}", (128, nslab), F32)
        P.tt("dve", t_.full(), ps, ub.full(), ALU.mult, tag=f"{tag} usc")
        sl = reg.get(f"rex{tt % 2}", (128, nslab), F32)
        P.dma("act", sl.full(), sgtm[tt * 128:(tt + 1) * 128,
                                     ns * nslab:(ns + 1) * nslab],
              tag=f"{tag} sgr")
        P.tt("dve", t_.full(), t_.full(), sl.full(), ALU.mult,
             tag=f"{tag} h")
        psT = ctx.pb[(tt % 2)]
        for j in range(nslab // 128):
            P.transpose(psT[:, j * 128:(j + 1) * 128],
                        TV(t_.full().ap[:, j * 128:(j + 1) * 128],
                           t_.full().key, 0, 128,
                           t_.full().b0 + j * 512,
                           t_.full().b0 + (j + 1) * 512),
                        ident.full(), tag=f"{tag} hT")
        o1 = reg.get("tsa", (128, nslab), F32)
        pv = psT[:, 0:nslab]
        P.copy("dve", o1.full(), pv, tag=f"{tag} hs1")
        P.dma("pool", kt4_block(hkt, tt, ns * (nslab // 128), nslab // 128),
              o1.full(), tag=f"{tag} hw1")

    emit_gemm(ctx, xln, w["wu"], up_cb, T, D, I,
              wprep="int_f32r", nslab=nslab, tag=tag + ".u")

    acc = sc("s_t2", (T, D), F32)
    kgc = min(16, I // 128)
    nkg = (I // 128) // kgc
    db_ref = {}

    def final_cb(ps, tt, ns):
        if tt % 6 == 0:
            db_ref[ns] = bcast_scale(2, w["ds"], ns, "ds")
        db = db_ref[ns]
        o = reg.get(f"reo{tt % 2}", (128, nslab), F32)
        if nkg > 1:
            a = reg.get(f"rex{tt % 2}", (128, nslab), F32)
            P.dma("act", a.full(), acc[tt * 128:(tt + 1) * 128,
                                       ns * nslab:(ns + 1) * nslab],
                  tag=f"{tag} far")
            P.tt("dve", o.full(), ps, a.full(), ALU.add, tag=f"{tag} fa")
            P.tt("dve", o.full(), o.full(), db.full(), ALU.mult,
                 tag=f"{tag} fs")
        else:
            P.tt("dve", o.full(), ps, db.full(), ALU.mult, tag=f"{tag} fs")
        xv = reg.get("tsd", (128, nslab), F32)
        P.dma("act", xv.full(), xres[tt * 128:(tt + 1) * 128,
                                     ns * nslab:(ns + 1) * nslab],
              tag=f"{tag} fx")
        P.tt("dve", o.full(), o.full(), xv.full(), ALU.add,
             tag=f"{tag} fr")
        P.dma("pool", out[tt * 128:(tt + 1) * 128,
                          ns * nslab:(ns + 1) * nslab], o.full(),
              tag=f"{tag} fw")

    for kg in range(nkg):
        last = (kg == nkg - 1)
        if last:
            cb = final_cb
        elif kg == 0:
            cb = store_cb(ctx, acc, T, D, nslab, tag)
        else:
            cb = accum_cb(ctx, acc, T, D, nslab, tag)
        emit_gemm(ctx, hkt, w["wd"], cb, T, kgc * 128, D,
                  wprep="int_f32r", nslab=nslab,
                  kgroup=(kg * kgc, kgc), tag=tag + f".d{kg}")



def emit_l3_attn(ctx, xres, out, w, T=2048, D=2048, NST=16, RNK=128,
                 nslab=512, tag="l3a"):
    P, reg = ctx.P, ctx.reg
    nt, nk = T // 128, D // 128
    NE = RNK + 2 * NST
    sc = ctx.scratch
    ident = ctx.sconst("ident", 128, 128)
    u_fm = sc("s_q1", kt4_shape(T, D), F32)
    emit_ln(ctx, xres, T, D, dst_f32=u_fm, tag=tag + ".ln")

    sdbl = sc("s_t1", (T, NE), F32)

    def dbl_cb(ps, tt, ns):
        o = reg.get(f"reo{tt % 2}", (128, NE), F32)
        P.copy("dve", o.full(), ps, tag=f"{tag} dble")
        P.dma("pool", sdbl[tt * 128:(tt + 1) * 128, :], o.full(),
              tag=f"{tag} dblw")

    emit_gemm(ctx, u_fm, w["xproj"], dbl_cb, T, D, NE,
              wprep="f32r_split", nslab=NE, tag=tag + ".dbl")

    drk = sc("s_t3", kt4_shape(T, RNK), F32)
    bct_d = sc("s_bct", (2 * NST, T), F32)
    for tt in range(nt):
        xt = reg.get("lnx", (128, NE), F32, off=(tt % 2) * 256)
        P.dma("sp", xt.full(), sdbl[tt * 128:(tt + 1) * 128, :],
              tag=f"{tag} dblr{tt}")
        ps = ctx.pb[tt % 2]
        P.transpose(ps[:, 0:128],
                    TV(xt.full().ap[:, 0:128], xt.full().key, 0, 128,
                       xt.full().b0, xt.full().b0 + 512),
                    ident.full(), tag=f"{tag} drT{tt}")
        o1 = reg.get("tsa", (128, 128), F32)
        pv = ps[:, 0:128]
        P.copy("dve", o1.full(), pv, tag=f"{tag} drs1")
        P.dma("pool", kt4_block(drk, tt, 0, 1), o1.full(),
              tag=f"{tag} drw1")
        ps2 = ctx.pb[2 + tt % 2]
        P.transpose(ps2[0:2 * NST, 0:128],
                    TV(xt.full().ap[:, RNK:NE], xt.full().key, 0, 128,
                       xt.full().b0 + RNK * 4, xt.full().b0 + NE * 4),
                    ident.full(), tag=f"{tag} bcT{tt}")
        ob = reg.get("tsc", (2 * NST, 128), F32)
        P.copy("dve", ob.full(), ps2[0:2 * NST, 0:128], tag=f"{tag} bce")
        P.dma("pool", bct_d[0:2 * NST, tt * 128:(tt + 1) * 128], ob.full(),
              tag=f"{tag} bcw{tt}")

    delta_tm = sc("s_t2", (T, D), F32)
    onesr = ctx.R[1][0:1, 0:128]
    P.memset("dve", onesr, 1.0, tag=f"{tag} o")
    btvs = []
    for i in range(D // nslab):
        rb = reg.get("msc1", (1, nslab), F32, off=i * nslab)
        P.dma("sp", rb.full(), w["dtb"][i * nslab:(i + 1) * nslab],
              tag=f"{tag} dtb{i}")
        btvs.append(rb.full())

    def d_cb(ps, tt, ns):
        t1 = reg.get("tsa", (128, nslab), F32)
        t2 = reg.get("tsb", (128, nslab), F32)
        o = reg.get(f"reo{tt % 2}", (128, nslab), F32)
        P.tscalar("dve", t2.full(), ps, 0.0, None, ALU.max,
                  tag=f"{tag} sp4")
        P.act(t1.full(), ps, AF.Abs, tag=f"{tag} sp1",
              extra=[t2.full()])
        P.act(t1.full(), t1.full(), AF.Exp, scale=-1.0, tag=f"{tag} sp2")
        P.act(t1.full(), t1.full(), AF.Ln, bias=1.0, tag=f"{tag} sp3")
        P.tt("dve", o.full(), t1.full(), t2.full(), ALU.add,
             tag=f"{tag} sp5")
        P.dma("pool", delta_tm[tt * 128:(tt + 1) * 128,
                               ns * nslab:(ns + 1) * nslab], o.full(),
              tag=f"{tag} dw")

    emit_gemm(ctx, drk, w["dtw"], d_cb, T, RNK, D,
              wprep="f32r_split", nslab=nslab,
              extra_pairs=lambda tt, ns: [(onesr, btvs[ns])],
              tag=tag + ".dt")

    df = sc("s_t5", kt4_shape(T, D), F32)
    emit_transpose_split(ctx, delta_tm, T, D, dst_f32=df, tag=tag + ".tdf")

    ao_tm = sc("s_t6", (T, D), F32)
    dcol = reg.get("st0", (128, nk, NST), F32, off=0)
    dvec = reg.get("st0", (128, nk), F32, off=nk * NST)
    hstate = reg.get("st1", (128, 2, NST), F32, off=0)
    P.dma("sp", dvec.full(), w["dvec"].full(), tag=f"{tag} dv")
    for c in range(nk):
        P.dma("sp", dcol[:, c, :], w["alog"][c * 128:(c + 1) * 128, :],
              tag=f"{tag} al{c}")
    P.act(dcol.full(), dcol.full(), AF.Exp, tag=f"{tag} ae")
    P.tscalar("dve", dcol.full(), dcol.full(), -1.0, None, ALU.mult,
              tag=f"{tag} an")

    CG = 2
    TH = T // 2
    nth = nt // 2
    for cg in range(nk // CG):
        cs = [cg * CG + i for i in range(CG)]
        for th in range(2):
            dl, du, yy = {}, {}, {}
            for i, c in enumerate(cs):
                dl[c] = reg.get("gwB", (128, TH), F32, off=i * TH)
                du[c] = reg.get("gwB", (128, TH), F32, off=(CG + i) * TH)
                yy[c] = reg.get("msc0", (128, TH), F32, off=i * TH)
                P.dma("sp", dl[c].full(),
                      TV(df.base[th * nth:(th + 1) * nth, :, c, :]
                         .rearrange("t p w -> p t w"), df.key, 0, 1, 0,
                         df.free_total * nt * 4), tag=f"{tag} dld{c}")
                ut = reg.get("gxB", (128, TH), F32, off=4 * TH)
                P.dma("sp", ut.full(),
                      TV(u_fm.base[th * nth:(th + 1) * nth, :, c, :]
                         .rearrange("t p w -> p t w"), u_fm.key, 0, 1, 0,
                         u_fm.free_total * nt * 4), tag=f"{tag} uld{c}")
                P.tt("dve", du[c].full(), dl[c].full(), ut.full(), ALU.mult,
                     tag=f"{tag} du{c}")
                P.memset("pool", yy[c].full(), 0.0, tag=f"{tag} y0{c}")
            for n in range(NST):
                bb = reg.get("gxB", (128, TH), F32, off=(n % 2) * TH)
                cbf = reg.get("gxB", (128, TH), F32, off=(2 + n % 2) * TH)
                P.dma("sp", bb.full(),
                      TV(bct_d.base[n:n + 1, th * TH:(th + 1) * TH]
                         .partition_broadcast(128), bct_d.key, 0, 1,
                         th * TH * 4, (th + 1) * TH * 4),
                      tag=f"{tag} bb{n}")
                P.dma("sp", cbf.full(),
                      TV(bct_d.base[NST + n:NST + n + 1,
                                    th * TH:(th + 1) * TH]
                         .partition_broadcast(128), bct_d.key, 0, 1,
                         (NST + n) * T * 4 + th * TH * 4,
                         (NST + n) * T * 4 + (th + 1) * TH * 4),
                      tag=f"{tag} cb{n}")
                for ci, c in enumerate(cs):
                    par = (n * CG + ci) % 2
                    dA = reg.get("lnx", (128, TH), F32, off=par * 2 * TH)
                    dBu = reg.get("lnx", (128, TH), F32,
                                  off=par * 2 * TH + TH)
                    hh = reg.get("stg", (128, TH), F32, off=par * TH)
                    P.act(dA.full(), dl[c].full(), AF.Exp,
                          scale=dcol[:, c, n:n + 1], tag=f"{tag} dA{c}.{n}")
                    P.tt("pool", dBu.full(), du[c].full(), bb.full(),
                         ALU.mult, tag=f"{tag} dBu{c}.{n}")
                    if th == 1:
                        tmp = reg.get("st1", (128, 1), F32, off=40)
                        P.tt("dve", tmp.full(), dA[:, 0:1],
                             hstate[:, ci, n:n + 1], ALU.mult,
                             tag=f"{tag} hc{c}.{n}")
                        P.tt("dve", dBu[:, 0:1], dBu[:, 0:1], tmp.full(),
                             ALU.add, tag=f"{tag} hc2{c}.{n}")
                    P.scan("dve", hh.full(), dA.full(), dBu.full(), 0.0,
                           ALU.mult, ALU.add, tag=f"{tag} sc{c}.{n}")
                    if th == 0:
                        P.copy("dve", hstate[:, ci, n:n + 1],
                               hh[:, TH - 1:TH], tag=f"{tag} hs{c}.{n}")
                    P.tt("dve", dA.full(), hh.full(), cbf.full(), ALU.mult,
                         tag=f"{tag} yc{c}.{n}")
                    P.tt("pool", yy[c].full(), yy[c].full(), dA.full(),
                         ALU.add, tag=f"{tag} ya{c}.{n}")
            for i, c in enumerate(cs):
                ut = reg.get("gxB", (128, TH), F32, off=4 * TH)
                P.dma("sp", ut.full(),
                      TV(u_fm.base[th * nth:(th + 1) * nth, :, c, :]
                         .rearrange("t p w -> p t w"), u_fm.key, 0, 1, 0,
                         u_fm.free_total * nt * 4), tag=f"{tag} uld2{c}")
                P.stt("dve", yy[c].full(), ut.full(), dvec[:, c:c + 1],
                      yy[c].full(), ALU.mult, ALU.add, tag=f"{tag} yf{c}")
                for q in range(nth):
                    tt = th * nth + q
                    ps = ctx.pb[4 + q % 2]
                    P.transpose(ps[:, 0:128],
                                TV(yy[c].full().ap[:, q * 128:(q + 1) * 128],
                                   yy[c].full().key, 0, 128,
                                   yy[c].full().b0 + q * 512,
                                   yy[c].full().b0 + (q + 1) * 512),
                                ident.full(), tag=f"{tag} yT{c}.{tt}")
                    o = reg.get("tsd", (128, 128), F32)
                    P.copy("dve", o.full(), ps[:, 0:128], tag=f"{tag} yE")
                    P.dma("pool", ao_tm[tt * 128:(tt + 1) * 128,
                                        c * 128:(c + 1) * 128], o.full(),
                          tag=f"{tag} yW{c}.{tt}")

    for tt in range(nt):
        a = reg.get("lnx", (128, D), F32, off=(tt % 2) * 2048)
        b = reg.get("stg", (128, D), F32, off=0)
        P.dma("sp", a.full(), ao_tm[tt * 128:(tt + 1) * 128, :],
              tag=f"{tag} ra")
        P.dma("act", b.full(), xres[tt * 128:(tt + 1) * 128, :],
              tag=f"{tag} rx")
        P.tt("dve", a.full(), a.full(), b.full(), ALU.add, tag=f"{tag} rr")
        P.dma("pool", out[tt * 128:(tt + 1) * 128, :], a.full(),
              tag=f"{tag} rw")



def emit_l3_ffn(ctx, xres, out, w, T=2048, D=2048, E=8, FF=512,
                nslab=512, tag="l3f"):
    P, reg = ctx.P, ctx.reg
    nt, nk = T // 128, D // 128
    EF = E * FF
    nkf = FF // 128
    sc = ctx.scratch
    xlnf = sc("s_x1", kt4_shape(T, D), F32)
    xb = sc("s_xb", kt4_shape(T, D), BF16)
    emit_ln(ctx, xres, T, D, dst_f32=xlnf, dst_bf16=xb,
            tag=tag + ".ln")

    wn = reg.get("msc0", (128, nt * E), F32, off=1024)

    def r_cb(ps, tt, ns):
        s_ = ctx.S
        sco = s_[5][:, 0:E]
        P.copy("dve", sco, ps, tag=f"{tag} le")
        v1 = s_[6][:, 8:9]
        P.reduce("dve", v1, sco, ALU.max, tag=f"{tag} v1")
        eq1 = s_[7][:, 0:E]
        P.tscalar("dve", eq1, sco, v1, None, ALU.is_ge, tag=f"{tag} eq1")
        s2 = s_[5][:, 8:8 + E]
        P.stt("dve", s2, eq1, -1e30, sco, ALU.mult, ALU.add,
              tag=f"{tag} s2")
        v2 = s_[6][:, 9:10]
        P.reduce("dve", v2, s2, ALU.max, tag=f"{tag} v2")
        bb = s_[6][:, 10:11]
        P.tscalar("dve", bb, v2, v1, None, ALU.subtract, tag=f"{tag} d21")
        P.act(bb, bb, AF.Exp, tag=f"{tag} eb")
        w1c = s_[6][:, 11:12]
        P.tscalar("dve", w1c, bb, 1.0, None, ALU.add, tag=f"{tag} 1b")
        P.recip(w1c, w1c, tag=f"{tag} w1")
        w2c = s_[6][:, 12:13]
        P.tt("dve", w2c, bb, w1c, ALU.mult, tag=f"{tag} w2")
        m2 = s_[7][:, 8:8 + E]
        P.tscalar("dve", m2, sco, v2, None, ALU.is_ge, tag=f"{tag} m2")
        P.tt("dve", m2, m2, eq1, ALU.subtract, tag=f"{tag} msec")
        P.tscalar("dve", eq1, eq1, w1c, None, ALU.mult, tag=f"{tag} a1")
        P.tscalar("dve", m2, m2, w2c, None, ALU.mult, tag=f"{tag} a2")
        P.tt("dve", wn[:, tt * E:(tt + 1) * E], eq1, m2, ALU.add,
             tag=f"{tag} wm")

    emit_gemm(ctx, xlnf, w["rw"], r_cb, T, D, E,
              wprep="f32r_split", nslab=E, tag=tag + ".r")

    s1tm = sc("s_b1", (T, EF), BF16)
    identb = ctx.sconst("ident_bf", 128, 128, dtype=BF16)

    def h1_cb(ps, tt, ns):
        c1 = reg.get("tsa", (128, FF), BF16)
        P.copy("dve", c1.full(), ps, tag=f"{tag} c1")
        o = reg.get("tsb", (128, FF), BF16)
        P.act(o.full(), c1.full(), AF.Silu, tag=f"{tag} silu")
        P.dma("pool", s1tm[tt * 128:(tt + 1) * 128,
                           ns * FF:(ns + 1) * FF], o.full(),
              tag=f"{tag} s1w")

    emit_gemm(ctx, xb, w["w1"], h1_cb, T, D, EF,
              wprep="bf16", nslab=FF, tag=tag + ".h1")

    hT = sc("s_b2", kt4_shape(T, EF), BF16)

    def h3_cb(ps, tt, ns):
        c3 = reg.get("tsa", (128, FF), BF16)
        P.copy("dve", c3.full(), ps, tag=f"{tag} c3")
        sl = reg.get("tsb", (128, FF), BF16)
        P.dma("act", sl.full(), s1tm[tt * 128:(tt + 1) * 128,
                                     ns * FF:(ns + 1) * FF],
              tag=f"{tag} s1r")
        h = reg.get("tsc", (128, FF), BF16)
        P.tt("dve", h.full(), c3.full(), sl.full(), ALU.mult,
             tag=f"{tag} h")
        psT = ctx.pb[(tt % 2)].cast(BF16)
        for j in range(nkf):
            P.transpose(psT[:, j * 128:(j + 1) * 128],
                        TV(h.full().ap[:, j * 128:(j + 1) * 128],
                           h.full().key, 0, 128,
                           h.full().b0 + j * 256,
                           h.full().b0 + (j + 1) * 256),
                        identb.full(), tag=f"{tag} hT")
        ob = reg.get("tsd", (128, FF), BF16)
        P.copy("dve", ob.full(), psT[:, 0:FF], tag=f"{tag} hE")
        P.dma("pool", kt4_block(hT, tt, ns * nkf, nkf), ob.full(),
              tag=f"{tag} hw")

    emit_gemm(ctx, xb, w["w3"], h3_cb, T, D, EF,
              wprep="bf16", nslab=FF, tag=tag + ".h3")

    nsl = D // nslab
    for ns in range(nsl):
        for ttg in range(nt // 2):
            tts = (2 * ttg, 2 * ttg + 1)
            hb = {}
            for i, tt in enumerate(tts):
                hb[tt] = reg.get("gxB", (128, EF // 128, 128), BF16,
                                 off=i * (EF // 128) * 128)
                P.dma("sp", TV(hb[tt].full().ap.rearrange(
                    "p c t -> p (c t)"),
                    "rg_gxB", 0, 128, hb[tt].full().b0, hb[tt].full().b1),
                    kt4_block(hT, tt), tag=f"{tag} hld{tt}")
                osum = reg.get("lnx", (128, nslab), F32, off=i * nslab)
                P.memset("dve", osum.full(), 0.0, tag=f"{tag} os0")
            for e in range(E):
                w2c = reg.get("gwB", (128, nkf, nslab), BF16,
                              off=(e % 2) * nkf * nslab)
                for c in range(nkf):
                    st = reg.get(f"st{c % 2}", (128, nslab), F32)
                    P.dma("sp", st.full(),
                          w["w2"][ns, :, e * nkf + c, :],
                          tag=f"{tag} w2l{e}.{c}")
                    P.copy("dve", w2c[:, c, :], st.full(),
                           tag=f"{tag} w2c{e}.{c}")
                for i, tt in enumerate(tts):
                    ps = ctx.pb[4 + (i + e * 2) % 2][:, 0:nslab]
                    pairs = [(hb[tt][:, e * nkf + c, :], w2c[:, c, :])
                             for c in range(nkf)]
                    P.mm_group(pairs, ps, tag=f"{tag} dmm{e}.{tt}.{ns}")
                    rb = reg.get("tsa", (128, nslab), BF16, off=(i % 2) *
                                 nslab)
                    P.copy("dve", rb.full(), ps, tag=f"{tag} rb")
                    osum = reg.get("lnx", (128, nslab), F32, off=i * nslab)
                    P.stt("dve", osum.full(), rb.full(),
                          wn[:, tt * E + e:tt * E + e + 1], osum.full(),
                          ALU.mult, ALU.add, tag=f"{tag} osa")
            for i, tt in enumerate(tts):
                osum = reg.get("lnx", (128, nslab), F32, off=i * nslab)
                xv = reg.get("lnx", (128, nslab), F32, off=2 * nslab + 
                             (i % 2) * nslab)
                P.dma("act", xv.full(), xres[tt * 128:(tt + 1) * 128,
                                             ns * nslab:(ns + 1) * nslab],
                      tag=f"{tag} fx")
                P.tt("dve", osum.full(), osum.full(), xv.full(), ALU.add,
                     tag=f"{tag} fr")
                P.dma("pool", out[tt * 128:(tt + 1) * 128,
                                  ns * nslab:(ns + 1) * nslab],
                      osum.full(), tag=f"{tag} fw")



T_, D_ = 2048, 2048


def _host_prep(inputs):
    g = lambda k: np.asarray(inputs[k])
    w = {}
    for k in ("Wq", "Wk", "Wv", "Wo"):
        w["l0a_" + k.lower()] = host_wslab(
            np.ascontiguousarray(g("L0_attn_" + k)), 512)
    w["l0f_w"] = np.ascontiguousarray(g("L0_ffn_weight"))
    w["l0f_b"] = np.ascontiguousarray(g("L0_ffn_bias"))
    w["l0f_ts"] = np.asarray(g("L0_ffn_x_tensor_scale"),
                             np.float32).reshape(1, 1)
    for k in ("Wq", "Wk", "Wv", "Wo"):
        w["l1a_" + k.lower()] = host_wslab(
            np.ascontiguousarray(g("L1_attn_" + k).T), 512)
    for k in ("bq", "bk", "bv", "bo"):
        w["l1a_" + k] = np.ascontiguousarray(g("L1_attn_" + k))
    w["l1f_wg"] = host_wslab(np.ascontiguousarray(g("L1_ffn_Wg").T), 8)
    w["l1f_w1"] = host_wslab(np.ascontiguousarray(
        g("L1_ffn_W1").reshape(8192, 2048).T), 512)
    w["l1f_w3"] = host_wslab(np.ascontiguousarray(
        g("L1_ffn_W3").reshape(8192, 2048).T), 512)
    w["l1f_w2"] = host_wslab(np.ascontiguousarray(
        g("L1_ffn_W2").transpose(0, 2, 1).reshape(8192, 2048)), 512)
    w["l1f_bias"] = np.ascontiguousarray(g("L1_ffn_bias"))
    w["l2a_wq"] = host_wslab(np.ascontiguousarray(g("L2_attn_Wq").T), 512)
    w["l2a_wk"] = host_wslab(np.ascontiguousarray(g("L2_attn_Wk").T), 512)
    w["l2a_wv"] = host_wslab(np.ascontiguousarray(g("L2_attn_Wv").T), 512)
    w["l2a_wo"] = host_wslab(np.ascontiguousarray(g("L2_attn_Wo").T), 512)
    w["l2f_wg"] = host_wslab(np.ascontiguousarray(g("L2_ffn_gate_w").T), 512)
    w["l2f_wu"] = host_wslab(np.ascontiguousarray(g("L2_ffn_up_w").T), 512)
    w["l2f_wd"] = host_wslab(np.ascontiguousarray(g("L2_ffn_down_w").T), 512)
    w["l2f_gs"] = np.ascontiguousarray(g("L2_ffn_gate_scale"))
    w["l2f_us"] = np.ascontiguousarray(g("L2_ffn_up_scale"))
    w["l2f_ds"] = np.ascontiguousarray(g("L2_ffn_down_scale"))
    w["l3a_xproj"] = host_wslab(np.ascontiguousarray(
        g("L3_attn_x_proj_w").T), 160)
    w["l3a_dtw"] = host_wslab(np.ascontiguousarray(
        g("L3_attn_dt_proj_w").T), 512)
    w["l3a_dtb"] = np.ascontiguousarray(g("L3_attn_dt_proj_b"))
    w["l3a_alog"] = np.ascontiguousarray(g("L3_attn_A_log"))
    w["l3a_dvec"] = np.ascontiguousarray(
        g("L3_attn_D").reshape(16, 128).T)
    w["l3f_rw"] = host_wslab(np.ascontiguousarray(g("L3_ffn_router_w").T), 8)
    w["l3f_w1"] = host_wslab(np.ascontiguousarray(
        g("L3_ffn_w1").transpose(1, 0, 2).reshape(2048, 4096)), 512)
    w["l3f_w3"] = host_wslab(np.ascontiguousarray(
        g("L3_ffn_w3").transpose(1, 0, 2).reshape(2048, 4096)), 512)
    w["l3f_w2"] = host_wslab(np.ascontiguousarray(
        g("L3_ffn_w2").reshape(4096, 2048)), 512)
    return w


_BLOCK_WDEFS = {
    "l0a": [("l0a_wq", None, F32), ("l0a_wk", None, F32),
            ("l0a_wv", None, F32), ("l0a_wo", None, F32)],
    "l0f": [("l0f_w", (2048, 2048), F32), ("l0f_b", (2048,), F32),
            ("l0f_ts", (1, 1), F32)],
    "l1a": [("l1a_wq", None, F32), ("l1a_wk", None, F32),
            ("l1a_wv", None, F32), ("l1a_wo", None, F32),
            ("l1a_bq", (2048,), F32), ("l1a_bk", (2048,), F32),
            ("l1a_bv", (2048,), F32), ("l1a_bo", (2048,), F32)],
    "l1f": [("l1f_wg", wslab_shape(2048, 8, 8), F32),
            ("l1f_w1", wslab_shape(2048, 8192), F32),
            ("l1f_w3", wslab_shape(2048, 8192), F32),
            ("l1f_w2", wslab_shape(8192, 2048), F32),
            ("l1f_bias", (8,), F32)],
    "l2a": [("l2a_wq", None, F32),
            ("l2a_wk", wslab_shape(2048, 512, 512), F32),
            ("l2a_wv", wslab_shape(2048, 512, 512), F32),
            ("l2a_wo", None, F32)],
    "l2f": [("l2f_wg", wslab_shape(2048, 8192), I32),
            ("l2f_wu", wslab_shape(2048, 8192), I32),
            ("l2f_wd", wslab_shape(8192, 2048), I32),
            ("l2f_gs", (8192,), F32), ("l2f_us", (8192,), F32),
            ("l2f_ds", (2048,), F32)],
    "l3a": [("l3a_xproj", wslab_shape(2048, 160, 160), F32),
            ("l3a_dtw", wslab_shape(128, 2048), F32),
            ("l3a_dtb", (2048,), F32), ("l3a_alog", (2048, 16), F32),
            ("l3a_dvec", (128, 16), F32)],
    "l3f": [("l3f_rw", wslab_shape(2048, 8, 8), F32),
            ("l3f_w1", wslab_shape(2048, 4096, 512), F32),
            ("l3f_w3", wslab_shape(2048, 4096, 512), F32),
            ("l3f_w2", wslab_shape(4096, 2048), F32)],
}


def _declare_block_inputs(P, block):
    w = {}
    for name, shape, dt in _BLOCK_WDEFS[block]:
        if shape is None:
            shape = wslab_shape(2048, 2048, 512)
        short = name.split("_", 1)[1]
        w[short] = P.dram(name, shape, dt, kind="ExternalInput")
    return w


def _emit_block(ctx, block, xin, xout, w):
    if block == "l0a":
        emit_l0_attn(ctx, xin, xout, w, T_, D_, 16)
    elif block == "l0f":
        emit_l0_ffn(ctx, xin, xout, w["w"], w["b"], w["ts"], T_, D_)
    elif block == "l1a":
        emit_l1_attn(ctx, xin, xout, w, T_, D_, 16)
    elif block == "l1f":
        emit_l1_ffn(ctx, xin, xout, w, T_, D_, 8, 1024)
    elif block == "l2a":
        emit_l2_attn(ctx, xin, xout, w, T_, D_, 16, 4, 512)
    elif block == "l2f":
        emit_l2_ffn(ctx, xin, xout, w, T_, D_, 8192)
    elif block == "l3a":
        emit_l3_attn(ctx, xin, xout, w, T_, D_, 16, 128)
    elif block == "l3f":
        emit_l3_ffn(ctx, xin, xout, w, T_, D_, 8, 512)


BLOCKS = ("l0a", "l0f", "l1a", "l1f", "l2a", "l2f", "l3a", "l3f")

_prog_cache = {}


def _build_block_prog(block):
    if block in _prog_cache:
        return _prog_cache[block]
    nc = bass.Bass("TRN2", target_bir_lowering=False)
    P = Prog(nc)
    x_in = P.dram("x", (T_, D_), F32, kind="ExternalInput")
    y_out = P.dram("y", (T_, D_), F32, kind="ExternalOutput")
    w = _declare_block_inputs(P, block)
    ctx = Ctx(P, std_consts(T_))
    _emit_block(ctx, block, x_in, y_out, w)
    P.emit()
    _prog_cache[block] = (nc, ctx)
    return nc, ctx


def _build_full_prog(nblocks=8):
    key = ("full", nblocks)
    if key in _prog_cache:
        return _prog_cache[key]
    nc = bass.Bass("TRN2", target_bir_lowering=False)
    P = Prog(nc)
    x_in = P.dram("x", (T_, D_), F32, kind="ExternalInput")
    y_out = P.dram("y", (T_, D_), F32, kind="ExternalOutput")
    ctx = Ctx(P, std_consts(T_))
    resA = ctx.scratch("s_resA", (T_, D_), F32)
    resB = ctx.scratch("s_resB", (T_, D_), F32)
    chain = [x_in, resA, resB, resA, resB, resA, resB, resA]
    chain = chain[:nblocks] + [y_out]
    for i, block in enumerate(BLOCKS[:nblocks]):
        w = _declare_block_inputs(P, block)
        _emit_block(ctx, block, chain[i], chain[i + 1], w)
    P.emit()
    _prog_cache[key] = (nc, ctx)
    return nc, ctx


FUSED = True


def kernel(**inputs):
    x = np.ascontiguousarray(np.asarray(inputs["L0_attn_x"], np.float32))
    B = x.shape[0]
    w = _host_prep(inputs)
    shards = [np.ascontiguousarray(x[c]) for c in range(B)]
    if FUSED:
        nc, ctx = _build_full_prog()
        in_maps = []
        for c in range(B):
            im = {"x": shards[c]}
            for block in BLOCKS:
                for name, shape, dt in _BLOCK_WDEFS[block]:
                    im[name] = w[name]
            for n, a in ctx.const_arrays.items():
                im["c_" + n] = a
            in_maps.append(im)
        res = run_bass_kernel_spmd(nc, in_maps, core_ids=list(range(B)))
        return np.stack([res.results[c]["y"] for c in range(B)], 0)
    for block in BLOCKS:
        nc, ctx = _build_block_prog(block)
        in_maps = []
        for c in range(B):
            im = {"x": shards[c]}
            for name, shape, dt in _BLOCK_WDEFS[block]:
                im[name] = w[name]
            for n, a in ctx.const_arrays.items():
                im["c_" + n] = a
            in_maps.append(im)
        res = run_bass_kernel_spmd(nc, in_maps, core_ids=list(range(B)))
        shards = [res.results[c]["y"] for c in range(B)]
    return np.stack(shards, 0)


# version 06
# speedup vs baseline: 1.2156x; 1.2156x over previous
import numpy as np
import concourse.bass as bass
import concourse.mybir as mybir
from concourse.bass_utils import run_bass_kernel_spmd

F32 = mybir.dt.float32
F32R = mybir.dt.float32r
BF16 = mybir.dt.bfloat16
I32 = mybir.dt.int32
AF = mybir.ActivationFunctionType
ALU = mybir.AluOpType
AX = mybir.AxisListType

MAGIC = float(np.float32(12582912.0))
MASK11 = -4096
EXPMASK = 0x7F800000

ENGINES = ("pe", "dve", "act", "pool", "sp")



class TV:
    __slots__ = ("ap", "key", "p0", "p1", "b0", "b1")

    def __init__(self, ap, key, p0, p1, b0, b1):
        self.ap, self.key = ap, key
        self.p0, self.p1, self.b0, self.b1 = p0, p1, b0, b1

    def overlaps(self, o):
        return (self.key == o.key and self.p0 < o.p1 and o.p0 < self.p1
                and self.b0 < o.b1 and o.b0 < self.b1)

    def covers(self, o):
        return (self.key == o.key and self.p0 <= o.p0 and self.p1 >= o.p1
                and self.b0 <= o.b0 and self.b1 >= o.b1)

    def r32(self):
        return TV(self.ap.bitcast(F32R), self.key, self.p0, self.p1,
                  self.b0, self.b1)

    def i32(self):
        return TV(self.ap.bitcast(I32), self.key, self.p0, self.p1,
                  self.b0, self.b1)


def _srange(idx, n):
    if isinstance(idx, slice):
        start, stop, step = idx.indices(n)
        assert step == 1
        return start, stop
    i = int(idx)
    return i, i + 1


class Buf:

    def __init__(self, base_ap, key, shape, dtype, space):
        self.base = base_ap
        self.key = key
        self.shape = tuple(shape)
        self.dtype = dtype
        self.space = space
        self.esize = mybir.dt.size(dtype)
        strides = []
        acc = 1
        for s in reversed(self.shape[1:]):
            strides.append(acc)
            acc *= s
        self.free_strides = tuple(reversed(strides))
        self.free_total = acc

    def cast(self, dtype, shape=None):
        if shape is None:
            f = self.esize // mybir.dt.size(dtype)
            if f >= 1:
                shape = self.shape[:-1] + (self.shape[-1] * f,)
            else:
                k = mybir.dt.size(dtype) // self.esize
                assert self.shape[-1] % k == 0
                shape = self.shape[:-1] + (self.shape[-1] // k,)
        ap = self.base.bitcast(dtype)
        b = Buf(ap, self.key, shape, dtype, self.space)
        return b

    def reshape(self, shape):
        total = 1
        for s in shape[1:]:
            total *= s
        assert shape[0] == self.shape[0] and total == self.free_total, \
            (shape, self.shape)
        assert len(self.shape) == 2, "reshape only from 2D"
        if len(shape) == 2:
            return self
        names = [f"b{i}" for i in range(len(shape) - 1)]
        pat = "p (" + " ".join(names) + ") -> p " + " ".join(names)
        kw = {n: s for n, s in zip(names, shape[1:])}
        ap = self.base.rearrange(pat, **kw)
        return Buf(ap, self.key, shape, self.dtype, self.space)

    def __getitem__(self, idx):
        if not isinstance(idx, tuple):
            idx = (idx,)
        idx = idx + (slice(None),) * (len(self.shape) - len(idx))
        p0, p1 = _srange(idx[0], self.shape[0])
        b_lo, b_hi = 0, 1
        for i, ix in enumerate(idx[1:]):
            s0, s1 = _srange(ix, self.shape[1 + i])
            b_lo += s0 * self.free_strides[i]
            b_hi += (s1 - 1) * self.free_strides[i]
        ap = self.base[idx]
        if self.space == "dram":
            lo = (p0 * self.free_total + b_lo) * self.esize
            hi = ((p1 - 1) * self.free_total + b_hi) * self.esize
            return TV(ap, self.key, 0, 1, lo, hi)
        return TV(ap, self.key, p0, p1, b_lo * self.esize, b_hi * self.esize)

    def full(self):
        return self[(slice(None),) * len(self.shape)]


class Op:
    __slots__ = ("eng", "fn", "reads", "writes", "idx", "inc", "waits",
                 "tag", "n_inst")

    def __init__(self, eng, fn, reads, writes, idx, inc, tag, n_inst=1):
        self.eng, self.fn = eng, fn
        self.reads, self.writes = reads, writes
        self.idx, self.inc = idx, inc
        self.waits = []
        self.tag = tag
        self.n_inst = n_inst


class Prog:
    ANNOTATE = False
    """Records ops with dependency tracking; emits a raw-Bass Block."""

    def __init__(self, nc):
        self.nc = nc
        self.ops = {e: [] for e in ENGINES}
        self.sems = {}
        for e in ENGINES:
            self.sems["e_" + e] = nc.alloc_semaphore(name="e_" + e)
        for q in ("sp", "act", "pool"):
            self.sems["q_" + q] = nc.alloc_semaphore(name="q_" + q)
        self.counts = {e: 0 for e in ENGINES}
        self.qcounts = {q: 0 for q in ("sp", "act", "pool")}
        self.mem = {}
        self.sbuf_bytes = 0
        self.n_inst = 0
        self._fconsts = {}
        self._fconst_buf = None
        self.qbounds = {q: [] for q in ("sp", "act", "pool")}
        self.qbound_starts = {q: set() for q in ("sp", "act", "pool")}
        self.pe_meta = []

    def float_const(self, val):
        if val in self._fconsts:
            return self._fconsts[val]
        if self._fconst_buf is None:
            self._fconst_buf = self.sbuf("fconst", (128, 8), F32)
        i = len(self._fconsts)
        assert i < 8
        tv = self._fconst_buf[:, i:i + 1]
        self.memset("pool", tv, val, tag=f"fconst {val}")
        self.nc.const_aps.register(F32, val, tv.ap)
        self._fconsts[val] = tv
        return tv

    def sbuf(self, name, shape, dtype):
        h = self.nc.alloc_sbuf_tensor(name, list(shape), dtype)
        b = Buf(h.ap(), name, shape, dtype, "sbuf")
        n = 1
        for s in shape[1:]:
            n *= s
        self.sbuf_bytes += n * mybir.dt.size(dtype)
        return b

    def psum(self, name, shape=(128, 512), dtype=F32):
        h = self.nc.alloc_psum_tensor(name, list(shape), dtype)
        return Buf(h.ap(), name, shape, dtype, "psum")

    def dram(self, name, shape, dtype, kind="Internal"):
        h = self.nc.dram_tensor(name, list(shape), dtype, kind=kind)
        return Buf(h.ap(), name, shape, dtype, "dram")

    def _record(self, eng, fn, reads, writes, queue=None, tag="", n_inst=1):
        reads = [r for r in reads if r is not None]
        writes = [w for w in writes if w is not None]
        if queue is None:
            src = ("e", eng, self.counts[eng])
            self.counts[eng] += 1
            inc = ("e_" + eng, 1)
        else:
            src = ("q", queue, self.qcounts[queue])
            self.qcounts[queue] += 1
            inc = ("q_" + queue, 16)
        if eng in ("act", "dve"):
            promoted = [r for r in reads if r.key.startswith("pb")]
            if promoted:
                writes = writes + promoted
        op = Op(eng, fn, reads, writes, src[2], inc, tag, n_inst)
        self.n_inst += n_inst

        deps = []
        for r in reads:
            lst = self.mem.get(r.key)
            if lst:
                deps += [s for tv, s in lst["w"] if tv.overlaps(r)]
        for w in writes:
            lst = self.mem.get(w.key)
            if lst:
                deps += [s for tv, s in lst["w"] if tv.overlaps(w)]
                deps += [s for tv, s in lst["r"] if tv.overlaps(w)]
        waits = {}
        for d in deps:
            if d == src:
                continue
            if (d[0] == "e" and src[0] == "e" and d[1] == src[1] == "pe"):
                continue
            sem = ("e_" if d[0] == "e" else "q_") + d[1]
            if d[0] == "e":
                val = d[2] + 1
            else:
                q, j = d[1], d[2]
                b = None
                for t in self.qbounds[q]:
                    if t >= j + 1:
                        b = t
                        break
                if b is None:
                    b = self.qcounts[q]
                    if src == ("q", q, b):
                        self.qbound_starts[q].add(b)
                        self.qbounds[q].append(b)
                    else:
                        self.qbounds[q].append(b)
                        self.qbound_starts[q].add(b)
                val = 16 * b
            waits[sem] = max(waits.get(sem, 0), val)
        op.waits = sorted(waits.items())

        for w in writes:
            lst = self.mem.setdefault(w.key, {"w": [], "r": []})
            lst["w"] = [(tv, s) for (tv, s) in lst["w"] if not w.covers(tv)]
            lst["r"] = [(tv, s) for (tv, s) in lst["r"] if not w.covers(tv)]
            lst["w"].append((w, src))
        for r in reads:
            lst = self.mem.setdefault(r.key, {"w": [], "r": []})
            lst["r"].append((r, src))
        self.ops[eng].append(op)
        return op

    def dma(self, eng, out_tv, in_tv, tag=""):
        def fn(e):
            return e.dma_start(out_tv.ap, in_tv.ap)
        return self._record(eng, fn, [in_tv], [out_tv], queue=eng, tag=tag)

    def memset(self, eng, out_tv, val, tag=""):
        def fn(e):
            return e.memset(out_tv.ap, val)
        return self._record(eng, fn, [], [out_tv], tag=tag)

    def mm_group(self, pairs, out_tv, tag="", start=True, stop=True):
        n = len(pairs)

        def fn(e):
            last = None
            for i, (l, r) in enumerate(pairs):
                last = e.matmul(out_tv.ap, l.ap, r.ap,
                                start=(start and i == 0),
                                stop=(stop and i == n - 1))
            return last
        reads = []
        for l, r in pairs:
            reads += [l, r]
        self.pe_meta.append([("M", tuple(l.ap.shape), tuple(r.ap.shape))
                             for l, r in pairs])
        return self._record("pe", fn, reads, [out_tv], tag=tag, n_inst=n)

    def transpose(self, out_psum_tv, in_tv, ident_tv, tag=""):
        def fn(e):
            return e.transpose(out_psum_tv.ap, in_tv.ap, ident_tv.ap)
        self.pe_meta.append([("T", tuple(in_tv.ap.shape))])
        return self._record("pe", fn, [in_tv, ident_tv], [out_psum_tv], tag=tag)

    def act(self, out_tv, in_tv, func, bias=0.0, scale=1.0, tag="",
            extra=()):
        extra = list(extra)
        kb = bias
        if isinstance(bias, TV):
            kb = bias.ap
            extra.append(bias)
        elif float(bias) != 0.0:
            extra.append(self.float_const(float(bias)))
        ks = scale
        if isinstance(scale, TV):
            ks = scale.ap
            extra.append(scale)
        elif float(scale) != 1.0:
            extra.append(self.float_const(float(scale)))

        def fn(e):
            return e.activation(out_tv.ap, in_tv.ap, func, bias=kb, scale=ks)
        return self._record("act", fn, [in_tv] + extra, [out_tv], tag=tag)

    def tscalar(self, eng, out_tv, in_tv, s1, s2, op0, op1=None, tag="",
                extra=()):
        extra = list(extra)
        a1 = s1
        if isinstance(s1, TV):
            a1 = s1.ap
            extra.append(s1)
        a2 = s2
        if isinstance(s2, TV):
            a2 = s2.ap
            extra.append(s2)

        def fn(e):
            if op1 is None:
                return e.tensor_scalar(out_tv.ap, in_tv.ap, a1, a2, op0)
            return e.tensor_scalar(out_tv.ap, in_tv.ap, a1, a2, op0, op1)
        return self._record(eng, fn, [in_tv] + extra, [out_tv], tag=tag)

    def tt(self, eng, out_tv, in0_tv, in1_tv, op, tag=""):
        def fn(e):
            return e.tensor_tensor(out_tv.ap, in0_tv.ap, in1_tv.ap, op)
        return self._record(eng, fn, [in0_tv, in1_tv], [out_tv], tag=tag)

    def stt(self, eng, out_tv, in0_tv, scalar, in1_tv, op0, op1, tag=""):
        extra = []
        sc = scalar
        if isinstance(scalar, TV):
            sc = scalar.ap
            extra.append(scalar)

        def fn(e):
            return e.scalar_tensor_tensor(out_tv.ap, in0_tv.ap, sc, in1_tv.ap,
                                          op0, op1)
        return self._record(eng, fn, [in0_tv, in1_tv] + extra, [out_tv],
                            tag=tag)

    def reduce(self, eng, out_tv, in_tv, op, axis=AX.X, absval=None,
               negate=None, tag=""):
        def fn(e):
            return e.tensor_reduce(out_tv.ap, in_tv.ap, axis, op,
                                   apply_absolute_value=absval, negate=negate)
        return self._record(eng, fn, [in_tv], [out_tv], tag=tag)

    def scan(self, eng, out_tv, d0_tv, d1_tv, init, op0, op1, tag=""):
        def fn(e):
            return e.tensor_tensor_scan(out_tv.ap, d0_tv.ap, d1_tv.ap, init,
                                        op0, op1)
        return self._record(eng, fn, [d0_tv, d1_tv], [out_tv], tag=tag)

    def recip(self, out_tv, in_tv, tag=""):
        def fn(e):
            return e.reciprocal(out_tv.ap, in_tv.ap)
        return self._record("dve", fn, [in_tv], [out_tv], tag=tag)

    def copy(self, eng, out_tv, in_tv, tag="", extra=()):
        if eng == "act":
            def fn(e):
                return e.copy(out_tv.ap, in_tv.ap)
        else:
            def fn(e):
                return e.tensor_copy(out_tv.ap, in_tv.ap)
        return self._record(eng, fn, [in_tv] + list(extra), [out_tv],
                            tag=tag)

    def select(self, out_tv, mask_tv, t_tv, f_tv, tag=""):
        def fn(e):
            return e.select(out_tv.ap, mask_tv.ap, t_tv.ap, f_tv.ap)
        return self._record("dve", fn, [mask_tv, t_tv, f_tv], [out_tv],
                            tag=tag, n_inst=2)

    def emit(self):
        nc = self.nc
        sem_handles = self.sems

        def replay(eng_name):
            qb = self.qbound_starts.get(eng_name, set())

            def body(e):
                waited = {}
                for op in self.ops[eng_name]:
                    if op.inc[0].startswith("q_") and op.idx in qb:
                        sem = "q_" + eng_name
                        if waited.get(sem, 0) < 16 * op.idx:
                            e.wait_ge(sem_handles[sem], 16 * op.idx)
                            waited[sem] = 16 * op.idx
                    for sem, val in op.waits:
                        if waited.get(sem, 0) < val:
                            e.wait_ge(sem_handles[sem], val)
                            waited[sem] = val
                    inst = op.fn(e)
                    if Prog.ANNOTATE:
                        inst.annotate(op.tag)
                    sem, amt = op.inc
                    inst.then_inc(sem_handles[sem], amt)
                if eng_name == "sp":
                    for q, cnt in self.qcounts.items():
                        if cnt:
                            e.wait_ge(sem_handles["q_" + q], 16 * cnt)
            return body

        with nc.Block() as block:
            block.tensor(replay("pe"))
            block.vector(replay("dve"))
            block.scalar(replay("act"))
            block.gpsimd(replay("pool"))
            block.sync(replay("sp"))



class Ctx:

    def __init__(self, P: Prog, host_consts: dict):
        self.P = P
        self.consts = {}
        self.const_arrays = dict(host_consts)
        for name, arr in host_consts.items():
            dt = {np.dtype(np.float32): F32,
                  np.dtype(np.int32): I32}[arr.dtype]
            self.consts[name] = P.dram("c_" + name, arr.shape, dt,
                                       kind="ExternalInput")
        self.pb = [P.psum(f"pb{i}") for i in range(8)]
        self.S = [P.sbuf(f"ar_s{i}", (128, 16), F32) for i in range(8)]
        self.R = [P.sbuf(f"ar_r{i}", (1, 128 if i < 2 else 512), F32)
                  for i in range(4)]
        self.reg = Reg(P)
        self._sbuf_consts = {}

    SCRATCH_MB = {"s_g1": 72, "s_g2": 72, "s_t1": 72}

    def scratch(self, name, shape, dtype):
        if not hasattr(self, "_scratch"):
            self._scratch = {}
        n = 1
        for s in shape:
            n *= s
        nbytes = n * mybir.dt.size(dtype)
        if name not in self._scratch:
            mb = self.SCRATCH_MB.get(name, 20)
            h = self.P.dram("zz_" + name, (mb * 1024 * 1024,),
                            mybir.dt.uint8)
            self._scratch[name] = h
        h = self._scratch[name]
        assert nbytes <= h.shape[0], (name, shape, dtype)
        ap = h.base.bitcast(dtype)[0:n]
        names = [f"b{i}" for i in range(len(shape))]
        pat = "(" + " ".join(names) + ") -> " + " ".join(names)
        kw = {nm: s for nm, s in zip(names, shape)}
        ap = ap.rearrange(pat, **kw)
        return Buf(ap, h.key, shape, dtype, "dram")

    def sconst(self, name, rows, cols, dtype=None):
        if name in self._sbuf_consts:
            return self._sbuf_consts[name]
        src = self.consts[name]
        dt = dtype or src.dtype
        b = self.P.sbuf("sc_" + name, (rows, cols), dt)
        q = "sp" if dt == src.dtype else "pool"
        self.P.dma(q, b.full(), src.full(), tag=f"const {name}")
        self._sbuf_consts[name] = b
        return b


def std_consts(T=2048):
    c = {}
    c["ident"] = np.eye(128, dtype=np.float32)
    c["ident_bf"] = np.eye(128, dtype=np.float32)
    iu = np.triu(np.ones((128, 128), np.float32), 0)
    c["mask_triu"] = iu
    c["mask_sl"] = np.tril(np.ones((128, 128), np.float32), -1)
    c["ones_row"] = np.ones((1, 512), np.float32)
    dec = np.exp(-np.arange(T, dtype=np.float64) / T).astype(np.float32)
    c["decay"] = np.ascontiguousarray(dec.reshape(T // 128, 128).T)
    inv_freq = (1.0 / 10000.0 ** (np.arange(0, 128, 2, dtype=np.float32)
                                  / 128.0))
    ang = np.arange(T, dtype=np.float32)[:, None] * inv_freq[None, :]
    c["ropec"] = np.cos(ang).astype(np.float32)
    c["ropes"] = np.sin(ang).astype(np.float32)
    return c




def kt4_shape(T, K):
    return (T // 128, 128, K // 128, 128)


def wslab_shape(K, N, nslab=512):
    return (N // nslab, 128, K // 128, nslab)


def host_wslab(w, nslab=512):
    K, N = w.shape
    return np.ascontiguousarray(
        w.reshape(K // 128, 128, N // nslab, nslab).transpose(2, 1, 0, 3))


RESERVE_KB = {
    "gwR": 61, "gwB": 16,
    "gxw": 32, "gxR0": 2, "gxR1": 2, "gxB": 24,
    "stg": 8,
    "lnx": 16,
    "tsa": 2, "tsb": 2, "tsc": 2, "tsd": 2,
    "rex0": 2, "rex1": 2, "reo0": 2, "reo1": 2,
    "st0": 2, "st1": 2,
    "msc0": 8, "msc1": 8,
}

GW_BASE = 11392


class Reg:

    def __init__(self, P):
        self.P = P
        self.raw = {}

    def _raw(self, name):
        if name not in self.raw:
            kb = RESERVE_KB[name]
            self.raw[name] = self.P.sbuf("rg_" + name, (128, kb * 256), F32)
        return self.raw[name]

    def get(self, name, shape, dtype=F32, off=0):
        b = self._raw(name)
        n = 1
        for s in shape[1:]:
            n *= s
        es = mybir.dt.size(dtype)
        cap = b.free_total * 4 // es
        assert off + n <= cap, (name, shape, dtype, off, cap)
        ap = b.base.bitcast(dtype)[0:shape[0], off:off + n]
        v = Buf(ap, b.key, (shape[0], n), dtype, "sbuf")
        if len(shape) > 2:
            v = v.reshape(shape)
        return _OffsetBuf(v, off * es)


class _OffsetBuf:

    def __init__(self, buf, byte_off):
        self._b = buf
        self._o = byte_off
        self.shape = buf.shape
        self.dtype = buf.dtype
        self.free_total = buf.free_total

    def __getitem__(self, idx):
        tv = self._b[idx]
        return TV(tv.ap, tv.key, tv.p0, tv.p1, tv.b0 + self._o,
                  tv.b1 + self._o)

    def full(self):
        return self[(slice(None),) * len(self.shape)]




def kt4_block(buf, tt, k0=0, nk=None):
    nkall = buf.shape[2]
    if nk is None:
        nk = nkall
    ap = buf.base[tt].rearrange("p c t -> p (c t)")
    if k0 or nk != nkall:
        ap = ap[:, k0 * 128:(k0 + nk) * 128]
    blk = buf.free_total
    es = buf.esize
    base_off = tt * 128 * nkall * 128 * es
    return TV(ap, buf.key, 0, 1, base_off + k0 * 128 * es,
              base_off + (nkall * 127 + nk) * 128 * es + 128 * es)


def wslab_block(buf, ns, c0=0, cn=None):
    _, _, kall, nslab = buf.shape
    if cn is None:
        cn = kall
    ap = buf.base[ns].rearrange("p c s -> p (c s)")
    if c0 or cn != kall:
        ap = ap[:, c0 * nslab:(c0 + cn) * nslab]
    es = buf.esize
    base = ns * 128 * kall * nslab * es
    return TV(ap, buf.key, 0, 1, base, base + 128 * kall * nslab * es)


def emit_gemm(ctx, x_src, w_src, out_cb, T, K, N, *, wprep, nslab=512,
              banks=(2, 3, 4, 5, 6, 7), wcg=2, extra_pairs=None,
              kgroup=None, tag="g"):
    P, reg = ctx.P, ctx.reg
    nt, nsl = T // 128, N // nslab
    kall = w_src.shape[2]
    k0, nk = kgroup if kgroup is not None else (0, kall)
    assert nk * 128 <= 2048 * 128 and nk <= 16
    G = len(banks)
    xbf = (x_src.dtype == BF16)
    ncg = (nk + wcg - 1) // wcg
    f32rW = wprep == "f32r_split"
    intW = wprep == "int_f32r"
    assert f32rW or intW or wprep in ("bf16", "raw")
    assert xbf == (wprep in ("bf16", "raw"))
    wq_ct, ws_ct, xs_ct = [0], [0], [0]
    WQ = ("sp", "pool")

    for g0 in range(0, nt, G):
        tts = list(range(g0, min(g0 + G, nt)))
        xs = []
        for ti, tt in enumerate(tts):
            if xbf:
                xb = reg.get("gxB", (128, nk, 128), BF16, off=ti * 2048)
            elif ti < 4:
                xb = reg.get("gxw", (128, nk, 128), F32, off=ti * 2048)
            else:
                xb = reg.get("gxB", (128, nk, 128), F32,
                             off=(ti - 4) * 2048)
            dst = xb.full()
            dst2 = TV(dst.ap.rearrange("p c t -> p (c t)"), dst.key,
                      dst.p0, dst.p1, dst.b0, dst.b1)
            P.dma("sp", dst2, kt4_block(x_src, tt, k0, nk),
                  tag=f"{tag} x{tt}")
            xs.append(xb)
        for ns in range(nsl):
            for ci in range(ncg):
                c0, cn = ci * wcg, min(wcg, nk - ci * wcg)
                sl = ws_ct[0] % 2
                ws_ct[0] += 1
                wsrc = wslab_block(w_src, ns, k0 + c0, cn)
                qn = WQ[wq_ct[0] % 2]
                wq_ct[0] += 1
                if wprep == "raw":
                    wb = reg.get("gwB", (128, cn, nslab), BF16,
                                 off=6144 + sl * wcg * nslab)
                    wtv = wb.full()
                    wtv2 = TV(wtv.ap.rearrange("p c s -> p (c s)"), wtv.key,
                              wtv.p0, wtv.p1, wtv.b0, wtv.b1)
                    P.dma(qn, wtv2, wsrc, tag=f"{tag} W{ns}.{ci}")
                    wparts = [wb]
                else:
                    sdt = I32 if intW else F32
                    st = reg.get("stg", (128, cn * nslab), sdt,
                                 off=sl * wcg * nslab)
                    P.dma(qn, st.full(), wsrc, tag=f"{tag} W{ns}.{ci}")
                    st3 = reg.get("stg", (128, cn, nslab), sdt,
                                  off=sl * wcg * nslab)
                    if f32rW:
                        w1 = reg.get("gwR", (128, cn, nslab), F32R,
                                     off=GW_BASE + sl * 2 * wcg * nslab)
                        w2 = reg.get("gwR", (128, cn, nslab), F32R,
                                     off=GW_BASE + sl * 2 * wcg * nslab
                                     + wcg * nslab)
                        P.copy("dve", w1.full(), st.full(),
                               tag=f"{tag} ws1")
                        P.tt("dve", w2.full(), st.full(), w1.full(),
                             ALU.subtract, tag=f"{tag} ws2")
                        wparts = [w1, w2]
                    elif intW:
                        wc = reg.get("gwR", (128, cn, nslab), F32R,
                                     off=GW_BASE + sl * wcg * nslab)
                        P.tscalar("dve", wc.full(), st.full(), 128.0, None,
                                  ALU.subtract, tag=f"{tag} wi")
                        wparts = [wc]
                    else:
                        wb = reg.get("gwB", (128, cn, nslab), BF16,
                                     off=6144 + sl * wcg * nslab)
                        P.copy("dve", wb.full(), st.full(),
                               tag=f"{tag} wc")
                        wparts = [wb]
                for ti, tt in enumerate(tts):
                    ps = ctx.pb[banks[ti]][:, 0:nslab]
                    pairs = []
                    if xbf:
                        for c in range(cn):
                            pairs.append((xs[ti][:, c0 + c, :],
                                          wparts[0][:, c, :]))
                    else:
                        xsl = xs_ct[0] % 2
                        par = xs_ct[0] % 2
                        xs_ct[0] += 1
                        xraw = xs[ti][:, c0:c0 + cn, :]
                        xraw = TV(xraw.ap.rearrange("p c t -> p (c t)"),
                                  xraw.key, xraw.p0, xraw.p1, xraw.b0,
                                  xraw.b1)
                        x1t = reg.get("gxR0", (128, cn, 128), F32R,
                                      off=xsl * wcg * 128)
                        P.copy("act", x1t.full(), xraw,
                               tag=f"{tag} xs1")
                        x2t = None
                        if f32rW or intW:
                            x2t = reg.get("gxR1", (128, cn, 128), F32R,
                                          off=xsl * wcg * 128)
                            P.tt("dve", x2t.full(), xraw,
                                 x1t.full(), ALU.subtract,
                                 tag=f"{tag} xs2")
                        for c in range(cn):
                            if f32rW:
                                pairs += [(x1t[:, c, :], wparts[0][:, c, :]),
                                          (x2t[:, c, :], wparts[0][:, c, :]),
                                          (x1t[:, c, :], wparts[1][:, c, :])]
                            elif intW:
                                pairs += [(x1t[:, c, :], wparts[0][:, c, :]),
                                          (x2t[:, c, :], wparts[0][:, c, :])]
                            else:
                                pairs.append((x1t[:, c, :],
                                              wparts[0][:, c, :]))
                    stop = (ci == ncg - 1)
                    if stop and extra_pairs is not None:
                        pairs = pairs + list(extra_pairs(tt, ns))
                    P.mm_group(pairs, ps, start=(ci == 0), stop=stop,
                               tag=f"{tag} mm{tt}.{ns}")
                    if stop:
                        out_cb(ps, tt, ns)


def _fm_outs(ctx, pv, tt, j0, jn, dst_split, dst_f32, dst_bf16,
             dst_bf163, tag):
    P = ctx.P
    if dst_split is not None:
        d1, d2 = dst_split
        o1 = ctx.reg.get("tsa", (128, jn * 128), F32R)
        o2 = ctx.reg.get("tsb", (128, jn * 128), F32R)
        P.copy("dve", o1.full(), pv, tag=f"{tag} s1")
        P.tt("dve", o2.full(), pv, o1.full(), ALU.subtract, tag=f"{tag} s2")
        P.dma("pool", kt4_block(d1, tt, j0, jn), o1.full(), tag=f"{tag} w1")
        P.dma("pool", kt4_block(d2, tt, j0, jn), o2.full(), tag=f"{tag} w2")
    prev = []
    if dst_split is not None:
        prev = [ctx.reg.get("tsb", (128, jn * 128), F32R).full()]
    if dst_f32 is not None:
        o3 = ctx.reg.get("tsc", (128, jn * 128), F32)
        P.copy("act", o3.full(), pv, tag=f"{tag} cf", extra=prev)
        prev = [o3.full()]
        P.dma("act", kt4_block(dst_f32, tt, j0, jn), o3.full(),
              tag=f"{tag} wf")
    if dst_bf16 is not None:
        o4 = ctx.reg.get("tsd", (128, jn * 128), BF16)
        P.copy("act", o4.full(), pv, tag=f"{tag} cb", extra=prev)
        P.dma("act", kt4_block(dst_bf16, tt, j0, jn), o4.full(),
              tag=f"{tag} wb")
    if dst_bf163 is not None:
        dh, dm, dl = dst_bf163
        hi = ctx.reg.get("tsa", (128, jn * 128), BF16)
        r = ctx.reg.get("tsb", (128, jn * 128), F32)
        mid = ctx.reg.get("tsc", (128, jn * 128), BF16)
        lo = ctx.reg.get("tsd", (128, jn * 128), BF16)
        P.copy("dve", hi.full(), pv, tag=f"{tag} b3h")
        P.tt("dve", r.full(), pv, hi.full(), ALU.subtract, tag=f"{tag} b3r")
        P.copy("dve", mid.full(), r.full(), tag=f"{tag} b3m")
        P.tt("dve", lo.full(), r.full(), mid.full(), ALU.subtract,
             tag=f"{tag} b3l")
        P.dma("pool", kt4_block(dh, tt, j0, jn), hi.full(), tag=f"{tag} w3h")
        P.dma("pool", kt4_block(dm, tt, j0, jn), mid.full(), tag=f"{tag} w3m")
        P.dma("pool", kt4_block(dl, tt, j0, jn), lo.full(), tag=f"{tag} w3l")


def emit_ln(ctx, src, T, D, dst_split=None, dst_f32=None, dst_bf16=None,
            dst_bf163=None, dst_tm=None, eps=1e-5, tag="ln"):
    P = ctx.P
    nt, nk = T // 128, D // 128
    ident = ctx.sconst("ident", 128, 128)
    want_fm = (dst_split is not None or dst_f32 is not None
               or dst_bf16 is not None or dst_bf163 is not None)
    for tt in range(nt):
        xb = ctx.reg.get("lnx", (128, D), F32, off=0)
        ln = ctx.reg.get("lnx", (128, D), F32, off=D)
        sq = ctx.reg.get("lnx", (128, D), F32, off=0)
        xt = xb.full()
        P.dma("sp", xt, src[tt * 128:(tt + 1) * 128, 0:D], tag=f"{tag} x{tt}")
        s = ctx.S
        mean, var, tmp, rstd, t2 = (s[0][:, 0:1], s[1][:, 0:1], s[2][:, 0:1],
                                    s[3][:, 0:1], s[4][:, 0:1])
        P.reduce("dve", mean, xt, ALU.add, tag=f"{tag} sum")
        P.tscalar("dve", mean, mean, 1.0 / D, None, ALU.mult,
                  tag=f"{tag} mean")
        P.tscalar("dve", ln.full(), xt, mean, None, ALU.subtract,
                  tag=f"{tag} ctr")
        P.act(sq.full(), ln.full(), AF.Square, tag=f"{tag} sq")
        P.reduce("dve", var, sq.full(), ALU.add, tag=f"{tag} var")
        P.tscalar("dve", var, var, 1.0 / D, eps, ALU.mult, ALU.add,
                  tag=f"{tag} var2")
        P.act(tmp, var, AF.Sqrt, tag=f"{tag} sqrt")
        P.recip(rstd, tmp, tag=f"{tag} rs")
        P.tt("dve", t2, rstd, rstd, ALU.mult, tag=f"{tag} nw1")
        P.tt("dve", t2, t2, var, ALU.mult, tag=f"{tag} nw2")
        P.tscalar("dve", t2, t2, -0.5, 1.5, ALU.mult, ALU.add,
                  tag=f"{tag} nw3")
        P.tt("dve", rstd, rstd, t2, ALU.mult, tag=f"{tag} nw4")
        P.tscalar("dve", ln.full(), ln.full(), rstd, None, ALU.mult,
                  tag=f"{tag} norm")
        if dst_tm is not None:
            P.dma("pool", dst_tm[tt * 128:(tt + 1) * 128, 0:D], ln.full(),
                  tag=f"{tag} tm{tt}")
        if want_fm:
            for j0 in range(0, nk, 4):
                jn = min(4, nk - j0)
                ps = ctx.pb[(j0 // 4) % 2]
                lt = ln.full()
                for j in range(j0, j0 + jn):
                    col = TV(lt.ap[:, j * 128:(j + 1) * 128], lt.key, lt.p0,
                             lt.p1, lt.b0 + j * 512, lt.b0 + (j + 1) * 512)
                    P.transpose(ps[:, (j - j0) * 128:(j - j0 + 1) * 128],
                                col, ident.full(), tag=f"{tag} T{tt}.{j}")
                _fm_outs(ctx, ps[:, 0:jn * 128], tt, j0, jn, dst_split,
                         dst_f32, dst_bf16, dst_bf163, tag)


def emit_transpose_split(ctx, src_tm, T, D, dst_split=None, dst_f32=None,
                         dst_bf16=None, dst_bf163=None, src_cb=None,
                         tag="ts"):
    P = ctx.P
    nt, nk = T // 128, D // 128
    ident = ctx.sconst("ident", 128, 128)
    for tt in range(nt):
        if src_cb is not None:
            xt = src_cb(tt)
        else:
            xb = ctx.reg.get("lnx", (128, D), F32, off=(tt % 2) * D)
            P.dma("sp", xb.full(), src_tm[tt * 128:(tt + 1) * 128, 0:D],
                  tag=f"{tag} r{tt}")
            xt = xb.full()
        for j0 in range(0, nk, 4):
            jn = min(4, nk - j0)
            ps = ctx.pb[(j0 // 4) % 2]
            for j in range(j0, j0 + jn):
                col = TV(xt.ap[:, j * 128:(j + 1) * 128], xt.key, xt.p0,
                         xt.p1, xt.b0 + j * 512, xt.b0 + (j + 1) * 512)
                P.transpose(ps[:, (j - j0) * 128:(j - j0 + 1) * 128], col,
                            ident.full(), tag=f"{tag} T{tt}.{j}")
            _fm_outs(ctx, ps[:, 0:jn * 128], tt, j0, jn, dst_split, dst_f32,
                     dst_bf16, dst_bf163, tag)


def res_add_cb(ctx, xres, out_dram, T, D, nslab=512, post=None, tag="res"):
    P = ctx.P

    def cb(ps, tt, ns):
        xt = ctx.reg.get(f"rex{tt % 2}", (128, nslab), F32)
        ot = ctx.reg.get(f"reo{tt % 2}", (128, nslab), F32)
        P.dma("act", xt.full(),
              xres[tt * 128:(tt + 1) * 128, ns * nslab:(ns + 1) * nslab],
              tag=f"{tag} rx")
        src = ps
        if post is not None:
            src = post(ps, tt, ns)
        P.tt("dve", ot.full(), src, xt.full(), ALU.add, tag=f"{tag} add")
        P.dma("pool", out_dram[tt * 128:(tt + 1) * 128,
                               ns * nslab:(ns + 1) * nslab], ot.full(),
              tag=f"{tag} wr")
    return cb



def emit_l0_attn(ctx, xres, out, w, T=2048, D=2048, H=16, nslab=512,
                 tag="l0a"):
    P, reg = ctx.P, ctx.reg
    nt, nk, nh = T // 128, D // 128, H
    assert D // H == 128
    sc = ctx.scratch

    xln = sc("s_x1", kt4_shape(T, D), F32)
    emit_ln(ctx, xres, T, D, dst_f32=xln, tag=tag + ".ln")

    qtm = sc("s_t1", (T, D), F32)
    ktm = sc("s_t2", (T, D), F32)
    v1tm = sc("s_t3", (T, D), F32R)
    v2tm = sc("s_t4", (T, D), F32R)

    def plain_cb(dst):
        def cb(ps, tt, ns):
            o = reg.get(f"reo{tt % 2}", (128, nslab), F32)
            P.copy("dve", o.full(), ps, tag=f"{tag} qkv evac")
            P.dma("pool", dst[tt * 128:(tt + 1) * 128,
                              ns * nslab:(ns + 1) * nslab], o.full(),
                  tag=f"{tag} qkv wr")
        return cb

    def vsplit_cb(ps, tt, ns):
        o1 = reg.get("reo0", (128, nslab), F32R)
        o2 = reg.get("reo1", (128, nslab), F32R)
        P.copy("dve", o1.full(), ps, tag=f"{tag} v s1")
        P.tt("dve", o2.full(), ps, o1.full(), ALU.subtract, tag=f"{tag} v s2")
        P.dma("pool", v1tm[tt * 128:(tt + 1) * 128,
                           ns * nslab:(ns + 1) * nslab], o1.full(),
              tag=f"{tag} v w1")
        P.dma("pool", v2tm[tt * 128:(tt + 1) * 128,
                           ns * nslab:(ns + 1) * nslab], o2.full(),
              tag=f"{tag} v w2")

    g3 = dict(wprep="f32r_split", nslab=nslab)
    emit_gemm(ctx, xln, w["wq"], plain_cb(qtm), T, D, D, tag=tag + ".q",
              **g3)
    emit_gemm(ctx, xln, w["wk"], plain_cb(ktm), T, D, D, tag=tag + ".k",
              **g3)
    emit_gemm(ctx, xln, w["wv"], vsplit_cb, T, D, D, tag=tag + ".v",
              **g3)

    q1 = sc("s_q1", kt4_shape(T, D), F32R)
    q2 = sc("s_q2", kt4_shape(T, D), F32R)
    k1 = sc("s_k1", kt4_shape(T, D), F32R)
    k2 = sc("s_k2", kt4_shape(T, D), F32R)
    emit_transpose_split(ctx, qtm, T, D, dst_split=(q1, q2), tag=tag + ".tq")
    emit_transpose_split(ctx, ktm, T, D, dst_split=(k1, k2), tag=tag + ".tk")

    ao = sc("s_a1", kt4_shape(T, D), F32)
    mtri = ctx.sconst("mask_triu", 128, 128)
    ones_row = ctx.sconst("ones_row", 1, 512)
    onesc = reg.get("gwR", (128, 1), F32R, off=11264)
    _o1f = reg.get("msc1", (128, 1), F32, off=2047)
    P.memset("pool", _o1f.full(), 1.0, tag=f"{tag} ones0")
    P.copy("dve", onesc.full(), _o1f.full(), tag=f"{tag} ones")
    inv_sqrt_T = 1.0 / float(np.sqrt(T))
    nts = T // 512

    for h in range(nh):
        kb1 = reg.get("gwR", (128, nt, 128), F32R, off=0)
        kb2 = reg.get("gwR", (128, nt, 128), F32R, off=nt * 128)
        vb1 = reg.get("gwR", (128, nt, 128), F32R, off=2 * nt * 128)
        vb2 = reg.get("gwR", (128, nt, 128), F32R, off=3 * nt * 128)
        P.dma("sp", kb1.full(),
              TV(k1.base[:, :, h, :].rearrange("t p w -> p t w"), k1.key,
                 0, 1, 0, k1.free_total * nt * 4), tag=f"{tag} kld1.{h}")
        P.dma("sp", kb2.full(),
              TV(k2.base[:, :, h, :].rearrange("t p w -> p t w"), k2.key,
                 0, 1, 0, k2.free_total * nt * 4), tag=f"{tag} kld2.{h}")
        P.dma("sp", vb1.full(),
              TV(v1tm.base.rearrange("(t p) (c w) -> c p t w", p=128, w=128)
                 [h], v1tm.key, 0, 1, 0, v1tm.free_total * T * 4),
              tag=f"{tag} vld1.{h}")
        P.dma("sp", vb2.full(),
              TV(v2tm.base.rearrange("(t p) (c w) -> c p t w", p=128, w=128)
                 [h], v2tm.key, 0, 1, 0, v2tm.free_total * T * 4),
              tag=f"{tag} vld2.{h}")

        vs_ps = ctx.pb[6][0:1, 0:128]
        pairs = []
        for tk in range(nt):
            pairs.append((onesc.full(), vb1[:, tk, :]))
            pairs.append((onesc.full(), vb2[:, tk, :]))
        P.mm_group(pairs, vs_ps, tag=f"{tag} vsum.{h}")
        vs_row = reg.get("msc1", (1, 128), F32, off=0)
        P.copy("dve", vs_row.full(), vs_ps, tag=f"{tag} vsumc.{h}")

        for ts in range(nts):
            qb1 = reg.get("gwR", (128, 4, 128), F32R, off=8192)
            qb2 = reg.get("gwR", (128, 4, 128), F32R, off=8704)
            P.dma("sp", qb1.full(),
                  TV(q1.base[4 * ts:4 * ts + 4, :, h, :]
                     .rearrange("t p w -> p t w"), q1.key, 0, 1, 0,
                     q1.free_total * nt * 4), tag=f"{tag} qld1.{h}.{ts}")
            P.dma("sp", qb2.full(),
                  TV(q2.base[4 * ts:4 * ts + 4, :, h, :]
                     .rearrange("t p w -> p t w"), q2.key, 0, 1, 0,
                     q2.free_total * nt * 4), tag=f"{tag} qld2.{h}.{ts}")
            q1s = TV(qb1.full().ap.rearrange("p t w -> p (t w)"), "rg_gwR",
                     0, 128, 8192 * 4, 8192 * 4 + 2048)
            q2s = TV(qb2.full().ap.rearrange("p t w -> p (t w)"), "rg_gwR",
                     0, 128, 8704 * 4, 8704 * 4 + 2048)
            nps = ctx.pb[2][:, 0:512]
            dps = ctx.pb[3][0:1, 0:512]
            ntk = 4 * ts + 4

            def emit_scores(tk):
                sps = ctx.pb[4 + tk % 3][:, 0:512]
                P.mm_group([(kb1[:, tk, :], q1s),
                            (kb2[:, tk, :], q1s),
                            (kb1[:, tk, :], q2s)], sps,
                           tag=f"{tag} sc.{h}.{ts}.{tk}")
                dcol = tk - 4 * ts
                if dcol >= 0:
                    dtv = TV(sps.ap[:, dcol * 128:(dcol + 1) * 128],
                             sps.key, 0, 128, sps.b0 + dcol * 512,
                             sps.b0 + (dcol + 1) * 512)
                    P.tt("dve", dtv, dtv, mtri.full(), ALU.mult,
                         tag=f"{tag} msk")
                    if dcol > 0:
                        P.memset("dve", TV(sps.ap[:, 0:dcol * 128], sps.key,
                                           0, 128, sps.b0,
                                           sps.b0 + dcol * 512), 0.0,
                                 tag=f"{tag} z")
                base = (tk % 2) * 1024
                eb = reg.get("lnx", (128, 512), F32, off=base)
                ef = reg.get("lnx", (128, 512), F32, off=base + 512)
                e2 = reg.get("gwR", (128, 512), F32R,
                             off=9216 + (tk % 2) * 1024)
                e1 = reg.get("gwR", (128, 512), F32R,
                             off=9216 + (tk % 2) * 1024 + 512)
                P.act(eb.full(), sps, AF.Exp, scale=inv_sqrt_T,
                      tag=f"{tag} exp")
                P.act(ef.full(), eb.full(), AF.Identity, bias=-1.0,
                      tag=f"{tag} em1")
                P.copy("dve", e2.full(), ef.full(), tag=f"{tag} es1")
                P.tt("pool", e1.full(), ef.full(), e2.full(), ALU.subtract,
                     tag=f"{tag} es2")
                return e1, e2

            epipe = [emit_scores(0)]
            for tk in range(ntk):
                if tk + 1 < ntk:
                    epipe.append(emit_scores(tk + 1))
                e1, e2 = epipe[tk]
                P.mm_group([(vb1[:, tk, :], e2.full()),
                            (vb2[:, tk, :], e2.full()),
                            (vb1[:, tk, :], e1.full())], nps,
                           start=(tk == 0), stop=False,
                           tag=f"{tag} av.{h}.{ts}.{tk}")
                P.mm_group([(onesc.full(), e2.full()),
                            (onesc.full(), e1.full())], dps,
                           start=(tk == 0), stop=(tk == ntk - 1),
                           tag=f"{tag} dn.{h}.{ts}.{tk}")
            P.mm_group([(vs_row.full(), ones_row.full())], nps,
                       start=False, stop=True, tag=f"{tag} vadd.{h}.{ts}")
            dn = reg.get("msc1", (1, 512), F32, off=256)
            P.tscalar("dve", dn.full(), dps, float(T), None, ALU.add,
                      tag=f"{tag} dT")
            P.recip(dn.full(), dn.full(), tag=f"{tag} rec")
            rb_ps = ctx.pb[7][:, 0:512]
            ones_r1 = reg.get("msc1", (1, 128), F32, off=1024)
            P.memset("pool", ones_r1.full(), 1.0, tag=f"{tag} o1")
            P.mm_group([(ones_r1.full(), dn.full())], rb_ps,
                       tag=f"{tag} rb.{h}.{ts}")
            rb = reg.get("lnx", (128, 512), F32, off=2048)
            P.copy("act", rb.full(), rb_ps, tag=f"{tag} rbc")
            att = reg.get("lnx", (128, 512), F32, off=2560)
            P.tt("dve", att.full(), nps, rb.full(), ALU.mult,
                 tag=f"{tag} norm")
            P.dma("pool",
                  TV(ao.base[4 * ts:4 * ts + 4, :, h, :]
                     .rearrange("t p w -> p t w"), ao.key, 0, 1, 0,
                     ao.free_total * nt * 4),
                  TV(att.full().ap.rearrange("p (t w) -> p t w", w=128),
                     att.full().key, 0, 128, att.full().b0, att.full().b1),
                  tag=f"{tag} aw1")

    emit_gemm(ctx, ao, w["wo"],
              res_add_cb(ctx, xres, out, T, D, nslab=nslab),
              T, D, D, wprep="f32r_split", nslab=nslab, tag=tag + ".o")



def build_l0a(T=2048, D=2048, H=16):
    nc = bass.Bass("TRN2", target_bir_lowering=False)
    P = Prog(nc)
    x_in = P.dram("x", (T, D), F32, kind="ExternalInput")
    y_out = P.dram("y", (T, D), F32, kind="ExternalOutput")
    ns = min(512, D)
    w = {k: P.dram(k, wslab_shape(D, D, ns), F32, kind="ExternalInput")
         for k in ("wq", "wk", "wv", "wo")}
    ctx = Ctx(P, std_consts(T))
    emit_l0_attn(ctx, x_in, y_out, w, T, D, H, nslab=ns)
    P.emit()
    return nc, ctx, P



def _bcast16(tv, groups):
    ap = tv.ap.unsqueeze(2).broadcast_to([tv.ap.shape[0], groups, 16])
    return TV(ap, tv.key, tv.p0, tv.p1, tv.b0, tv.b1)


def emit_e4m3fn_round(ctx, eng, out_tv, in_tv, tmp1, tmp2, tag):
    P = ctx.P
    P.tscalar(eng, out_tv, in_tv, 448.0, None, ALU.min, tag=tag + " clip")
    P.tscalar(eng, tmp1.i32(), out_tv.i32(), EXPMASK, None, ALU.bitwise_and,
              tag=tag + " eb")
    P.tscalar(eng, tmp1, tmp1, 0.125, float(2.0 ** -9), ALU.mult, ALU.max,
              tag=tag + " st")
    P.recip(tmp2, tmp1, tag=tag + " rc")
    P.tt(eng, tmp2, out_tv, tmp2, ALU.mult, tag=tag + " dv")
    P.tscalar(eng, tmp2, tmp2, MAGIC, MAGIC, ALU.add, ALU.subtract,
              tag=tag + " rn")
    P.tt(eng, out_tv, tmp2, tmp1, ALU.mult, tag=tag + " ml")


def emit_e2m1_round(ctx, out_tv, in_tv, t, tag):
    P = ctx.P
    a, rlo, rmid, c2 = t[0], t[1], t[2], t[3]
    sgn = out_tv
    P.tscalar("dve", a.i32(), in_tv.i32(), 0x7FFFFFFF, None,
              ALU.bitwise_and, tag=tag + " ab")
    P.tscalar("dve", sgn.i32(), in_tv.i32(), -2147483648, None,
              ALU.bitwise_and, tag=tag + " sg")
    P.tscalar("pool", a, a, 6.0, None, ALU.min, tag=tag + " cl")
    P.tscalar("dve", rlo, a, 2.0, MAGIC, ALU.mult, ALU.add, tag=tag + " l1")
    P.tscalar("dve", rlo, rlo, MAGIC, 0.5, ALU.subtract, ALU.mult,
              tag=tag + " l2")
    P.tscalar("pool", rmid, a, MAGIC, MAGIC, ALU.add, ALU.subtract,
              tag=tag + " m1")
    P.tscalar("dve", c2, a, 2.0, None, ALU.is_ge, tag=tag + " c2")
    P.tt("dve", rmid, rmid, rlo, ALU.subtract, tag=tag + " d1")
    P.tt("dve", rmid, rmid, c2, ALU.mult, tag=tag + " d2")
    P.tt("dve", rlo, rlo, rmid, ALU.add, tag=tag + " d3")
    P.tscalar("pool", rmid, a, 0.5, MAGIC, ALU.mult, ALU.add, tag=tag + " h1")
    P.tscalar("pool", rmid, rmid, MAGIC, 2.0, ALU.subtract, ALU.mult,
              tag=tag + " h2")
    P.tscalar("dve", c2, a, 4.0, None, ALU.is_ge, tag=tag + " c4")
    P.tt("dve", rmid, rmid, rlo, ALU.subtract, tag=tag + " e1")
    P.tt("dve", rmid, rmid, c2, ALU.mult, tag=tag + " e2")
    P.tt("dve", rlo, rlo, rmid, ALU.add, tag=tag + " e3")
    P.tt("dve", out_tv.i32(), rlo.i32(), sgn.i32(), ALU.bitwise_or,
         tag=tag + " or")


def emit_nvfp4_quant_chunk(ctx, dq_out_bf16, src_tv, K, recip6ts_col,
                           recip_ts_col, tag):
    P, reg = ctx.P, ctx.reg
    G = K // 16
    am = reg.get("msc0", (128, G), F32, off=0)
    t1 = reg.get("msc0", (128, G), F32, off=G)
    t2 = reg.get("msc0", (128, G), F32, off=2 * G)
    P.reduce("dve", am.full(), TV(src_tv.ap.rearrange("p (g k) -> p g k",
                                                      k=16), src_tv.key,
                                  src_tv.p0, src_tv.p1, src_tv.b0,
                                  src_tv.b1),
             ALU.max, absval=True, tag=tag + " am")
    P.tscalar("dve", am.full(), am.full(), recip6ts_col, None, ALU.mult,
              tag=tag + " bs")
    emit_e4m3fn_round(ctx, "dve", am.full(), am.full(), t1.full(), t2.full(),
                      tag + " e4")
    P.tscalar("dve", t1.full(), am.full(), 0.0, None, ALU.is_equal,
              tag=tag + " eq")
    P.tt("dve", t1.full(), t1.full(), am.full(), ALU.add, tag=tag + " de")
    P.recip(t1.full(), t1.full(), tag=tag + " rd")
    P.tscalar("dve", t1.full(), t1.full(), recip_ts_col, None, ALU.mult,
              tag=tag + " rt")
    y = reg.get("msc1", (128, K), F32, off=0)
    yv = TV(y.full().ap.rearrange("p (g k) -> p g k", k=16), y.full().key,
            0, 128, y.full().b0, y.full().b1)
    sv = TV(src_tv.ap.rearrange("p (g k) -> p g k", k=16), src_tv.key,
            src_tv.p0, src_tv.p1, src_tv.b0, src_tv.b1)
    P.tt("dve", yv, sv, _bcast16(t1.full(), G), ALU.mult, tag=tag + " sc")
    scr = [TV(ctx.reg.get("gxB", (128, K), F32, off=i * K).full().ap,
              "rg_gxB", 0, 128, i * K * 4, (i + 1) * K * 4)
           for i in range(3)]
    scr.append(TV(ctx.reg.get("lnx", (128, K), F32, off=0).full().ap,
                  "rg_lnx", 0, 128, 0, K * 4))
    emit_e2m1_round(ctx, y.full(), y.full(), scr, tag + " e2")
    ov = TV(dq_out_bf16.ap.rearrange("p (g k) -> p g k", k=16),
            dq_out_bf16.key, dq_out_bf16.p0, dq_out_bf16.p1,
            dq_out_bf16.b0, dq_out_bf16.b1)
    P.tt("dve", ov, yv, _bcast16(am.full(), G), ALU.mult, tag=tag + " dq")

def emit_l0_ffn(ctx, xres, out, wdram, bias_dram, xts_dram, T=2048, D=2048,
                nslab=512, tag="l0f"):
    P, reg = ctx.P, ctx.reg
    N = K = D
    nt, nk = T // 128, K // 128
    sc = ctx.scratch
    xln_tm = sc("s_t1", (T, D), F32)
    emit_ln(ctx, xres, T, D, dst_tm=xln_tm, tag=tag + ".ln")

    S = ctx.S
    wmax = S[5][:, 0:1]
    P.memset("dve", wmax, 0.0, tag=f"{tag} wm0")
    for c in range(N // 128):
        wt = reg.get("msc1", (128, K), F32, off=0)
        P.dma("sp", wt.full(), wdram[c * 128:(c + 1) * 128, :],
              tag=f"{tag} WA{c}")
        cm = S[6][:, 0:1]
        P.reduce("dve", cm, wt.full(), ALU.max, absval=True,
                 tag=f"{tag} wmr{c}")
        P.tt("dve", wmax, wmax, cm, ALU.max, tag=f"{tag} wmm{c}")
    ident = ctx.sconst("ident", 128, 128)
    pad = reg.get("lnx", (128, 128), F32, off=0)
    P.memset("dve", pad.full(), 0.0, tag=f"{tag} pz")
    P.copy("dve", pad[:, 0:1], wmax, tag=f"{tag} pc")
    tps = ctx.pb[0][:, 0:128]
    P.transpose(tps, pad.full(), ident.full(), tag=f"{tag} pt")
    row = ctx.R[0][0:1, 0:128]
    P.copy("dve", row, TV(tps.ap[0:1, :], tps.key, 0, 1, tps.b0,
                          tps.b0 + 512), tag=f"{tag} rc")
    wts = S[5][0:1, 1:2]
    P.reduce("dve", wts, row, ALU.max, tag=f"{tag} wts")
    P.tscalar("dve", wts, wts, 1.0 / 2688.0, None, ALU.mult,
              tag=f"{tag} ts")
    onesr = ctx.R[1][0:1, 0:128]
    P.memset("dve", onesr, 1.0, tag=f"{tag} o")
    s6 = S[5][0:1, 2:3]
    P.tscalar("dve", s6, wts, 6.0, None, ALU.mult, tag=f"{tag} s6")
    P.recip(s6, s6, tag=f"{tag} s6r")
    rts = S[5][0:1, 3:4]
    P.recip(rts, wts, tag=f"{tag} rts")
    P.dma("sp", S[6][0:1, 0:1], xts_dram[0:1, 0:1], tag=f"{tag} xts")
    x6 = S[6][0:1, 1:2]
    P.tscalar("dve", x6, S[6][0:1, 0:1], 6.0, None, ALU.mult,
              tag=f"{tag} x6")
    P.recip(x6, x6, tag=f"{tag} x6r")
    xrts = S[6][0:1, 2:3]
    P.recip(xrts, S[6][0:1, 0:1], tag=f"{tag} xrts")
    osc = S[6][0:1, 3:4]
    P.tt("dve", osc, wts, S[6][0:1, 0:1], ALU.mult, tag=f"{tag} osc")

    def bcast_col(dst_col, src11, bank, tg):
        ps = ctx.pb[bank][:, 0:1]
        P.mm_group([(onesr, src11)], ps, tag=tg)
        P.copy("dve", dst_col, ps, tag=tg + "c")
    rw6c = S[7][:, 0:1]
    rwtc = S[7][:, 1:2]
    rx6c = S[7][:, 2:3]
    rxtc = S[7][:, 3:4]
    oscc = S[7][:, 4:5]
    bcast_col(rw6c, s6, 1, f"{tag} b1")
    bcast_col(rwtc, rts, 1, f"{tag} b2")
    bcast_col(rx6c, x6, 1, f"{tag} b3")
    bcast_col(rxtc, xrts, 1, f"{tag} b4")
    bcast_col(oscc, osc, 1, f"{tag} b5")

    bias_tvs = []
    for i in range(N // nslab):
        if i < 2:
            tvb = ctx.R[2 + i][0:1, 0:nslab]
        else:
            tvb = reg.get("msc0", (1, nslab), F32,
                          off=512 * (i - 1)).full()
        P.dma("sp", tvb, bias_dram[i * nslab:(i + 1) * nslab],
              tag=f"{tag} bld{i}")
        bias_tvs.append(tvb)

    def bias_row_tv(ns):
        return bias_tvs[ns]

    identb = ctx.sconst("ident_bf", 128, 128, dtype=BF16)
    wdqT = sc("s_wq", wslab_shape(K, N, nslab), BF16)
    for c in range(N // 128):
        wt = reg.get("msc1", (128, K), F32, off=0)
        P.dma("sp", wt.full(), wdram[c * 128:(c + 1) * 128, :],
              tag=f"{tag} WB{c}")
        dq = reg.get("gwB", (128, K), BF16, off=0)
        emit_nvfp4_quant_chunk(ctx, dq.full(), wt.full(), K, rw6c, rwtc,
                               f"{tag} wq{c}")
        for j0 in range(0, nk, 4):
            jn = min(4, nk - j0)
            psb = ctx.pb[2 + (j0 // 4) % 2].cast(BF16)
            for j in range(j0, j0 + jn):
                P.transpose(psb[:, (j - j0) * 128:(j - j0 + 1) * 128],
                            TV(dq.full().ap[:, j * 128:(j + 1) * 128],
                               "rg_gwB", 0, 128, j * 256, (j + 1) * 256),
                            identb.full(), tag=f"{tag} wT{c}.{j}")
            ob = reg.get("gwB", (128, jn * 128), BF16, off=4096)
            P.copy("dve", ob.full(), psb[:, 0:jn * 128], tag=f"{tag} wE")
            dst = TV(wdqT.base[c // 4, :, j0:j0 + jn,
                               (c % 4) * 128:(c % 4 + 1) * 128],
                     wdqT.key, 0, 1, 0, wdqT.free_total * 8)
            src = TV(ob.full().ap.rearrange("p (c t) -> p c t", t=128),
                     ob.full().key, 0, 128, ob.full().b0, ob.full().b1)
            P.dma("pool", dst, src, tag=f"{tag} wW{c}.{j0}")

    xdqT = sc("s_xq", kt4_shape(T, K), BF16)
    for tt in range(nt):
        xt = reg.get("msc1", (128, D), F32, off=0)
        P.dma("sp", xt.full(), xln_tm[tt * 128:(tt + 1) * 128, :],
              tag=f"{tag} xr{tt}")
        dq = reg.get("gwB", (128, K), BF16, off=0)
        emit_nvfp4_quant_chunk(ctx, dq.full(), xt.full(), K, rx6c, rxtc,
                               f"{tag} xq{tt}")
        for j0 in range(0, nk, 4):
            jn = min(4, nk - j0)
            psb = ctx.pb[2 + (j0 // 4) % 2].cast(BF16)
            for j in range(j0, j0 + jn):
                P.transpose(psb[:, (j - j0) * 128:(j - j0 + 1) * 128],
                            TV(dq.full().ap[:, j * 128:(j + 1) * 128],
                               "rg_gwB", 0, 128, j * 256, (j + 1) * 256),
                            identb.full(), tag=f"{tag} xT{tt}.{j}")
            ob = reg.get("gwB", (128, jn * 128), BF16, off=4096)
            P.copy("dve", ob.full(), psb[:, 0:jn * 128], tag=f"{tag} xE")
            P.dma("pool", kt4_block(xdqT, tt, j0, jn), ob.full(),
                  tag=f"{tag} xW{tt}.{j0}")

    def post(ps, tt, ns):
        o = reg.get("tsa", (128, nslab), F32)
        bb = ctx.pb[(tt % 2)][:, 0:nslab]
        P.mm_group([(onesr, bias_row_tv(ns))], bb, tag=f"{tag} bb{tt}.{ns}")
        bs = reg.get("tsb", (128, nslab), F32)
        P.copy("act", bs.full(), bb, tag=f"{tag} bbc")
        P.stt("dve", o.full(), ps, oscc, bs.full(), ALU.mult, ALU.add,
              tag=f"{tag} ep")
        return o.full()

    emit_gemm(ctx, xdqT, wdqT,
              res_add_cb(ctx, xres, out, T, D, nslab=nslab, post=post),
              T, K, N, wprep="raw", nslab=nslab, tag=tag + ".mm")



def emit_l1_attn(ctx, xres, out, w, T=2048, D=2048, H=16, nslab=512,
                 tag="l1a"):
    P, reg = ctx.P, ctx.reg
    nt, nh = T // 128, H
    D_ = D
    eps = 1e-6
    sc = ctx.scratch
    xln = sc("s_x1", kt4_shape(T, D), F32)
    emit_ln(ctx, xres, T, D, dst_f32=xln, tag=tag + ".ln")

    dec = reg.get("msc0", (128, nt), F32, off=1024)
    P.dma("sp", dec.full(), ctx.consts["decay"].full(),
          tag=f"{tag} dec")
    onesr = ctx.R[1][0:1, 0:128]
    P.memset("dve", onesr, 1.0, tag=f"{tag} o")

    bias_tv = {}

    def load_bias(nm):
        tvs = []
        for i in range(D // nslab):
            rb = reg.get("msc1", (1, nslab), F32, off=i * nslab)
            P.dma("sp", rb.full(), w[nm][i * nslab:(i + 1) * nslab],
                  tag=f"{tag} b{nm}{i}")
            tvs.append(rb.full())
        bias_tv[nm] = tvs

    qtm = sc("s_t1", (T, D), F32)
    k1tm = sc("s_t3", (T, D), F32R)
    k2tm = sc("s_t4", (T, D), F32R)
    v1tm = sc("s_t5", (T, D), F32R)
    v2tm = sc("s_t6", (T, D), F32R)

    def bias_pairs(nm):
        def ep(tt, ns):
            return [(onesr, bias_tv[nm][ns])]
        return ep

    def q_cb(ps, tt, ns):
        o = reg.get(f"reo{tt % 2}", (128, nslab), F32)
        P.act(o.full(), ps, AF.Relu, tag=f"{tag} qrelu")
        P.tscalar("dve", o.full(), o.full(), dec[:, tt:tt + 1], None,
                  ALU.mult, tag=f"{tag} qdec")
        P.dma("pool", qtm[tt * 128:(tt + 1) * 128,
                          ns * nslab:(ns + 1) * nslab], o.full(),
              tag=f"{tag} qw")

    def kv_cb(d1tm, d2tm, relu):
        def cb(ps, tt, ns):
            o = reg.get("tsc", (128, nslab), F32)
            if relu:
                P.act(o.full(), ps, AF.Relu, tag=f"{tag} krelu")
                P.tscalar("dve", o.full(), o.full(), dec[:, tt:tt + 1],
                          None, ALU.mult, tag=f"{tag} kdec")
            else:
                P.copy("act", o.full(), ps, tag=f"{tag} vev")
            o1 = reg.get("reo0", (128, nslab), F32R)
            o2 = reg.get("reo1", (128, nslab), F32R)
            P.copy("dve", o1.full(), o.full(), tag=f"{tag} s1")
            P.tt("dve", o2.full(), o.full(), o1.full(), ALU.subtract,
                 tag=f"{tag} s2")
            P.dma("pool", d1tm[tt * 128:(tt + 1) * 128,
                               ns * nslab:(ns + 1) * nslab], o1.full(),
                  tag=f"{tag} w1")
            P.dma("pool", d2tm[tt * 128:(tt + 1) * 128,
                               ns * nslab:(ns + 1) * nslab], o2.full(),
                  tag=f"{tag} w2")
        return cb

    g3 = dict(wprep="f32r_split", nslab=nslab)
    load_bias("bq")
    emit_gemm(ctx, xln, w["wq"], q_cb, T, D, D, tag=tag + ".q",
              extra_pairs=bias_pairs("bq"), **g3)
    load_bias("bk")
    emit_gemm(ctx, xln, w["wk"], kv_cb(k1tm, k2tm, True), T, D, D,
              tag=tag + ".k", extra_pairs=bias_pairs("bk"), **g3)
    load_bias("bv")
    emit_gemm(ctx, xln, w["wv"], kv_cb(v1tm, v2tm, False), T, D, D,
              tag=tag + ".v", extra_pairs=bias_pairs("bv"), **g3)

    q1 = sc("s_q1", kt4_shape(T, D), F32R)
    q2 = sc("s_q2", kt4_shape(T, D), F32R)
    emit_transpose_split(ctx, qtm, T, D, dst_split=(q1, q2), tag=tag + ".tq")

    ao = sc("s_t2", (T, D), F32)
    onesc = reg.get("gwR", (128, 2), F32R, off=9100)
    _o1f = ctx.S[6][:, 8:10]
    P.memset("dve", _o1f, 1.0, tag=f"{tag} oc0")
    P.copy("dve", onesc.full(), _o1f, tag=f"{tag} oc")
    for h in range(nh):
        kb1 = reg.get("gwR", (128, nt, 128), F32R, off=0)
        kb2 = reg.get("gwR", (128, nt, 128), F32R, off=nt * 128)
        vb1 = reg.get("gwR", (128, nt, 128), F32R, off=2 * nt * 128)
        vb2 = reg.get("gwR", (128, nt, 128), F32R, off=3 * nt * 128)
        for (b, tm) in ((kb1, k1tm), (kb2, k2tm), (vb1, v1tm), (vb2, v2tm)):
            P.dma("sp", b.full(),
                  TV(tm.base.rearrange("(t p) (c w) -> c p t w", p=128,
                                       w=128)[h], tm.key, 0, 1, 0,
                     tm.free_total * T * 4), tag=f"{tag} ld{h}")
        kv_p = ctx.pb[2][:, 0:128]
        ks_p = ctx.pb[3][:, 0:2]
        pairs_kv = []
        pairs_ks = []
        for tk in range(nt):
            pairs_kv += [(kb1[:, tk, :], vb1[:, tk, :]),
                         (kb2[:, tk, :], vb1[:, tk, :]),
                         (kb1[:, tk, :], vb2[:, tk, :])]
            pairs_ks += [(kb1[:, tk, :], onesc.full()),
                         (kb2[:, tk, :], onesc.full())]
        P.mm_group(pairs_kv, kv_p, tag=f"{tag} kv.{h}")
        P.mm_group(pairs_ks, ks_p, tag=f"{tag} ks.{h}")
        r1 = reg.get("gwR", (128, 130), F32R, off=4 * nt * 128)
        r2 = reg.get("gwR", (128, 130), F32R, off=4 * nt * 128 + 130)
        t0 = reg.get("lnx", (128, 130), F32, off=0)
        P.memset("dve", t0[:, 128:130], 0.0, tag=f"{tag} kvz")
        P.copy("dve", t0[:, 0:128], kv_p, tag=f"{tag} kve")
        P.copy("dve", t0[:, 128:129],
               TV(ks_p.ap[:, 0:1], ks_p.key, 0, 128, ks_p.b0, ks_p.b0 + 4),
               tag=f"{tag} kse")
        P.copy("dve", r1.full(), t0.full(), tag=f"{tag} kvs1")
        P.tt("dve", r2.full(), t0.full(), r1.full(), ALU.subtract,
             tag=f"{tag} kvs2")
        for tt in range(nt):
            qb1 = reg.get("gwR", (128, 128), F32R,
                          off=4 * nt * 128 + 300 + (tt % 2) * 256)
            qb2 = reg.get("gwR", (128, 128), F32R,
                          off=4 * nt * 128 + 300 + (tt % 2) * 256 + 128)
            P.dma("sp", qb1.full(), kt4_block(q1, tt, h, 1),
                  tag=f"{tag} q1.{h}.{tt}")
            P.dma("sp", qb2.full(), kt4_block(q2, tt, h, 1),
                  tag=f"{tag} q2.{h}.{tt}")
            ops_ = ctx.pb[4 + tt % 2][:, 0:130]
            P.mm_group([(qb1.full(), r1.full()),
                        (qb2.full(), r1.full()),
                        (qb1.full(), r2.full())], ops_,
                       tag=f"{tag} out.{h}.{tt}")
            z = ctx.S[6][:, 9:10]
            P.tscalar("dve", z, TV(ops_.ap[:, 128:129], ops_.key, 0, 128,
                                   ops_.b0 + 512, ops_.b0 + 516),
                      eps, None, ALU.add, tag=f"{tag} z")
            P.recip(z, z, tag=f"{tag} zr")
            o = reg.get("lnx", (128, 128), F32, off=256 + (tt % 2) * 128)
            P.tscalar("dve", o.full(), TV(ops_.ap[:, 0:128], ops_.key, 0,
                                          128, ops_.b0, ops_.b0 + 512),
                      z, None, ALU.mult, tag=f"{tag} on")
            P.dma("pool", ao[tt * 128:(tt + 1) * 128,
                            h * 128:(h + 1) * 128], o.full(),
                  tag=f"{tag} ow.{h}.{tt}")

    aoT = sc("s_a1", kt4_shape(T, D), F32)
    emit_transpose_split(ctx, ao, T, D, dst_f32=aoT, tag=tag + ".ta")
    load_bias("bo")
    emit_gemm(ctx, aoT, w["wo"],
              res_add_cb(ctx, xres, out, T, D, nslab=nslab),
              T, D, D, wprep="f32r_split",
              nslab=nslab, extra_pairs=bias_pairs("bo"), tag=tag + ".o")



def emit_l1_ffn(ctx, xres, out, w, T=2048, D=2048, E=8, FFN=1024,
                nslab=512, tag="l1f"):
    P, reg = ctx.P, ctx.reg
    nt, nk = T // 128, D // 128
    EF = E * FFN
    sc = ctx.scratch
    xln = sc("s_x1", kt4_shape(T, D), F32)
    emit_ln(ctx, xres, T, D, dst_f32=xln, tag=tag + ".ln")

    onesr = ctx.R[1][0:1, 0:128]
    P.memset("dve", onesr, 1.0, tag=f"{tag} o")
    brow = ctx.R[2][0:1, 0:E]
    P.dma("sp", brow, w["bias"][0:E], tag=f"{tag} bias")
    bps = ctx.pb[0][:, 0:E]
    P.mm_group([(onesr, brow)], bps, tag=f"{tag} bb")
    bbc = ctx.S[6][:, 0:E]
    P.copy("dve", bbc, bps, tag=f"{tag} bbc")

    wn = reg.get("msc0", (128, nt * E), F32, off=1024)

    def r_cb(ps, tt, ns):
        s_ = ctx.S
        r = s_[5][:, 0:E]
        P.act(r, ps, AF.Sigmoid, tag=f"{tag} sg")
        sco = s_[5][:, 8:8 + E]
        P.tt("dve", sco, r, bbc, ALU.add, tag=f"{tag} sb")
        m1 = s_[6][:, 8:9]
        P.reduce("dve", m1, sco, ALU.max, tag=f"{tag} m1")
        eq = s_[7][:, 0:E]
        P.tscalar("dve", eq, sco, m1, None, ALU.is_ge, tag=f"{tag} eq")
        s2 = s_[7][:, 8:8 + E]
        P.stt("dve", s2, eq, -1e30, sco, ALU.mult, ALU.add, tag=f"{tag} s2")
        m2 = s_[6][:, 9:10]
        P.reduce("dve", m2, s2, ALU.max, tag=f"{tag} m2")
        mm_ = s_[7][:, 0:E]
        P.tscalar("dve", mm_, sco, m2, None, ALU.is_ge, tag=f"{tag} mtop")
        rm = s_[5][:, 8:8 + E]
        P.tt("dve", rm, r, mm_, ALU.mult, tag=f"{tag} rm")
        ds = s_[6][:, 8:9]
        P.reduce("dve", ds, rm, ALU.add, tag=f"{tag} ds")
        P.recip(ds, ds, tag=f"{tag} dr")
        P.tscalar("dve", wn[:, tt * E:(tt + 1) * E], rm, ds, None, ALU.mult,
                  tag=f"{tag} wn")

    emit_gemm(ctx, xln, w["wg"], r_cb, T, D, E,
              wprep="f32r_split", nslab=E, tag=tag + ".r")

    s1tm = sc("s_t1", (T, EF), F32)
    ident = ctx.sconst("ident", 128, 128)

    def h1_cb(ps, tt, ns):
        o = reg.get(f"reo{tt % 2}", (128, nslab), F32)
        P.act(o.full(), ps, AF.Silu, tag=f"{tag} silu")
        P.dma("pool", s1tm[tt * 128:(tt + 1) * 128,
                           ns * nslab:(ns + 1) * nslab], o.full(),
              tag=f"{tag} s1w")

    emit_gemm(ctx, xln, w["w1"], h1_cb, T, D, EF,
              wprep="f32r_split", nslab=nslab, tag=tag + ".h1")

    gkt = sc("s_g1", kt4_shape(T, EF), F32)
    nsl_per_e = FFN // nslab

    def h3_cb(ps, tt, ns):
        e = ns // nsl_per_e
        sl = reg.get(f"rex{tt % 2}", (128, nslab), F32)
        P.dma("act", sl.full(), s1tm[tt * 128:(tt + 1) * 128,
                                     ns * nslab:(ns + 1) * nslab],
              tag=f"{tag} s1r")
        g = reg.get("tsc", (128, nslab), F32)
        P.stt("dve", g.full(), ps, wn[:, tt * E + e:tt * E + e + 1],
              sl.full(), ALU.mult, ALU.mult, tag=f"{tag} g")
        psT = ctx.pb[(tt % 2)]
        for j in range(nslab // 128):
            P.transpose(psT[:, j * 128:(j + 1) * 128],
                        TV(g.full().ap[:, j * 128:(j + 1) * 128],
                           "rg_tsc", 0, 128, j * 512, (j + 1) * 512),
                        ident.full(), tag=f"{tag} gT")
        o1 = reg.get("tsa", (128, nslab), F32)
        pv = psT[:, 0:nslab]
        P.copy("dve", o1.full(), pv, tag=f"{tag} gs1")
        P.dma("pool", kt4_block(gkt, tt, ns * (nslab // 128), nslab // 128),
              o1.full(), tag=f"{tag} gw1")

    emit_gemm(ctx, xln, w["w3"], h3_cb, T, D, EF,
              wprep="f32r_split", nslab=nslab, tag=tag + ".h3")

    acc = sc("s_t2", (T, D), F32)
    nkg = EF // 2048
    for kg in range(nkg):
        first = (kg == 0)
        last = (kg == nkg - 1)

        if last:
            cb = res_add_acc_cb(ctx, acc, xres, out, T, D, nslab, tag)
        elif first:
            cb = store_cb(ctx, acc, T, D, nslab, tag)
        else:
            cb = accum_cb(ctx, acc, T, D, nslab, tag)
        emit_gemm(ctx, gkt, w["w2"], cb, T, 2048, D,
                  wprep="f32r_split", nslab=nslab, kgroup=(kg * 16, 16),
                  tag=tag + f".d{kg}")


def store_cb(ctx, dst, T, D, nslab, tag):
    P = ctx.P

    def cb(ps, tt, ns):
        o = ctx.reg.get(f"reo{tt % 2}", (128, nslab), F32)
        P.copy("dve", o.full(), ps, tag=f"{tag} st")
        P.dma("pool", dst[tt * 128:(tt + 1) * 128,
                          ns * nslab:(ns + 1) * nslab], o.full(),
              tag=f"{tag} stw")
    return cb


def accum_cb(ctx, acc, T, D, nslab, tag):
    P = ctx.P

    def cb(ps, tt, ns):
        a = ctx.reg.get(f"rex{tt % 2}", (128, nslab), F32)
        P.dma("act", a.full(), acc[tt * 128:(tt + 1) * 128,
                                   ns * nslab:(ns + 1) * nslab],
              tag=f"{tag} accr")
        o = ctx.reg.get(f"reo{tt % 2}", (128, nslab), F32)
        P.tt("dve", o.full(), ps, a.full(), ALU.add, tag=f"{tag} acca")
        P.dma("pool", acc[tt * 128:(tt + 1) * 128,
                          ns * nslab:(ns + 1) * nslab], o.full(),
              tag=f"{tag} accw")
    return cb


def res_add_acc_cb(ctx, acc, xres, out, T, D, nslab, tag):
    P = ctx.P

    def cb(ps, tt, ns):
        a = ctx.reg.get(f"rex{tt % 2}", (128, nslab), F32)
        P.dma("act", a.full(), acc[tt * 128:(tt + 1) * 128,
                                   ns * nslab:(ns + 1) * nslab],
              tag=f"{tag} fr1")
        xv = ctx.reg.get("tsd", (128, nslab), F32)
        P.dma("act", xv.full(), xres[tt * 128:(tt + 1) * 128,
                                     ns * nslab:(ns + 1) * nslab],
              tag=f"{tag} fr2")
        o = ctx.reg.get(f"reo{tt % 2}", (128, nslab), F32)
        P.tt("dve", o.full(), ps, a.full(), ALU.add, tag=f"{tag} fa1")
        P.tt("dve", o.full(), o.full(), xv.full(), ALU.add, tag=f"{tag} fa2")
        P.dma("pool", out[tt * 128:(tt + 1) * 128,
                          ns * nslab:(ns + 1) * nslab], o.full(),
              tag=f"{tag} fw")
    return cb



def emit_rope_tile(ctx, o_buf, nheads, cosb, sinb, tag):
    P = ctx.P
    W = nheads * 64

    def half(i):
        ap = o_buf.full().ap.rearrange("p (h two d) -> p h two d", two=2,
                                       d=64)[:, :, i, :]
        return TV(ap, o_buf.full().key, 0, 128, o_buf.full().b0,
                  o_buf.full().b1)

    def cbc(ctv):
        ap = ctv.ap.unsqueeze(1).broadcast_to([128, nheads, 64])
        return TV(ap, ctv.key, ctv.p0, ctv.p1, ctv.b0, ctv.b1)

    q1, q2 = half(0), half(1)
    a = ctx.reg.get("tsc", (128, W), F32)
    d = ctx.reg.get("tsd", (128, W), F32)

    def v3(buf):
        ap = buf.full().ap.rearrange("p (h d) -> p h d", d=64)
        return TV(ap, buf.full().key, 0, 128, buf.full().b0, buf.full().b1)

    P.tt("dve", v3(a), q1, cbc(cosb), ALU.mult, tag=tag + " r1")
    P.tt("pool", v3(d), q1, cbc(sinb), ALU.mult, tag=tag + " r2")
    t2 = ctx.reg.get("rex0", (128, W), F32)
    P.tt("dve", v3(t2), q2, cbc(sinb), ALU.mult, tag=tag + " r3")
    P.tt("dve", v3(a), v3(a), v3(t2), ALU.subtract, tag=tag + " r4")
    P.tt("pool", v3(t2), q2, cbc(cosb), ALU.mult, tag=tag + " r5")
    P.tt("pool", v3(d), v3(t2), v3(d), ALU.add, tag=tag + " r6")
    P.copy("dve", q1, v3(a), tag=tag + " r7")
    P.copy("pool", q2, v3(d), tag=tag + " r8")


def emit_l2_attn(ctx, xres, out, w, T=2048, D=2048, H=16, KV=4, WIN=512,
                 nslab=512, tag="l2a"):
    P, reg = ctx.P, ctx.reg
    nt, nh = T // 128, H
    dh = 128
    Dkv = KV * dh
    nrep = H // KV
    sc = ctx.scratch
    xln = sc("s_x1", kt4_shape(T, D), F32)
    emit_ln(ctx, xres, T, D, dst_f32=xln, tag=tag + ".ln")

    qtm = sc("s_t1", (T, D), F32)
    ktm = sc("s_t2", (T, Dkv), F32)
    v1tm = sc("s_t3", (T, Dkv), F32R)
    v2tm = sc("s_t4", (T, Dkv), F32R)

    def rope_cb(dst, nhd, nsl):
        def cb(ps, tt, ns):
            o = reg.get(f"reo{tt % 2}", (128, nsl), F32)
            P.copy("dve", o.full(), ps, tag=f"{tag} ev")
            cosb = reg.get("msc0", (128, 64), F32, off=0)
            sinb = reg.get("msc0", (128, 64), F32, off=64)
            P.dma("sp", cosb.full(),
                  ctx.consts["ropec"][tt * 128:(tt + 1) * 128, :],
                  tag=f"{tag} cs")
            P.dma("sp", sinb.full(),
                  ctx.consts["ropes"][tt * 128:(tt + 1) * 128, :],
                  tag=f"{tag} sn")
            emit_rope_tile(ctx, o, nsl // 128, cosb.full(),
                           sinb.full(), f"{tag} rp{tt}.{ns}")
            P.dma("pool", dst[tt * 128:(tt + 1) * 128,
                              ns * nsl:(ns + 1) * nsl], o.full(),
                  tag=f"{tag} qw")
        return cb

    emit_gemm(ctx, xln, w["wq"], rope_cb(qtm, H, nslab), T, D, D,
              tag=tag + ".q", wprep="f32r_split", nslab=nslab)
    emit_gemm(ctx, xln, w["wk"], rope_cb(ktm, KV, min(nslab, Dkv)),
              T, D, Dkv, tag=tag + ".k", nslab=min(nslab, Dkv),
              wprep="f32r_split")

    def vsplit_cb(ps, tt, ns):
        o1 = reg.get("reo0", (128, min(nslab, Dkv)), F32R)
        o2 = reg.get("reo1", (128, min(nslab, Dkv)), F32R)
        P.copy("dve", o1.full(), ps, tag=f"{tag} v s1")
        P.tt("dve", o2.full(), ps, o1.full(), ALU.subtract,
             tag=f"{tag} v s2")
        for d_, o_ in ((v1tm, o1), (v2tm, o2)):
            P.dma("pool", d_[tt * 128:(tt + 1) * 128,
                            ns * min(nslab, Dkv):(ns + 1) * min(nslab, Dkv)],
                  o_.full(), tag=f"{tag} vw")

    emit_gemm(ctx, xln, w["wv"], vsplit_cb, T, D, Dkv,
              nslab=min(nslab, Dkv), wprep="f32r_split", tag=tag + ".v")

    q1 = sc("s_q1", kt4_shape(T, D), F32R)
    q2 = sc("s_q2", kt4_shape(T, D), F32R)
    k1 = sc("s_k1", kt4_shape(T, Dkv), F32R)
    k2 = sc("s_k2", kt4_shape(T, Dkv), F32R)
    emit_transpose_split(ctx, qtm, T, D, dst_split=(q1, q2), tag=tag + ".tq")
    emit_transpose_split(ctx, ktm, T, Dkv, dst_split=(k1, k2),
                         tag=tag + ".tk")

    ao = sc("s_a1", kt4_shape(T, D), F32)
    mtri = ctx.sconst("mask_triu", 128, 128)
    msl = ctx.sconst("mask_sl", 128, 128)
    ones_row = ctx.sconst("ones_row", 1, 512)
    onesc = reg.get("gwR", (128, 2), F32R, off=11264)
    _o1f = ctx.S[6][:, 8:10]
    P.memset("dve", _o1f, 1.0, tag=f"{tag} oc0")
    P.copy("dve", onesc.full(), _o1f, tag=f"{tag} oc")
    scale = 1.0 / float(np.sqrt(dh))
    nts = T // 512
    wtiles = WIN // 128

    for h in range(nh):
        g = h // nrep
        kb1 = reg.get("gwR", (128, nt, 128), F32R, off=0)
        kb2 = reg.get("gwR", (128, nt, 128), F32R, off=nt * 128)
        vb1 = reg.get("gwR", (128, nt, 128), F32R, off=2 * nt * 128)
        vb2 = reg.get("gwR", (128, nt, 128), F32R, off=3 * nt * 128)
        if h % nrep == 0:
            for (b, src, iskt4) in ((kb1, k1, True), (kb2, k2, True),
                                    (vb1, v1tm, False), (vb2, v2tm, False)):
                if iskt4:
                    P.dma("sp", b.full(),
                          TV(src.base[:, :, g, :].rearrange(
                              "t p w -> p t w"), src.key, 0, 1, 0,
                             src.free_total * nt * 4),
                          tag=f"{tag} kld.{h}")
                else:
                    P.dma("sp", b.full(),
                          TV(src.base.rearrange(
                              "(t p) (c w) -> c p t w", p=128, w=128)[g],
                             src.key, 0, 1, 0, src.free_total * T * 4),
                          tag=f"{tag} vld.{h}")
        for ts in range(nts):
            qb1 = reg.get("gwR", (128, 4, 128), F32R, off=8192)
            qb2 = reg.get("gwR", (128, 4, 128), F32R, off=8704)
            P.dma("sp", qb1.full(),
                  TV(q1.base[4 * ts:4 * ts + 4, :, h, :]
                     .rearrange("t p w -> p t w"), q1.key, 0, 1, 0,
                     q1.free_total * nt * 4), tag=f"{tag} qld1.{h}.{ts}")
            P.dma("sp", qb2.full(),
                  TV(q2.base[4 * ts:4 * ts + 4, :, h, :]
                     .rearrange("t p w -> p t w"), q2.key, 0, 1, 0,
                     q2.free_total * nt * 4), tag=f"{tag} qld2.{h}.{ts}")
            q1s = TV(qb1.full().ap.rearrange("p t w -> p (t w)"), "rg_gwR",
                     0, 128, 8192 * 4, 8192 * 4 + 2048)
            q2s = TV(qb2.full().ap.rearrange("p t w -> p (t w)"), "rg_gwR",
                     0, 128, 8704 * 4, 8704 * 4 + 2048)
            nps = ctx.pb[2][:, 0:512]
            dps = ctx.pb[3][0:2, 0:512]
            tks = [tk for tk in range(max(0, 4 * ts - wtiles),
                                      4 * ts + 4)]

            def emit_sc(tk):
                sps = ctx.pb[4 + tk % 3][:, 0:512]
                P.mm_group([(kb1[:, tk, :], q1s),
                            (kb2[:, tk, :], q1s),
                            (kb1[:, tk, :], q2s)], sps,
                           tag=f"{tag} sc.{h}.{ts}.{tk}")
                base = (tk % 2) * 1024
                eb = reg.get("lnx", (128, 512), F32, off=base)
                e2 = reg.get("gwR", (128, 512), F32R,
                             off=9216 + (tk % 2) * 1024)
                e1 = reg.get("gwR", (128, 512), F32R,
                             off=9216 + (tk % 2) * 1024 + 512)
                P.act(eb.full(), sps, AF.Exp, scale=scale,
                      tag=f"{tag} exp")
                for qq in range(4):
                    tq = 4 * ts + qq
                    dcol = tq - tk
                    sl = TV(eb.full().ap[:, qq * 128:(qq + 1) * 128],
                            "rg_lnx", 0, 128, eb.full().b0 + qq * 512,
                            eb.full().b0 + (qq + 1) * 512)
                    if dcol == 0:
                        P.tt("dve", sl, sl, mtri.full(), ALU.mult,
                             tag=f"{tag} mk")
                    elif dcol < 0:
                        P.memset("dve", sl, 0.0, tag=f"{tag} z")
                    elif dcol == wtiles:
                        P.tt("dve", sl, sl, msl.full(), ALU.mult,
                             tag=f"{tag} mw")
                    elif dcol > wtiles:
                        P.memset("dve", sl, 0.0, tag=f"{tag} z2")
                P.copy("dve", e2.full(), eb.full(), tag=f"{tag} es1")
                P.tt("pool", e1.full(), eb.full(), e2.full(), ALU.subtract,
                     tag=f"{tag} es2")
                return e1, e2

            epipe = {tks[0]: emit_sc(tks[0])}
            for i, tk in enumerate(tks):
                if i + 1 < len(tks):
                    epipe[tks[i + 1]] = emit_sc(tks[i + 1])
                e1, e2 = epipe[tk]
                P.mm_group([(vb1[:, tk, :], e2.full()),
                            (vb2[:, tk, :], e2.full()),
                            (vb1[:, tk, :], e1.full())], nps,
                           start=(i == 0), stop=(i == len(tks) - 1),
                           tag=f"{tag} av.{h}.{ts}.{tk}")
                P.mm_group([(onesc.full(), e2.full()),
                            (onesc.full(), e1.full())], dps,
                           start=(i == 0), stop=(i == len(tks) - 1),
                           tag=f"{tag} dn.{h}.{ts}.{tk}")
            dn = reg.get("msc1", (1, 512), F32, off=256)
            P.copy("dve", dn.full(), TV(dps.ap[0:1, :], dps.key, 0, 1,
                                        dps.b0, dps.b0 + 2048),
                   tag=f"{tag} dnc")
            P.recip(dn.full(), dn.full(), tag=f"{tag} rec")
            rb_ps = ctx.pb[7][:, 0:512]
            ones_r1 = reg.get("msc1", (1, 128), F32, off=1024)
            P.memset("pool", ones_r1.full(), 1.0, tag=f"{tag} o1")
            P.mm_group([(ones_r1.full(), dn.full())], rb_ps,
                       tag=f"{tag} rb.{h}.{ts}")
            rb = reg.get("lnx", (128, 512), F32, off=2048)
            P.copy("act", rb.full(), rb_ps, tag=f"{tag} rbc")
            att = reg.get("lnx", (128, 512), F32, off=2560)
            P.tt("dve", att.full(), nps, rb.full(), ALU.mult,
                 tag=f"{tag} norm")
            P.dma("pool",
                  TV(ao.base[4 * ts:4 * ts + 4, :, h, :]
                     .rearrange("t p w -> p t w"), ao.key, 0, 1, 0,
                     ao.free_total * nt * 4),
                  TV(att.full().ap.rearrange("p (t w) -> p t w", w=128),
                     att.full().key, 0, 128, att.full().b0,
                     att.full().b1),
                  tag=f"{tag} aw")

    emit_gemm(ctx, ao, w["wo"],
              res_add_cb(ctx, xres, out, T, D, nslab=nslab),
              T, D, D, wprep="f32r_split",
              nslab=nslab, tag=tag + ".o")



def emit_l2_ffn(ctx, xres, out, w, T=2048, D=2048, I=8192, nslab=512,
                tag="l2f"):
    P, reg = ctx.P, ctx.reg
    nt = T // 128
    sc = ctx.scratch
    xln = sc("s_x1", kt4_shape(T, D), F32)
    emit_ln(ctx, xres, T, D, dst_f32=xln, tag=tag + ".ln")

    onesr = ctx.R[1][0:1, 0:128]
    P.memset("dve", onesr, 1.0, tag=f"{tag} o")
    ident = ctx.sconst("ident", 128, 128)
    sgtm = sc("s_t1", (T, I), F32)
    hkt = sc("s_g1", kt4_shape(T, I), F32)

    def bcast_scale(slot, src_dram, ns, nm):
        bt = reg.get("msc0", (128, nslab), F32, off=slot * 512)
        stv = src_dram[ns * nslab:(ns + 1) * nslab]
        bsrc = TV(stv.ap.partition_broadcast(128), stv.key, stv.p0,
                  stv.p1, stv.b0, stv.b1)
        P.dma("sp", bt.full(), bsrc, tag=f"{tag} {nm}r{ns}")
        return bt

    gb_ref = {}

    def gate_cb(ps, tt, ns):
        if tt % 6 == 0:
            gb_ref[ns] = bcast_scale(0, w["gs"], ns, "gs")
        gb = gb_ref[ns]
        o = reg.get(f"reo{tt % 2}", (128, nslab), F32)
        P.tt("dve", o.full(), ps, gb.full(), ALU.mult, tag=f"{tag} gsc")
        P.act(o.full(), o.full(), AF.Silu, tag=f"{tag} silu")
        P.dma("pool", sgtm[tt * 128:(tt + 1) * 128,
                           ns * nslab:(ns + 1) * nslab], o.full(),
              tag=f"{tag} sgw")

    emit_gemm(ctx, xln, w["wg"], gate_cb, T, D, I,
              wprep="int_f32r", nslab=nslab, tag=tag + ".g")

    ub_ref = {}

    def up_cb(ps, tt, ns):
        if tt % 6 == 0:
            ub_ref[ns] = bcast_scale(1, w["us"], ns, "us")
        ub = ub_ref[ns]
        t_ = reg.get(f"reo{tt % 2}", (128, nslab), F32)
        P.tt("dve", t_.full(), ps, ub.full(), ALU.mult, tag=f"{tag} usc")
        sl = reg.get(f"rex{tt % 2}", (128, nslab), F32)
        P.dma("act", sl.full(), sgtm[tt * 128:(tt + 1) * 128,
                                     ns * nslab:(ns + 1) * nslab],
              tag=f"{tag} sgr")
        P.tt("dve", t_.full(), t_.full(), sl.full(), ALU.mult,
             tag=f"{tag} h")
        psT = ctx.pb[(tt % 2)]
        for j in range(nslab // 128):
            P.transpose(psT[:, j * 128:(j + 1) * 128],
                        TV(t_.full().ap[:, j * 128:(j + 1) * 128],
                           t_.full().key, 0, 128,
                           t_.full().b0 + j * 512,
                           t_.full().b0 + (j + 1) * 512),
                        ident.full(), tag=f"{tag} hT")
        o1 = reg.get("tsa", (128, nslab), F32)
        pv = psT[:, 0:nslab]
        P.copy("dve", o1.full(), pv, tag=f"{tag} hs1")
        P.dma("pool", kt4_block(hkt, tt, ns * (nslab // 128), nslab // 128),
              o1.full(), tag=f"{tag} hw1")

    emit_gemm(ctx, xln, w["wu"], up_cb, T, D, I,
              wprep="int_f32r", nslab=nslab, tag=tag + ".u")

    acc = sc("s_t2", (T, D), F32)
    kgc = min(16, I // 128)
    nkg = (I // 128) // kgc
    db_ref = {}

    def final_cb(ps, tt, ns):
        if tt % 6 == 0:
            db_ref[ns] = bcast_scale(2, w["ds"], ns, "ds")
        db = db_ref[ns]
        o = reg.get(f"reo{tt % 2}", (128, nslab), F32)
        if nkg > 1:
            a = reg.get(f"rex{tt % 2}", (128, nslab), F32)
            P.dma("act", a.full(), acc[tt * 128:(tt + 1) * 128,
                                       ns * nslab:(ns + 1) * nslab],
                  tag=f"{tag} far")
            P.tt("dve", o.full(), ps, a.full(), ALU.add, tag=f"{tag} fa")
            P.tt("dve", o.full(), o.full(), db.full(), ALU.mult,
                 tag=f"{tag} fs")
        else:
            P.tt("dve", o.full(), ps, db.full(), ALU.mult, tag=f"{tag} fs")
        xv = reg.get("tsd", (128, nslab), F32)
        P.dma("act", xv.full(), xres[tt * 128:(tt + 1) * 128,
                                     ns * nslab:(ns + 1) * nslab],
              tag=f"{tag} fx")
        P.tt("dve", o.full(), o.full(), xv.full(), ALU.add,
             tag=f"{tag} fr")
        P.dma("pool", out[tt * 128:(tt + 1) * 128,
                          ns * nslab:(ns + 1) * nslab], o.full(),
              tag=f"{tag} fw")

    for kg in range(nkg):
        last = (kg == nkg - 1)
        if last:
            cb = final_cb
        elif kg == 0:
            cb = store_cb(ctx, acc, T, D, nslab, tag)
        else:
            cb = accum_cb(ctx, acc, T, D, nslab, tag)
        emit_gemm(ctx, hkt, w["wd"], cb, T, kgc * 128, D,
                  wprep="int_f32r", nslab=nslab,
                  kgroup=(kg * kgc, kgc), tag=tag + f".d{kg}")



def emit_l3_attn(ctx, xres, out, w, T=2048, D=2048, NST=16, RNK=128,
                 nslab=512, tag="l3a"):
    P, reg = ctx.P, ctx.reg
    nt, nk = T // 128, D // 128
    NE = RNK + 2 * NST
    sc = ctx.scratch
    ident = ctx.sconst("ident", 128, 128)
    u_fm = sc("s_q1", kt4_shape(T, D), F32)
    emit_ln(ctx, xres, T, D, dst_f32=u_fm, tag=tag + ".ln")

    sdbl = sc("s_t1", (T, NE), F32)

    def dbl_cb(ps, tt, ns):
        o = reg.get(f"reo{tt % 2}", (128, NE), F32)
        P.copy("dve", o.full(), ps, tag=f"{tag} dble")
        P.dma("pool", sdbl[tt * 128:(tt + 1) * 128, :], o.full(),
              tag=f"{tag} dblw")

    emit_gemm(ctx, u_fm, w["xproj"], dbl_cb, T, D, NE,
              wprep="f32r_split", nslab=NE, tag=tag + ".dbl")

    drk = sc("s_t3", kt4_shape(T, RNK), F32)
    bct_d = sc("s_bct", (2 * NST, T), F32)
    for tt in range(nt):
        xt = reg.get("lnx", (128, NE), F32, off=(tt % 2) * 256)
        P.dma("sp", xt.full(), sdbl[tt * 128:(tt + 1) * 128, :],
              tag=f"{tag} dblr{tt}")
        ps = ctx.pb[tt % 2]
        P.transpose(ps[:, 0:128],
                    TV(xt.full().ap[:, 0:128], xt.full().key, 0, 128,
                       xt.full().b0, xt.full().b0 + 512),
                    ident.full(), tag=f"{tag} drT{tt}")
        o1 = reg.get("tsa", (128, 128), F32)
        pv = ps[:, 0:128]
        P.copy("dve", o1.full(), pv, tag=f"{tag} drs1")
        P.dma("pool", kt4_block(drk, tt, 0, 1), o1.full(),
              tag=f"{tag} drw1")
        ps2 = ctx.pb[2 + tt % 2]
        P.transpose(ps2[0:2 * NST, 0:128],
                    TV(xt.full().ap[:, RNK:NE], xt.full().key, 0, 128,
                       xt.full().b0 + RNK * 4, xt.full().b0 + NE * 4),
                    ident.full(), tag=f"{tag} bcT{tt}")
        ob = reg.get("tsc", (2 * NST, 128), F32)
        P.copy("dve", ob.full(), ps2[0:2 * NST, 0:128], tag=f"{tag} bce")
        P.dma("pool", bct_d[0:2 * NST, tt * 128:(tt + 1) * 128], ob.full(),
              tag=f"{tag} bcw{tt}")

    delta_tm = sc("s_t2", (T, D), F32)
    onesr = ctx.R[1][0:1, 0:128]
    P.memset("dve", onesr, 1.0, tag=f"{tag} o")
    btvs = []
    for i in range(D // nslab):
        rb = reg.get("msc1", (1, nslab), F32, off=i * nslab)
        P.dma("sp", rb.full(), w["dtb"][i * nslab:(i + 1) * nslab],
              tag=f"{tag} dtb{i}")
        btvs.append(rb.full())

    def d_cb(ps, tt, ns):
        t1 = reg.get("tsa", (128, nslab), F32)
        t2 = reg.get("tsb", (128, nslab), F32)
        o = reg.get(f"reo{tt % 2}", (128, nslab), F32)
        P.tscalar("dve", t2.full(), ps, 0.0, None, ALU.max,
                  tag=f"{tag} sp4")
        P.act(t1.full(), ps, AF.Abs, tag=f"{tag} sp1",
              extra=[t2.full()])
        P.act(t1.full(), t1.full(), AF.Exp, scale=-1.0, tag=f"{tag} sp2")
        P.act(t1.full(), t1.full(), AF.Ln, bias=1.0, tag=f"{tag} sp3")
        P.tt("dve", o.full(), t1.full(), t2.full(), ALU.add,
             tag=f"{tag} sp5")
        P.dma("pool", delta_tm[tt * 128:(tt + 1) * 128,
                               ns * nslab:(ns + 1) * nslab], o.full(),
              tag=f"{tag} dw")

    emit_gemm(ctx, drk, w["dtw"], d_cb, T, RNK, D,
              wprep="f32r_split", nslab=nslab,
              extra_pairs=lambda tt, ns: [(onesr, btvs[ns])],
              tag=tag + ".dt")

    df = sc("s_t5", kt4_shape(T, D), F32)
    emit_transpose_split(ctx, delta_tm, T, D, dst_f32=df, tag=tag + ".tdf")

    ao_tm = sc("s_t6", (T, D), F32)
    dcol = reg.get("st0", (128, nk, NST), F32, off=0)
    dvec = reg.get("st0", (128, nk), F32, off=nk * NST)
    hstate = reg.get("st1", (128, 2, NST), F32, off=0)
    P.dma("sp", dvec.full(), w["dvec"].full(), tag=f"{tag} dv")
    for c in range(nk):
        P.dma("sp", dcol[:, c, :], w["alog"][c * 128:(c + 1) * 128, :],
              tag=f"{tag} al{c}")
    P.act(dcol.full(), dcol.full(), AF.Exp, tag=f"{tag} ae")
    P.tscalar("dve", dcol.full(), dcol.full(), -1.0, None, ALU.mult,
              tag=f"{tag} an")

    CG = 2
    TH = T // 2
    nth = nt // 2
    for cg in range(nk // CG):
        cs = [cg * CG + i for i in range(CG)]
        for th in range(2):
            dl, du, yy = {}, {}, {}
            for i, c in enumerate(cs):
                dl[c] = reg.get("gwB", (128, TH), F32, off=i * TH)
                du[c] = reg.get("gwB", (128, TH), F32, off=(CG + i) * TH)
                yy[c] = reg.get("msc0", (128, TH), F32, off=i * TH)
                P.dma("sp", dl[c].full(),
                      TV(df.base[th * nth:(th + 1) * nth, :, c, :]
                         .rearrange("t p w -> p t w"), df.key, 0, 1, 0,
                         df.free_total * nt * 4), tag=f"{tag} dld{c}")
                ut = reg.get("gxB", (128, TH), F32, off=4 * TH)
                P.dma("sp", ut.full(),
                      TV(u_fm.base[th * nth:(th + 1) * nth, :, c, :]
                         .rearrange("t p w -> p t w"), u_fm.key, 0, 1, 0,
                         u_fm.free_total * nt * 4), tag=f"{tag} uld{c}")
                P.tt("dve", du[c].full(), dl[c].full(), ut.full(), ALU.mult,
                     tag=f"{tag} du{c}")
                P.memset("pool", yy[c].full(), 0.0, tag=f"{tag} y0{c}")
            for n in range(NST):
                bb = reg.get("gxB", (128, TH), F32, off=(n % 2) * TH)
                cbf = reg.get("gxB", (128, TH), F32, off=(2 + n % 2) * TH)
                P.dma("sp", bb.full(),
                      TV(bct_d.base[n:n + 1, th * TH:(th + 1) * TH]
                         .partition_broadcast(128), bct_d.key, 0, 1,
                         th * TH * 4, (th + 1) * TH * 4),
                      tag=f"{tag} bb{n}")
                P.dma("sp", cbf.full(),
                      TV(bct_d.base[NST + n:NST + n + 1,
                                    th * TH:(th + 1) * TH]
                         .partition_broadcast(128), bct_d.key, 0, 1,
                         (NST + n) * T * 4 + th * TH * 4,
                         (NST + n) * T * 4 + (th + 1) * TH * 4),
                      tag=f"{tag} cb{n}")
                for ci, c in enumerate(cs):
                    par = (n * CG + ci) % 2
                    dA = reg.get("lnx", (128, TH), F32, off=par * 2 * TH)
                    dBu = reg.get("lnx", (128, TH), F32,
                                  off=par * 2 * TH + TH)
                    hh = reg.get("stg", (128, TH), F32, off=par * TH)
                    P.act(dA.full(), dl[c].full(), AF.Exp,
                          scale=dcol[:, c, n:n + 1], tag=f"{tag} dA{c}.{n}")
                    P.tt("pool", dBu.full(), du[c].full(), bb.full(),
                         ALU.mult, tag=f"{tag} dBu{c}.{n}")
                    if th == 1:
                        tmp = reg.get("st1", (128, 1), F32, off=40)
                        P.tt("dve", tmp.full(), dA[:, 0:1],
                             hstate[:, ci, n:n + 1], ALU.mult,
                             tag=f"{tag} hc{c}.{n}")
                        P.tt("dve", dBu[:, 0:1], dBu[:, 0:1], tmp.full(),
                             ALU.add, tag=f"{tag} hc2{c}.{n}")
                    P.scan("dve", hh.full(), dA.full(), dBu.full(), 0.0,
                           ALU.mult, ALU.add, tag=f"{tag} sc{c}.{n}")
                    if th == 0:
                        P.copy("dve", hstate[:, ci, n:n + 1],
                               hh[:, TH - 1:TH], tag=f"{tag} hs{c}.{n}")
                    P.tt("dve", dA.full(), hh.full(), cbf.full(), ALU.mult,
                         tag=f"{tag} yc{c}.{n}")
                    P.tt("pool", yy[c].full(), yy[c].full(), dA.full(),
                         ALU.add, tag=f"{tag} ya{c}.{n}")
            for i, c in enumerate(cs):
                ut = reg.get("gxB", (128, TH), F32, off=4 * TH)
                P.dma("sp", ut.full(),
                      TV(u_fm.base[th * nth:(th + 1) * nth, :, c, :]
                         .rearrange("t p w -> p t w"), u_fm.key, 0, 1, 0,
                         u_fm.free_total * nt * 4), tag=f"{tag} uld2{c}")
                P.stt("dve", yy[c].full(), ut.full(), dvec[:, c:c + 1],
                      yy[c].full(), ALU.mult, ALU.add, tag=f"{tag} yf{c}")
                for q in range(nth):
                    tt = th * nth + q
                    ps = ctx.pb[4 + q % 2]
                    P.transpose(ps[:, 0:128],
                                TV(yy[c].full().ap[:, q * 128:(q + 1) * 128],
                                   yy[c].full().key, 0, 128,
                                   yy[c].full().b0 + q * 512,
                                   yy[c].full().b0 + (q + 1) * 512),
                                ident.full(), tag=f"{tag} yT{c}.{tt}")
                    o = reg.get("tsd", (128, 128), F32)
                    P.copy("dve", o.full(), ps[:, 0:128], tag=f"{tag} yE")
                    P.dma("pool", ao_tm[tt * 128:(tt + 1) * 128,
                                        c * 128:(c + 1) * 128], o.full(),
                          tag=f"{tag} yW{c}.{tt}")

    for tt in range(nt):
        a = reg.get("lnx", (128, D), F32, off=(tt % 2) * 2048)
        b = reg.get("stg", (128, D), F32, off=0)
        P.dma("sp", a.full(), ao_tm[tt * 128:(tt + 1) * 128, :],
              tag=f"{tag} ra")
        P.dma("act", b.full(), xres[tt * 128:(tt + 1) * 128, :],
              tag=f"{tag} rx")
        P.tt("dve", a.full(), a.full(), b.full(), ALU.add, tag=f"{tag} rr")
        P.dma("pool", out[tt * 128:(tt + 1) * 128, :], a.full(),
              tag=f"{tag} rw")



def emit_l3_ffn(ctx, xres, out, w, T=2048, D=2048, E=8, FF=512,
                nslab=512, tag="l3f"):
    P, reg = ctx.P, ctx.reg
    nt, nk = T // 128, D // 128
    EF = E * FF
    nkf = FF // 128
    sc = ctx.scratch
    xlnf = sc("s_x1", kt4_shape(T, D), F32)
    xb = sc("s_xb", kt4_shape(T, D), BF16)
    emit_ln(ctx, xres, T, D, dst_f32=xlnf, dst_bf16=xb,
            tag=tag + ".ln")

    wn = reg.get("msc0", (128, nt * E), F32, off=1024)

    def r_cb(ps, tt, ns):
        s_ = ctx.S
        sco = s_[5][:, 0:E]
        P.copy("dve", sco, ps, tag=f"{tag} le")
        v1 = s_[6][:, 8:9]
        P.reduce("dve", v1, sco, ALU.max, tag=f"{tag} v1")
        eq1 = s_[7][:, 0:E]
        P.tscalar("dve", eq1, sco, v1, None, ALU.is_ge, tag=f"{tag} eq1")
        s2 = s_[5][:, 8:8 + E]
        P.stt("dve", s2, eq1, -1e30, sco, ALU.mult, ALU.add,
              tag=f"{tag} s2")
        v2 = s_[6][:, 9:10]
        P.reduce("dve", v2, s2, ALU.max, tag=f"{tag} v2")
        bb = s_[6][:, 10:11]
        P.tscalar("dve", bb, v2, v1, None, ALU.subtract, tag=f"{tag} d21")
        P.act(bb, bb, AF.Exp, tag=f"{tag} eb")
        w1c = s_[6][:, 11:12]
        P.tscalar("dve", w1c, bb, 1.0, None, ALU.add, tag=f"{tag} 1b")
        P.recip(w1c, w1c, tag=f"{tag} w1")
        w2c = s_[6][:, 12:13]
        P.tt("dve", w2c, bb, w1c, ALU.mult, tag=f"{tag} w2")
        m2 = s_[7][:, 8:8 + E]
        P.tscalar("dve", m2, sco, v2, None, ALU.is_ge, tag=f"{tag} m2")
        P.tt("dve", m2, m2, eq1, ALU.subtract, tag=f"{tag} msec")
        P.tscalar("dve", eq1, eq1, w1c, None, ALU.mult, tag=f"{tag} a1")
        P.tscalar("dve", m2, m2, w2c, None, ALU.mult, tag=f"{tag} a2")
        P.tt("dve", wn[:, tt * E:(tt + 1) * E], eq1, m2, ALU.add,
             tag=f"{tag} wm")

    emit_gemm(ctx, xlnf, w["rw"], r_cb, T, D, E,
              wprep="f32r_split", nslab=E, tag=tag + ".r")

    s1tm = sc("s_b1", (T, EF), BF16)
    identb = ctx.sconst("ident_bf", 128, 128, dtype=BF16)

    def h1_cb(ps, tt, ns):
        c1 = reg.get("tsa", (128, FF), BF16)
        P.copy("dve", c1.full(), ps, tag=f"{tag} c1")
        o = reg.get("tsb", (128, FF), BF16)
        P.act(o.full(), c1.full(), AF.Silu, tag=f"{tag} silu")
        P.dma("pool", s1tm[tt * 128:(tt + 1) * 128,
                           ns * FF:(ns + 1) * FF], o.full(),
              tag=f"{tag} s1w")

    emit_gemm(ctx, xb, w["w1"], h1_cb, T, D, EF,
              wprep="bf16", nslab=FF, tag=tag + ".h1")

    hT = sc("s_b2", kt4_shape(T, EF), BF16)

    def h3_cb(ps, tt, ns):
        c3 = reg.get("tsa", (128, FF), BF16)
        P.copy("dve", c3.full(), ps, tag=f"{tag} c3")
        sl = reg.get("tsb", (128, FF), BF16)
        P.dma("act", sl.full(), s1tm[tt * 128:(tt + 1) * 128,
                                     ns * FF:(ns + 1) * FF],
              tag=f"{tag} s1r")
        h = reg.get("tsc", (128, FF), BF16)
        P.tt("dve", h.full(), c3.full(), sl.full(), ALU.mult,
             tag=f"{tag} h")
        psT = ctx.pb[(tt % 2)].cast(BF16)
        for j in range(nkf):
            P.transpose(psT[:, j * 128:(j + 1) * 128],
                        TV(h.full().ap[:, j * 128:(j + 1) * 128],
                           h.full().key, 0, 128,
                           h.full().b0 + j * 256,
                           h.full().b0 + (j + 1) * 256),
                        identb.full(), tag=f"{tag} hT")
        ob = reg.get("tsd", (128, FF), BF16)
        P.copy("dve", ob.full(), psT[:, 0:FF], tag=f"{tag} hE")
        P.dma("pool", kt4_block(hT, tt, ns * nkf, nkf), ob.full(),
              tag=f"{tag} hw")

    emit_gemm(ctx, xb, w["w3"], h3_cb, T, D, EF,
              wprep="bf16", nslab=FF, tag=tag + ".h3")

    nsl = D // nslab
    for ns in range(nsl):
        for ttg in range(nt // 2):
            tts = (2 * ttg, 2 * ttg + 1)
            hb = {}
            for i, tt in enumerate(tts):
                hb[tt] = reg.get("gxB", (128, EF // 128, 128), BF16,
                                 off=i * (EF // 128) * 128)
                P.dma("sp", TV(hb[tt].full().ap.rearrange(
                    "p c t -> p (c t)"),
                    "rg_gxB", 0, 128, hb[tt].full().b0, hb[tt].full().b1),
                    kt4_block(hT, tt), tag=f"{tag} hld{tt}")
                osum = reg.get("lnx", (128, nslab), F32, off=i * nslab)
                P.memset("dve", osum.full(), 0.0, tag=f"{tag} os0")
            for e in range(E):
                w2c = reg.get("gwB", (128, nkf, nslab), BF16,
                              off=(e % 2) * nkf * nslab)
                for c in range(nkf):
                    st = reg.get(f"st{c % 2}", (128, nslab), F32)
                    P.dma("sp", st.full(),
                          w["w2"][ns, :, e * nkf + c, :],
                          tag=f"{tag} w2l{e}.{c}")
                    P.copy("dve", w2c[:, c, :], st.full(),
                           tag=f"{tag} w2c{e}.{c}")
                for i, tt in enumerate(tts):
                    ps = ctx.pb[4 + (i + e * 2) % 2][:, 0:nslab]
                    pairs = [(hb[tt][:, e * nkf + c, :], w2c[:, c, :])
                             for c in range(nkf)]
                    P.mm_group(pairs, ps, tag=f"{tag} dmm{e}.{tt}.{ns}")
                    rb = reg.get("tsa", (128, nslab), BF16, off=(i % 2) *
                                 nslab)
                    P.copy("dve", rb.full(), ps, tag=f"{tag} rb")
                    osum = reg.get("lnx", (128, nslab), F32, off=i * nslab)
                    P.stt("dve", osum.full(), rb.full(),
                          wn[:, tt * E + e:tt * E + e + 1], osum.full(),
                          ALU.mult, ALU.add, tag=f"{tag} osa")
            for i, tt in enumerate(tts):
                osum = reg.get("lnx", (128, nslab), F32, off=i * nslab)
                xv = reg.get("lnx", (128, nslab), F32, off=2 * nslab + 
                             (i % 2) * nslab)
                P.dma("act", xv.full(), xres[tt * 128:(tt + 1) * 128,
                                             ns * nslab:(ns + 1) * nslab],
                      tag=f"{tag} fx")
                P.tt("dve", osum.full(), osum.full(), xv.full(), ALU.add,
                     tag=f"{tag} fr")
                P.dma("pool", out[tt * 128:(tt + 1) * 128,
                                  ns * nslab:(ns + 1) * nslab],
                      osum.full(), tag=f"{tag} fw")



T_, D_ = 2048, 2048


def _host_prep(inputs):
    g = lambda k: np.asarray(inputs[k])
    w = {}
    for k in ("Wq", "Wk", "Wv", "Wo"):
        w["l0a_" + k.lower()] = host_wslab(
            np.ascontiguousarray(g("L0_attn_" + k)), 512)
    w["l0f_w"] = np.ascontiguousarray(g("L0_ffn_weight"))
    w["l0f_b"] = np.ascontiguousarray(g("L0_ffn_bias"))
    w["l0f_ts"] = np.asarray(g("L0_ffn_x_tensor_scale"),
                             np.float32).reshape(1, 1)
    for k in ("Wq", "Wk", "Wv", "Wo"):
        w["l1a_" + k.lower()] = host_wslab(
            np.ascontiguousarray(g("L1_attn_" + k).T), 512)
    for k in ("bq", "bk", "bv", "bo"):
        w["l1a_" + k] = np.ascontiguousarray(g("L1_attn_" + k))
    w["l1f_wg"] = host_wslab(np.ascontiguousarray(g("L1_ffn_Wg").T), 8)
    w["l1f_w1"] = host_wslab(np.ascontiguousarray(
        g("L1_ffn_W1").reshape(8192, 2048).T), 512)
    w["l1f_w3"] = host_wslab(np.ascontiguousarray(
        g("L1_ffn_W3").reshape(8192, 2048).T), 512)
    w["l1f_w2"] = host_wslab(np.ascontiguousarray(
        g("L1_ffn_W2").transpose(0, 2, 1).reshape(8192, 2048)), 512)
    w["l1f_bias"] = np.ascontiguousarray(g("L1_ffn_bias"))
    w["l2a_wq"] = host_wslab(np.ascontiguousarray(g("L2_attn_Wq").T), 512)
    w["l2a_wk"] = host_wslab(np.ascontiguousarray(g("L2_attn_Wk").T), 512)
    w["l2a_wv"] = host_wslab(np.ascontiguousarray(g("L2_attn_Wv").T), 512)
    w["l2a_wo"] = host_wslab(np.ascontiguousarray(g("L2_attn_Wo").T), 512)
    w["l2f_wg"] = host_wslab(np.ascontiguousarray(g("L2_ffn_gate_w").T), 512)
    w["l2f_wu"] = host_wslab(np.ascontiguousarray(g("L2_ffn_up_w").T), 512)
    w["l2f_wd"] = host_wslab(np.ascontiguousarray(g("L2_ffn_down_w").T), 512)
    w["l2f_gs"] = np.ascontiguousarray(g("L2_ffn_gate_scale"))
    w["l2f_us"] = np.ascontiguousarray(g("L2_ffn_up_scale"))
    w["l2f_ds"] = np.ascontiguousarray(g("L2_ffn_down_scale"))
    w["l3a_xproj"] = host_wslab(np.ascontiguousarray(
        g("L3_attn_x_proj_w").T), 160)
    w["l3a_dtw"] = host_wslab(np.ascontiguousarray(
        g("L3_attn_dt_proj_w").T), 512)
    w["l3a_dtb"] = np.ascontiguousarray(g("L3_attn_dt_proj_b"))
    w["l3a_alog"] = np.ascontiguousarray(g("L3_attn_A_log"))
    w["l3a_dvec"] = np.ascontiguousarray(
        g("L3_attn_D").reshape(16, 128).T)
    w["l3f_rw"] = host_wslab(np.ascontiguousarray(g("L3_ffn_router_w").T), 8)
    w["l3f_w1"] = host_wslab(np.ascontiguousarray(
        g("L3_ffn_w1").transpose(1, 0, 2).reshape(2048, 4096)), 512)
    w["l3f_w3"] = host_wslab(np.ascontiguousarray(
        g("L3_ffn_w3").transpose(1, 0, 2).reshape(2048, 4096)), 512)
    w["l3f_w2"] = host_wslab(np.ascontiguousarray(
        g("L3_ffn_w2").reshape(4096, 2048)), 512)
    return w


_BLOCK_WDEFS = {
    "l0a": [("l0a_wq", None, F32), ("l0a_wk", None, F32),
            ("l0a_wv", None, F32), ("l0a_wo", None, F32)],
    "l0f": [("l0f_w", (2048, 2048), F32), ("l0f_b", (2048,), F32),
            ("l0f_ts", (1, 1), F32)],
    "l1a": [("l1a_wq", None, F32), ("l1a_wk", None, F32),
            ("l1a_wv", None, F32), ("l1a_wo", None, F32),
            ("l1a_bq", (2048,), F32), ("l1a_bk", (2048,), F32),
            ("l1a_bv", (2048,), F32), ("l1a_bo", (2048,), F32)],
    "l1f": [("l1f_wg", wslab_shape(2048, 8, 8), F32),
            ("l1f_w1", wslab_shape(2048, 8192), F32),
            ("l1f_w3", wslab_shape(2048, 8192), F32),
            ("l1f_w2", wslab_shape(8192, 2048), F32),
            ("l1f_bias", (8,), F32)],
    "l2a": [("l2a_wq", None, F32),
            ("l2a_wk", wslab_shape(2048, 512, 512), F32),
            ("l2a_wv", wslab_shape(2048, 512, 512), F32),
            ("l2a_wo", None, F32)],
    "l2f": [("l2f_wg", wslab_shape(2048, 8192), I32),
            ("l2f_wu", wslab_shape(2048, 8192), I32),
            ("l2f_wd", wslab_shape(8192, 2048), I32),
            ("l2f_gs", (8192,), F32), ("l2f_us", (8192,), F32),
            ("l2f_ds", (2048,), F32)],
    "l3a": [("l3a_xproj", wslab_shape(2048, 160, 160), F32),
            ("l3a_dtw", wslab_shape(128, 2048), F32),
            ("l3a_dtb", (2048,), F32), ("l3a_alog", (2048, 16), F32),
            ("l3a_dvec", (128, 16), F32)],
    "l3f": [("l3f_rw", wslab_shape(2048, 8, 8), F32),
            ("l3f_w1", wslab_shape(2048, 4096, 512), F32),
            ("l3f_w3", wslab_shape(2048, 4096, 512), F32),
            ("l3f_w2", wslab_shape(4096, 2048), F32)],
}


def _declare_block_inputs(P, block):
    w = {}
    for name, shape, dt in _BLOCK_WDEFS[block]:
        if shape is None:
            shape = wslab_shape(2048, 2048, 512)
        short = name.split("_", 1)[1]
        w[short] = P.dram(name, shape, dt, kind="ExternalInput")
    return w


def _emit_block(ctx, block, xin, xout, w):
    if block == "l0a":
        emit_l0_attn(ctx, xin, xout, w, T_, D_, 16)
    elif block == "l0f":
        emit_l0_ffn(ctx, xin, xout, w["w"], w["b"], w["ts"], T_, D_)
    elif block == "l1a":
        emit_l1_attn(ctx, xin, xout, w, T_, D_, 16)
    elif block == "l1f":
        emit_l1_ffn(ctx, xin, xout, w, T_, D_, 8, 1024)
    elif block == "l2a":
        emit_l2_attn(ctx, xin, xout, w, T_, D_, 16, 4, 512)
    elif block == "l2f":
        emit_l2_ffn(ctx, xin, xout, w, T_, D_, 8192)
    elif block == "l3a":
        emit_l3_attn(ctx, xin, xout, w, T_, D_, 16, 128)
    elif block == "l3f":
        emit_l3_ffn(ctx, xin, xout, w, T_, D_, 8, 512)


BLOCKS = ("l0a", "l0f", "l1a", "l1f", "l2a", "l2f", "l3a", "l3f")

_prog_cache = {}


def _build_block_prog(block):
    if block in _prog_cache:
        return _prog_cache[block]
    nc = bass.Bass("TRN2", target_bir_lowering=False)
    P = Prog(nc)
    x_in = P.dram("x", (T_, D_), F32, kind="ExternalInput")
    y_out = P.dram("y", (T_, D_), F32, kind="ExternalOutput")
    w = _declare_block_inputs(P, block)
    ctx = Ctx(P, std_consts(T_))
    _emit_block(ctx, block, x_in, y_out, w)
    P.emit()
    _prog_cache[block] = (nc, ctx)
    return nc, ctx


def _build_full_prog(nblocks=8):
    key = ("full", nblocks)
    if key in _prog_cache:
        return _prog_cache[key]
    nc = bass.Bass("TRN2", target_bir_lowering=False)
    P = Prog(nc)
    x_in = P.dram("x", (T_, D_), F32, kind="ExternalInput")
    y_out = P.dram("y", (T_, D_), F32, kind="ExternalOutput")
    ctx = Ctx(P, std_consts(T_))
    resA = ctx.scratch("s_resA", (T_, D_), F32)
    resB = ctx.scratch("s_resB", (T_, D_), F32)
    chain = [x_in, resA, resB, resA, resB, resA, resB, resA]
    chain = chain[:nblocks] + [y_out]
    for i, block in enumerate(BLOCKS[:nblocks]):
        w = _declare_block_inputs(P, block)
        _emit_block(ctx, block, chain[i], chain[i + 1], w)
    P.emit()
    _prog_cache[key] = (nc, ctx)
    return nc, ctx


FUSED = True


def kernel(**inputs):
    x = np.ascontiguousarray(np.asarray(inputs["L0_attn_x"], np.float32))
    B = x.shape[0]
    w = _host_prep(inputs)
    shards = [np.ascontiguousarray(x[c]) for c in range(B)]
    if FUSED:
        nc, ctx = _build_full_prog()
        in_maps = []
        for c in range(B):
            im = {"x": shards[c]}
            for block in BLOCKS:
                for name, shape, dt in _BLOCK_WDEFS[block]:
                    im[name] = w[name]
            for n, a in ctx.const_arrays.items():
                im["c_" + n] = a
            in_maps.append(im)
        res = run_bass_kernel_spmd(nc, in_maps, core_ids=list(range(B)))
        return np.stack([res.results[c]["y"] for c in range(B)], 0)
    for block in BLOCKS:
        nc, ctx = _build_block_prog(block)
        in_maps = []
        for c in range(B):
            im = {"x": shards[c]}
            for name, shape, dt in _BLOCK_WDEFS[block]:
                im[name] = w[name]
            for n, a in ctx.const_arrays.items():
                im["c_" + n] = a
            in_maps.append(im)
        res = run_bass_kernel_spmd(nc, in_maps, core_ids=list(range(B)))
        shards = [res.results[c]["y"] for c in range(B)]
    return np.stack(shards, 0)


# version 08 (final)
# speedup vs baseline: 1.2755x; 1.0493x over previous
import numpy as np
import concourse.bass as bass
import concourse.mybir as mybir
from concourse.bass_utils import run_bass_kernel_spmd

F32 = mybir.dt.float32
F32R = mybir.dt.float32r
BF16 = mybir.dt.bfloat16
I32 = mybir.dt.int32
AF = mybir.ActivationFunctionType
ALU = mybir.AluOpType
AX = mybir.AxisListType

MAGIC = float(np.float32(12582912.0))
MASK11 = -4096
EXPMASK = 0x7F800000

ENGINES = ("pe", "dve", "act", "pool", "sp")



class TV:
    __slots__ = ("ap", "key", "p0", "p1", "b0", "b1")

    def __init__(self, ap, key, p0, p1, b0, b1):
        self.ap, self.key = ap, key
        self.p0, self.p1, self.b0, self.b1 = p0, p1, b0, b1

    def overlaps(self, o):
        return (self.key == o.key and self.p0 < o.p1 and o.p0 < self.p1
                and self.b0 < o.b1 and o.b0 < self.b1)

    def covers(self, o):
        return (self.key == o.key and self.p0 <= o.p0 and self.p1 >= o.p1
                and self.b0 <= o.b0 and self.b1 >= o.b1)

    def r32(self):
        return TV(self.ap.bitcast(F32R), self.key, self.p0, self.p1,
                  self.b0, self.b1)

    def i32(self):
        return TV(self.ap.bitcast(I32), self.key, self.p0, self.p1,
                  self.b0, self.b1)


def _srange(idx, n):
    if isinstance(idx, slice):
        start, stop, step = idx.indices(n)
        assert step == 1
        return start, stop
    i = int(idx)
    return i, i + 1


class Buf:

    def __init__(self, base_ap, key, shape, dtype, space):
        self.base = base_ap
        self.key = key
        self.shape = tuple(shape)
        self.dtype = dtype
        self.space = space
        self.esize = mybir.dt.size(dtype)
        strides = []
        acc = 1
        for s in reversed(self.shape[1:]):
            strides.append(acc)
            acc *= s
        self.free_strides = tuple(reversed(strides))
        self.free_total = acc

    def cast(self, dtype, shape=None):
        if shape is None:
            f = self.esize // mybir.dt.size(dtype)
            if f >= 1:
                shape = self.shape[:-1] + (self.shape[-1] * f,)
            else:
                k = mybir.dt.size(dtype) // self.esize
                assert self.shape[-1] % k == 0
                shape = self.shape[:-1] + (self.shape[-1] // k,)
        ap = self.base.bitcast(dtype)
        b = Buf(ap, self.key, shape, dtype, self.space)
        return b

    def reshape(self, shape):
        total = 1
        for s in shape[1:]:
            total *= s
        assert shape[0] == self.shape[0] and total == self.free_total, \
            (shape, self.shape)
        assert len(self.shape) == 2, "reshape only from 2D"
        if len(shape) == 2:
            return self
        names = [f"b{i}" for i in range(len(shape) - 1)]
        pat = "p (" + " ".join(names) + ") -> p " + " ".join(names)
        kw = {n: s for n, s in zip(names, shape[1:])}
        ap = self.base.rearrange(pat, **kw)
        return Buf(ap, self.key, shape, self.dtype, self.space)

    def __getitem__(self, idx):
        if not isinstance(idx, tuple):
            idx = (idx,)
        idx = idx + (slice(None),) * (len(self.shape) - len(idx))
        p0, p1 = _srange(idx[0], self.shape[0])
        b_lo, b_hi = 0, 1
        for i, ix in enumerate(idx[1:]):
            s0, s1 = _srange(ix, self.shape[1 + i])
            b_lo += s0 * self.free_strides[i]
            b_hi += (s1 - 1) * self.free_strides[i]
        ap = self.base[idx]
        if self.space == "dram":
            lo = (p0 * self.free_total + b_lo) * self.esize
            hi = ((p1 - 1) * self.free_total + b_hi) * self.esize
            return TV(ap, self.key, 0, 1, lo, hi)
        return TV(ap, self.key, p0, p1, b_lo * self.esize, b_hi * self.esize)

    def full(self):
        return self[(slice(None),) * len(self.shape)]


class Op:
    __slots__ = ("eng", "fn", "reads", "writes", "idx", "inc", "waits",
                 "tag", "n_inst")

    def __init__(self, eng, fn, reads, writes, idx, inc, tag, n_inst=1):
        self.eng, self.fn = eng, fn
        self.reads, self.writes = reads, writes
        self.idx, self.inc = idx, inc
        self.waits = []
        self.tag = tag
        self.n_inst = n_inst


class Prog:
    ANNOTATE = False
    """Records ops with dependency tracking; emits a raw-Bass Block."""

    def __init__(self, nc):
        self.nc = nc
        self.ops = {e: [] for e in ENGINES}
        self.sems = {}
        for e in ENGINES:
            self.sems["e_" + e] = nc.alloc_semaphore(name="e_" + e)
        for q in ("sp", "act", "pool"):
            self.sems["q_" + q] = nc.alloc_semaphore(name="q_" + q)
        self.counts = {e: 0 for e in ENGINES}
        self.qcounts = {q: 0 for q in ("sp", "act", "pool")}
        self.mem = {}
        self.sbuf_bytes = 0
        self.n_inst = 0
        self._fconsts = {}
        self._fconst_buf = None
        self.qbounds = {q: [] for q in ("sp", "act", "pool")}
        self.qbound_starts = {q: set() for q in ("sp", "act", "pool")}
        self.pe_meta = []

    def float_const(self, val):
        if val in self._fconsts:
            return self._fconsts[val]
        if self._fconst_buf is None:
            self._fconst_buf = self.sbuf("fconst", (128, 8), F32)
        i = len(self._fconsts)
        assert i < 8
        tv = self._fconst_buf[:, i:i + 1]
        self.memset("pool", tv, val, tag=f"fconst {val}")
        self.nc.const_aps.register(F32, val, tv.ap)
        self._fconsts[val] = tv
        return tv

    def sbuf(self, name, shape, dtype):
        h = self.nc.alloc_sbuf_tensor(name, list(shape), dtype)
        b = Buf(h.ap(), name, shape, dtype, "sbuf")
        n = 1
        for s in shape[1:]:
            n *= s
        self.sbuf_bytes += n * mybir.dt.size(dtype)
        return b

    def psum(self, name, shape=(128, 512), dtype=F32):
        h = self.nc.alloc_psum_tensor(name, list(shape), dtype)
        return Buf(h.ap(), name, shape, dtype, "psum")

    def dram(self, name, shape, dtype, kind="Internal"):
        h = self.nc.dram_tensor(name, list(shape), dtype, kind=kind)
        return Buf(h.ap(), name, shape, dtype, "dram")

    def _record(self, eng, fn, reads, writes, queue=None, tag="", n_inst=1):
        reads = [r for r in reads if r is not None]
        writes = [w for w in writes if w is not None]
        if queue is None:
            src = ("e", eng, self.counts[eng])
            self.counts[eng] += 1
            inc = ("e_" + eng, 1)
        else:
            src = ("q", queue, self.qcounts[queue])
            self.qcounts[queue] += 1
            inc = ("q_" + queue, 16)
        if eng in ("act", "dve"):
            promoted = [r for r in reads if r.key.startswith("pb")]
            if promoted:
                writes = writes + promoted
        op = Op(eng, fn, reads, writes, src[2], inc, tag, n_inst)
        self.n_inst += n_inst

        deps = []
        for r in reads:
            lst = self.mem.get(r.key)
            if lst:
                deps += [s for tv, s in lst["w"] if tv.overlaps(r)]
        for w in writes:
            lst = self.mem.get(w.key)
            if lst:
                deps += [s for tv, s in lst["w"] if tv.overlaps(w)]
                deps += [s for tv, s in lst["r"] if tv.overlaps(w)]
        waits = {}
        for d in deps:
            if d == src:
                continue
            if (d[0] == "e" and src[0] == "e" and d[1] == src[1] == "pe"):
                continue
            sem = ("e_" if d[0] == "e" else "q_") + d[1]
            if d[0] == "e":
                val = d[2] + 1
            else:
                q, j = d[1], d[2]
                b = None
                for t in self.qbounds[q]:
                    if t >= j + 1:
                        b = t
                        break
                if b is None:
                    b = self.qcounts[q]
                    if src == ("q", q, b):
                        self.qbound_starts[q].add(b)
                        self.qbounds[q].append(b)
                    else:
                        self.qbounds[q].append(b)
                        self.qbound_starts[q].add(b)
                val = 16 * b
            waits[sem] = max(waits.get(sem, 0), val)
        op.waits = sorted(waits.items())

        for w in writes:
            lst = self.mem.setdefault(w.key, {"w": [], "r": []})
            lst["w"] = [(tv, s) for (tv, s) in lst["w"] if not w.covers(tv)]
            lst["r"] = [(tv, s) for (tv, s) in lst["r"] if not w.covers(tv)]
            lst["w"].append((w, src))
        for r in reads:
            lst = self.mem.setdefault(r.key, {"w": [], "r": []})
            lst["r"].append((r, src))
        self.ops[eng].append(op)
        return op

    def dma(self, eng, out_tv, in_tv, tag=""):
        def fn(e):
            return e.dma_start(out_tv.ap, in_tv.ap)
        return self._record(eng, fn, [in_tv], [out_tv], queue=eng, tag=tag)

    def memset(self, eng, out_tv, val, tag=""):
        def fn(e):
            return e.memset(out_tv.ap, val)
        return self._record(eng, fn, [], [out_tv], tag=tag)

    def mm_group(self, pairs, out_tv, tag="", start=True, stop=True):
        n = len(pairs)

        def fn(e):
            last = None
            for i, (l, r) in enumerate(pairs):
                last = e.matmul(out_tv.ap, l.ap, r.ap,
                                start=(start and i == 0),
                                stop=(stop and i == n - 1))
            return last
        reads = []
        for l, r in pairs:
            reads += [l, r]
        self.pe_meta.append([("M", tuple(l.ap.shape), tuple(r.ap.shape))
                             for l, r in pairs])
        return self._record("pe", fn, reads, [out_tv], tag=tag, n_inst=n)

    def transpose(self, out_psum_tv, in_tv, ident_tv, tag=""):
        def fn(e):
            return e.transpose(out_psum_tv.ap, in_tv.ap, ident_tv.ap)
        self.pe_meta.append([("T", tuple(in_tv.ap.shape))])
        return self._record("pe", fn, [in_tv, ident_tv], [out_psum_tv], tag=tag)

    def act(self, out_tv, in_tv, func, bias=0.0, scale=1.0, tag="",
            extra=()):
        extra = list(extra)
        kb = bias
        if isinstance(bias, TV):
            kb = bias.ap
            extra.append(bias)
        elif float(bias) != 0.0:
            extra.append(self.float_const(float(bias)))
        ks = scale
        if isinstance(scale, TV):
            ks = scale.ap
            extra.append(scale)
        elif float(scale) != 1.0:
            extra.append(self.float_const(float(scale)))

        def fn(e):
            return e.activation(out_tv.ap, in_tv.ap, func, bias=kb, scale=ks)
        return self._record("act", fn, [in_tv] + extra, [out_tv], tag=tag)

    def tscalar(self, eng, out_tv, in_tv, s1, s2, op0, op1=None, tag="",
                extra=()):
        extra = list(extra)
        a1 = s1
        if isinstance(s1, TV):
            a1 = s1.ap
            extra.append(s1)
        a2 = s2
        if isinstance(s2, TV):
            a2 = s2.ap
            extra.append(s2)

        def fn(e):
            if op1 is None:
                return e.tensor_scalar(out_tv.ap, in_tv.ap, a1, a2, op0)
            return e.tensor_scalar(out_tv.ap, in_tv.ap, a1, a2, op0, op1)
        return self._record(eng, fn, [in_tv] + extra, [out_tv], tag=tag)

    def tt(self, eng, out_tv, in0_tv, in1_tv, op, tag=""):
        def fn(e):
            return e.tensor_tensor(out_tv.ap, in0_tv.ap, in1_tv.ap, op)
        return self._record(eng, fn, [in0_tv, in1_tv], [out_tv], tag=tag)

    def stt(self, eng, out_tv, in0_tv, scalar, in1_tv, op0, op1, tag=""):
        extra = []
        sc = scalar
        if isinstance(scalar, TV):
            sc = scalar.ap
            extra.append(scalar)

        def fn(e):
            return e.scalar_tensor_tensor(out_tv.ap, in0_tv.ap, sc, in1_tv.ap,
                                          op0, op1)
        return self._record(eng, fn, [in0_tv, in1_tv] + extra, [out_tv],
                            tag=tag)

    def reduce(self, eng, out_tv, in_tv, op, axis=AX.X, absval=None,
               negate=None, tag=""):
        def fn(e):
            return e.tensor_reduce(out_tv.ap, in_tv.ap, axis, op,
                                   apply_absolute_value=absval, negate=negate)
        return self._record(eng, fn, [in_tv], [out_tv], tag=tag)

    def scan(self, eng, out_tv, d0_tv, d1_tv, init, op0, op1, tag=""):
        def fn(e):
            return e.tensor_tensor_scan(out_tv.ap, d0_tv.ap, d1_tv.ap, init,
                                        op0, op1)
        return self._record(eng, fn, [d0_tv, d1_tv], [out_tv], tag=tag)

    def recip(self, out_tv, in_tv, tag=""):
        def fn(e):
            return e.reciprocal(out_tv.ap, in_tv.ap)
        return self._record("dve", fn, [in_tv], [out_tv], tag=tag)

    def copy(self, eng, out_tv, in_tv, tag="", extra=()):
        if eng == "act":
            def fn(e):
                return e.copy(out_tv.ap, in_tv.ap)
        else:
            def fn(e):
                return e.tensor_copy(out_tv.ap, in_tv.ap)
        return self._record(eng, fn, [in_tv] + list(extra), [out_tv],
                            tag=tag)

    def select(self, out_tv, mask_tv, t_tv, f_tv, tag=""):
        def fn(e):
            return e.select(out_tv.ap, mask_tv.ap, t_tv.ap, f_tv.ap)
        return self._record("dve", fn, [mask_tv, t_tv, f_tv], [out_tv],
                            tag=tag, n_inst=2)

    def emit(self):
        nc = self.nc
        sem_handles = self.sems

        def replay(eng_name):
            qb = self.qbound_starts.get(eng_name, set())

            def body(e):
                waited = {}
                for op in self.ops[eng_name]:
                    if op.inc[0].startswith("q_") and op.idx in qb:
                        sem = "q_" + eng_name
                        if waited.get(sem, 0) < 16 * op.idx:
                            e.wait_ge(sem_handles[sem], 16 * op.idx)
                            waited[sem] = 16 * op.idx
                    for sem, val in op.waits:
                        if waited.get(sem, 0) < val:
                            e.wait_ge(sem_handles[sem], val)
                            waited[sem] = val
                    inst = op.fn(e)
                    if Prog.ANNOTATE:
                        inst.annotate(op.tag)
                    sem, amt = op.inc
                    inst.then_inc(sem_handles[sem], amt)
                if eng_name == "sp":
                    for q, cnt in self.qcounts.items():
                        if cnt:
                            e.wait_ge(sem_handles["q_" + q], 16 * cnt)
            return body

        with nc.Block() as block:
            block.tensor(replay("pe"))
            block.vector(replay("dve"))
            block.scalar(replay("act"))
            block.gpsimd(replay("pool"))
            block.sync(replay("sp"))



class Ctx:

    def __init__(self, P: Prog, host_consts: dict):
        self.P = P
        self.consts = {}
        self.const_arrays = dict(host_consts)
        for name, arr in host_consts.items():
            dt = {np.dtype(np.float32): F32,
                  np.dtype(np.int32): I32}[arr.dtype]
            self.consts[name] = P.dram("c_" + name, arr.shape, dt,
                                       kind="ExternalInput")
        self.pb = [P.psum(f"pb{i}") for i in range(8)]
        self.S = [P.sbuf(f"ar_s{i}", (128, 16), F32) for i in range(8)]
        self.R = [P.sbuf(f"ar_r{i}", (1, 128 if i < 2 else 512), F32)
                  for i in range(4)]
        self.reg = Reg(P)
        self._sbuf_consts = {}

    SCRATCH_MB = {"s_g1": 72, "s_g2": 72, "s_t1": 72}

    def scratch(self, name, shape, dtype):
        if not hasattr(self, "_scratch"):
            self._scratch = {}
        n = 1
        for s in shape:
            n *= s
        nbytes = n * mybir.dt.size(dtype)
        if name not in self._scratch:
            mb = self.SCRATCH_MB.get(name, 20)
            h = self.P.dram("zz_" + name, (mb * 1024 * 1024,),
                            mybir.dt.uint8)
            self._scratch[name] = h
        h = self._scratch[name]
        assert nbytes <= h.shape[0], (name, shape, dtype)
        ap = h.base.bitcast(dtype)[0:n]
        names = [f"b{i}" for i in range(len(shape))]
        pat = "(" + " ".join(names) + ") -> " + " ".join(names)
        kw = {nm: s for nm, s in zip(names, shape)}
        ap = ap.rearrange(pat, **kw)
        return Buf(ap, h.key, shape, dtype, "dram")

    def sconst(self, name, rows, cols, dtype=None):
        if name in self._sbuf_consts:
            return self._sbuf_consts[name]
        src = self.consts[name]
        dt = dtype or src.dtype
        b = self.P.sbuf("sc_" + name, (rows, cols), dt)
        q = "sp" if dt == src.dtype else "pool"
        self.P.dma(q, b.full(), src.full(), tag=f"const {name}")
        self._sbuf_consts[name] = b
        return b


def std_consts(T=2048):
    c = {}
    c["ident"] = np.eye(128, dtype=np.float32)
    c["ident_bf"] = np.eye(128, dtype=np.float32)
    iu = np.triu(np.ones((128, 128), np.float32), 0)
    c["mask_triu"] = iu
    c["mask_sl"] = np.tril(np.ones((128, 128), np.float32), -1)
    c["ones_row"] = np.ones((1, 512), np.float32)
    dec = np.exp(-np.arange(T, dtype=np.float64) / T).astype(np.float32)
    c["decay"] = np.ascontiguousarray(dec.reshape(T // 128, 128).T)
    inv_freq = (1.0 / 10000.0 ** (np.arange(0, 128, 2, dtype=np.float32)
                                  / 128.0))
    ang = np.arange(T, dtype=np.float32)[:, None] * inv_freq[None, :]
    c["ropec"] = np.cos(ang).astype(np.float32)
    c["ropes"] = np.sin(ang).astype(np.float32)
    return c




def kt4_shape(T, K):
    return (T // 128, 128, K // 128, 128)


def wslab_shape(K, N, nslab=512):
    return (N // nslab, 128, K // 128, nslab)


def host_wslab(w, nslab=512):
    K, N = w.shape
    return np.ascontiguousarray(
        w.reshape(K // 128, 128, N // nslab, nslab).transpose(2, 1, 0, 3))


RESERVE_KB = {
    "gwR": 61, "gwB": 16,
    "gxw": 32, "gxR0": 2, "gxR1": 2, "gxB": 24,
    "stg": 8,
    "lnx": 16,
    "tsa": 2, "tsb": 2, "tsc": 2, "tsd": 2,
    "rex0": 2, "rex1": 2, "reo0": 2, "reo1": 2,
    "st0": 2, "st1": 2,
    "msc0": 8, "msc1": 8,
}

GW_BASE = 11392


class Reg:

    def __init__(self, P):
        self.P = P
        self.raw = {}

    def _raw(self, name):
        if name not in self.raw:
            kb = RESERVE_KB[name]
            self.raw[name] = self.P.sbuf("rg_" + name, (128, kb * 256), F32)
        return self.raw[name]

    def get(self, name, shape, dtype=F32, off=0):
        b = self._raw(name)
        n = 1
        for s in shape[1:]:
            n *= s
        es = mybir.dt.size(dtype)
        cap = b.free_total * 4 // es
        assert off + n <= cap, (name, shape, dtype, off, cap)
        ap = b.base.bitcast(dtype)[0:shape[0], off:off + n]
        v = Buf(ap, b.key, (shape[0], n), dtype, "sbuf")
        if len(shape) > 2:
            v = v.reshape(shape)
        return _OffsetBuf(v, off * es)


class _OffsetBuf:

    def __init__(self, buf, byte_off):
        self._b = buf
        self._o = byte_off
        self.shape = buf.shape
        self.dtype = buf.dtype
        self.free_total = buf.free_total

    def __getitem__(self, idx):
        tv = self._b[idx]
        return TV(tv.ap, tv.key, tv.p0, tv.p1, tv.b0 + self._o,
                  tv.b1 + self._o)

    def full(self):
        return self[(slice(None),) * len(self.shape)]




def kt4_block(buf, tt, k0=0, nk=None):
    nkall = buf.shape[2]
    if nk is None:
        nk = nkall
    ap = buf.base[tt].rearrange("p c t -> p (c t)")
    if k0 or nk != nkall:
        ap = ap[:, k0 * 128:(k0 + nk) * 128]
    blk = buf.free_total
    es = buf.esize
    base_off = tt * 128 * nkall * 128 * es
    return TV(ap, buf.key, 0, 1, base_off + k0 * 128 * es,
              base_off + (nkall * 127 + nk) * 128 * es + 128 * es)


def wslab_block(buf, ns, c0=0, cn=None):
    _, _, kall, nslab = buf.shape
    if cn is None:
        cn = kall
    ap = buf.base[ns].rearrange("p c s -> p (c s)")
    if c0 or cn != kall:
        ap = ap[:, c0 * nslab:(c0 + cn) * nslab]
    es = buf.esize
    base = ns * 128 * kall * nslab * es
    return TV(ap, buf.key, 0, 1, base, base + 128 * kall * nslab * es)


def emit_gemm(ctx, x_src, w_src, out_cb, T, K, N, *, wprep, nslab=512,
              banks=(2, 3, 4, 5, 6, 7), wcg=2, extra_pairs=None,
              kgroup=None, tag="g"):
    P, reg = ctx.P, ctx.reg
    nt, nsl = T // 128, N // nslab
    kall = w_src.shape[2]
    k0, nk = kgroup if kgroup is not None else (0, kall)
    assert nk * 128 <= 2048 * 128 and nk <= 16
    G = len(banks)
    xbf = (x_src.dtype == BF16)
    ncg = (nk + wcg - 1) // wcg
    f32rW = wprep == "f32r_split"
    intW = wprep == "int_f32r"
    assert f32rW or intW or wprep in ("bf16", "raw")
    assert xbf == (wprep in ("bf16", "raw"))
    wq_ct, ws_ct, xs_ct = [0], [0], [0]
    WQ = ("sp", "pool")

    for g0 in range(0, nt, G):
        tts = list(range(g0, min(g0 + G, nt)))
        xs = []
        for ti, tt in enumerate(tts):
            if xbf:
                xb = reg.get("gxB", (128, nk, 128), BF16, off=ti * 2048)
            elif ti < 4:
                xb = reg.get("gxw", (128, nk, 128), F32, off=ti * 2048)
            else:
                xb = reg.get("gxB", (128, nk, 128), F32,
                             off=(ti - 4) * 2048)
            dst = xb.full()
            dst2 = TV(dst.ap.rearrange("p c t -> p (c t)"), dst.key,
                      dst.p0, dst.p1, dst.b0, dst.b1)
            P.dma("sp", dst2, kt4_block(x_src, tt, k0, nk),
                  tag=f"{tag} x{tt}")
            xs.append(xb)
        for ns in range(nsl):
            for ci in range(ncg):
                c0, cn = ci * wcg, min(wcg, nk - ci * wcg)
                sl = ws_ct[0] % 2
                ws_ct[0] += 1
                wsrc = wslab_block(w_src, ns, k0 + c0, cn)
                qn = WQ[wq_ct[0] % 2]
                wq_ct[0] += 1
                if wprep == "raw":
                    wb = reg.get("gwB", (128, cn, nslab), BF16,
                                 off=6144 + sl * wcg * nslab)
                    wtv = wb.full()
                    wtv2 = TV(wtv.ap.rearrange("p c s -> p (c s)"), wtv.key,
                              wtv.p0, wtv.p1, wtv.b0, wtv.b1)
                    P.dma(qn, wtv2, wsrc, tag=f"{tag} W{ns}.{ci}")
                    wparts = [wb]
                else:
                    sdt = I32 if intW else F32
                    st = reg.get("stg", (128, cn * nslab), sdt,
                                 off=sl * wcg * nslab)
                    P.dma(qn, st.full(), wsrc, tag=f"{tag} W{ns}.{ci}")
                    st3 = reg.get("stg", (128, cn, nslab), sdt,
                                  off=sl * wcg * nslab)
                    if f32rW:
                        w1 = reg.get("gwR", (128, cn, nslab), F32R,
                                     off=GW_BASE + sl * 2 * wcg * nslab)
                        w2 = reg.get("gwR", (128, cn, nslab), F32R,
                                     off=GW_BASE + sl * 2 * wcg * nslab
                                     + wcg * nslab)
                        P.copy("dve", w1.full(), st.full(),
                               tag=f"{tag} ws1")
                        P.tt("dve", w2.full(), st.full(), w1.full(),
                             ALU.subtract, tag=f"{tag} ws2")
                        wparts = [w1, w2]
                    elif intW:
                        wc = reg.get("gwR", (128, cn, nslab), F32R,
                                     off=GW_BASE + sl * wcg * nslab)
                        P.tscalar("dve", wc.full(), st.full(), 128.0, None,
                                  ALU.subtract, tag=f"{tag} wi")
                        wparts = [wc]
                    else:
                        wb = reg.get("gwB", (128, cn, nslab), BF16,
                                     off=6144 + sl * wcg * nslab)
                        P.copy("dve", wb.full(), st.full(),
                               tag=f"{tag} wc")
                        wparts = [wb]
                for ti, tt in enumerate(tts):
                    ps = ctx.pb[banks[ti]][:, 0:nslab]
                    pairs = []
                    if xbf:
                        for c in range(cn):
                            pairs.append((xs[ti][:, c0 + c, :],
                                          wparts[0][:, c, :]))
                    else:
                        xsl = xs_ct[0] % 2
                        par = xs_ct[0] % 2
                        xs_ct[0] += 1
                        xraw = xs[ti][:, c0:c0 + cn, :]
                        xraw = TV(xraw.ap.rearrange("p c t -> p (c t)"),
                                  xraw.key, xraw.p0, xraw.p1, xraw.b0,
                                  xraw.b1)
                        x1t = reg.get("gxR0", (128, cn, 128), F32R,
                                      off=xsl * wcg * 128)
                        P.copy("act", x1t.full(), xraw,
                               tag=f"{tag} xs1")
                        x2t = None
                        if f32rW or intW:
                            x2t = reg.get("gxR1", (128, cn, 128), F32R,
                                          off=xsl * wcg * 128)
                            P.tt("dve", x2t.full(), xraw,
                                 x1t.full(), ALU.subtract,
                                 tag=f"{tag} xs2")
                        for c in range(cn):
                            if f32rW:
                                pairs += [(x1t[:, c, :], wparts[0][:, c, :]),
                                          (x2t[:, c, :], wparts[0][:, c, :]),
                                          (x1t[:, c, :], wparts[1][:, c, :])]
                            elif intW:
                                pairs += [(x1t[:, c, :], wparts[0][:, c, :]),
                                          (x2t[:, c, :], wparts[0][:, c, :])]
                            else:
                                pairs.append((x1t[:, c, :],
                                              wparts[0][:, c, :]))
                    stop = (ci == ncg - 1)
                    if stop and extra_pairs is not None:
                        pairs = pairs + list(extra_pairs(tt, ns))
                    P.mm_group(pairs, ps, start=(ci == 0), stop=stop,
                               tag=f"{tag} mm{tt}.{ns}")
                    if stop:
                        out_cb(ps, tt, ns)


def _fm_outs(ctx, pv, tt, j0, jn, dst_split, dst_f32, dst_bf16,
             dst_bf163, tag):
    P = ctx.P
    if dst_split is not None:
        d1, d2 = dst_split
        o1 = ctx.reg.get("tsa", (128, jn * 128), F32R)
        o2 = ctx.reg.get("tsb", (128, jn * 128), F32R)
        P.copy("dve", o1.full(), pv, tag=f"{tag} s1")
        P.tt("dve", o2.full(), pv, o1.full(), ALU.subtract, tag=f"{tag} s2")
        P.dma("pool", kt4_block(d1, tt, j0, jn), o1.full(), tag=f"{tag} w1")
        P.dma("pool", kt4_block(d2, tt, j0, jn), o2.full(), tag=f"{tag} w2")
    prev = []
    if dst_split is not None:
        prev = [ctx.reg.get("tsb", (128, jn * 128), F32R).full()]
    if dst_f32 is not None:
        o3 = ctx.reg.get("tsc", (128, jn * 128), F32)
        P.copy("act", o3.full(), pv, tag=f"{tag} cf", extra=prev)
        prev = [o3.full()]
        P.dma("act", kt4_block(dst_f32, tt, j0, jn), o3.full(),
              tag=f"{tag} wf")
    if dst_bf16 is not None:
        o4 = ctx.reg.get("tsd", (128, jn * 128), BF16)
        P.copy("act", o4.full(), pv, tag=f"{tag} cb", extra=prev)
        P.dma("act", kt4_block(dst_bf16, tt, j0, jn), o4.full(),
              tag=f"{tag} wb")
    if dst_bf163 is not None:
        dh, dm, dl = dst_bf163
        hi = ctx.reg.get("tsa", (128, jn * 128), BF16)
        r = ctx.reg.get("tsb", (128, jn * 128), F32)
        mid = ctx.reg.get("tsc", (128, jn * 128), BF16)
        lo = ctx.reg.get("tsd", (128, jn * 128), BF16)
        P.copy("dve", hi.full(), pv, tag=f"{tag} b3h")
        P.tt("dve", r.full(), pv, hi.full(), ALU.subtract, tag=f"{tag} b3r")
        P.copy("dve", mid.full(), r.full(), tag=f"{tag} b3m")
        P.tt("dve", lo.full(), r.full(), mid.full(), ALU.subtract,
             tag=f"{tag} b3l")
        P.dma("pool", kt4_block(dh, tt, j0, jn), hi.full(), tag=f"{tag} w3h")
        P.dma("pool", kt4_block(dm, tt, j0, jn), mid.full(), tag=f"{tag} w3m")
        P.dma("pool", kt4_block(dl, tt, j0, jn), lo.full(), tag=f"{tag} w3l")


def emit_ln(ctx, src, T, D, dst_split=None, dst_f32=None, dst_bf16=None,
            dst_bf163=None, dst_tm=None, eps=1e-5, tag="ln"):
    P = ctx.P
    nt, nk = T // 128, D // 128
    ident = ctx.sconst("ident", 128, 128)
    want_fm = (dst_split is not None or dst_f32 is not None
               or dst_bf16 is not None or dst_bf163 is not None)
    for tt in range(nt):
        xb = ctx.reg.get("lnx", (128, D), F32, off=0)
        ln = ctx.reg.get("lnx", (128, D), F32, off=D)
        sq = ctx.reg.get("lnx", (128, D), F32, off=0)
        xt = xb.full()
        P.dma("sp", xt, src[tt * 128:(tt + 1) * 128, 0:D], tag=f"{tag} x{tt}")
        s = ctx.S
        mean, var, tmp, rstd, t2 = (s[0][:, 0:1], s[1][:, 0:1], s[2][:, 0:1],
                                    s[3][:, 0:1], s[4][:, 0:1])
        P.reduce("dve", mean, xt, ALU.add, tag=f"{tag} sum")
        P.tscalar("dve", mean, mean, 1.0 / D, None, ALU.mult,
                  tag=f"{tag} mean")
        P.tscalar("dve", ln.full(), xt, mean, None, ALU.subtract,
                  tag=f"{tag} ctr")
        P.act(sq.full(), ln.full(), AF.Square, tag=f"{tag} sq")
        P.reduce("dve", var, sq.full(), ALU.add, tag=f"{tag} var")
        P.tscalar("dve", var, var, 1.0 / D, eps, ALU.mult, ALU.add,
                  tag=f"{tag} var2")
        P.act(tmp, var, AF.Sqrt, tag=f"{tag} sqrt")
        P.recip(rstd, tmp, tag=f"{tag} rs")
        P.tt("dve", t2, rstd, rstd, ALU.mult, tag=f"{tag} nw1")
        P.tt("dve", t2, t2, var, ALU.mult, tag=f"{tag} nw2")
        P.tscalar("dve", t2, t2, -0.5, 1.5, ALU.mult, ALU.add,
                  tag=f"{tag} nw3")
        P.tt("dve", rstd, rstd, t2, ALU.mult, tag=f"{tag} nw4")
        P.tscalar("dve", ln.full(), ln.full(), rstd, None, ALU.mult,
                  tag=f"{tag} norm")
        if dst_tm is not None:
            P.dma("pool", dst_tm[tt * 128:(tt + 1) * 128, 0:D], ln.full(),
                  tag=f"{tag} tm{tt}")
        if want_fm:
            for j0 in range(0, nk, 4):
                jn = min(4, nk - j0)
                ps = ctx.pb[(j0 // 4) % 2]
                lt = ln.full()
                for j in range(j0, j0 + jn):
                    col = TV(lt.ap[:, j * 128:(j + 1) * 128], lt.key, lt.p0,
                             lt.p1, lt.b0 + j * 512, lt.b0 + (j + 1) * 512)
                    P.transpose(ps[:, (j - j0) * 128:(j - j0 + 1) * 128],
                                col, ident.full(), tag=f"{tag} T{tt}.{j}")
                _fm_outs(ctx, ps[:, 0:jn * 128], tt, j0, jn, dst_split,
                         dst_f32, dst_bf16, dst_bf163, tag)


def emit_transpose_split(ctx, src_tm, T, D, dst_split=None, dst_f32=None,
                         dst_bf16=None, dst_bf163=None, src_cb=None,
                         tag="ts"):
    P = ctx.P
    nt, nk = T // 128, D // 128
    ident = ctx.sconst("ident", 128, 128)
    for tt in range(nt):
        if src_cb is not None:
            xt = src_cb(tt)
        else:
            xb = ctx.reg.get("lnx", (128, D), F32, off=(tt % 2) * D)
            P.dma("sp", xb.full(), src_tm[tt * 128:(tt + 1) * 128, 0:D],
                  tag=f"{tag} r{tt}")
            xt = xb.full()
        for j0 in range(0, nk, 4):
            jn = min(4, nk - j0)
            ps = ctx.pb[(j0 // 4) % 2]
            for j in range(j0, j0 + jn):
                col = TV(xt.ap[:, j * 128:(j + 1) * 128], xt.key, xt.p0,
                         xt.p1, xt.b0 + j * 512, xt.b0 + (j + 1) * 512)
                P.transpose(ps[:, (j - j0) * 128:(j - j0 + 1) * 128], col,
                            ident.full(), tag=f"{tag} T{tt}.{j}")
            _fm_outs(ctx, ps[:, 0:jn * 128], tt, j0, jn, dst_split, dst_f32,
                     dst_bf16, dst_bf163, tag)


def res_add_cb(ctx, xres, out_dram, T, D, nslab=512, post=None, tag="res"):
    P = ctx.P

    def cb(ps, tt, ns):
        xt = ctx.reg.get(f"rex{tt % 2}", (128, nslab), F32)
        ot = ctx.reg.get(f"reo{tt % 2}", (128, nslab), F32)
        P.dma("act", xt.full(),
              xres[tt * 128:(tt + 1) * 128, ns * nslab:(ns + 1) * nslab],
              tag=f"{tag} rx")
        src = ps
        if post is not None:
            src = post(ps, tt, ns)
        P.tt("dve", ot.full(), src, xt.full(), ALU.add, tag=f"{tag} add")
        P.dma("pool", out_dram[tt * 128:(tt + 1) * 128,
                               ns * nslab:(ns + 1) * nslab], ot.full(),
              tag=f"{tag} wr")
    return cb



def emit_l0_attn(ctx, xres, out, w, T=2048, D=2048, H=16, nslab=512,
                 tag="l0a"):
    P, reg = ctx.P, ctx.reg
    nt, nk, nh = T // 128, D // 128, H
    assert D // H == 128
    sc = ctx.scratch

    xln = sc("s_x1", kt4_shape(T, D), F32)
    emit_ln(ctx, xres, T, D, dst_f32=xln, tag=tag + ".ln")

    qtm = sc("s_t1", (T, D), F32)
    ktm = sc("s_t2", (T, D), F32)
    v1tm = sc("s_t3", (T, D), F32R)
    v2tm = sc("s_t4", (T, D), F32R)

    def plain_cb(dst):
        def cb(ps, tt, ns):
            o = reg.get(f"reo{tt % 2}", (128, nslab), F32)
            P.copy("dve", o.full(), ps, tag=f"{tag} qkv evac")
            P.dma("pool", dst[tt * 128:(tt + 1) * 128,
                              ns * nslab:(ns + 1) * nslab], o.full(),
                  tag=f"{tag} qkv wr")
        return cb

    def vsplit_cb(ps, tt, ns):
        o1 = reg.get("reo0", (128, nslab), F32R)
        o2 = reg.get("reo1", (128, nslab), F32R)
        P.copy("dve", o1.full(), ps, tag=f"{tag} v s1")
        P.tt("dve", o2.full(), ps, o1.full(), ALU.subtract, tag=f"{tag} v s2")
        P.dma("pool", v1tm[tt * 128:(tt + 1) * 128,
                           ns * nslab:(ns + 1) * nslab], o1.full(),
              tag=f"{tag} v w1")
        P.dma("pool", v2tm[tt * 128:(tt + 1) * 128,
                           ns * nslab:(ns + 1) * nslab], o2.full(),
              tag=f"{tag} v w2")

    g3 = dict(wprep="f32r_split", nslab=nslab)
    emit_gemm(ctx, xln, w["wq"], plain_cb(qtm), T, D, D, tag=tag + ".q",
              **g3)
    emit_gemm(ctx, xln, w["wk"], plain_cb(ktm), T, D, D, tag=tag + ".k",
              **g3)
    emit_gemm(ctx, xln, w["wv"], vsplit_cb, T, D, D, tag=tag + ".v",
              **g3)

    q1 = sc("s_q1", kt4_shape(T, D), F32R)
    q2 = sc("s_q2", kt4_shape(T, D), F32R)
    k1 = sc("s_k1", kt4_shape(T, D), F32R)
    k2 = sc("s_k2", kt4_shape(T, D), F32R)
    emit_transpose_split(ctx, qtm, T, D, dst_split=(q1, q2), tag=tag + ".tq")
    emit_transpose_split(ctx, ktm, T, D, dst_split=(k1, k2), tag=tag + ".tk")

    ao = sc("s_a1", kt4_shape(T, D), F32)
    mtri = ctx.sconst("mask_triu", 128, 128)
    ones_row = ctx.sconst("ones_row", 1, 512)
    onesc = reg.get("gwR", (128, 1), F32R, off=11264)
    _o1f = reg.get("msc1", (128, 1), F32, off=2047)
    P.memset("pool", _o1f.full(), 1.0, tag=f"{tag} ones0")
    P.copy("dve", onesc.full(), _o1f.full(), tag=f"{tag} ones")
    inv_sqrt_T = 1.0 / float(np.sqrt(T))
    nts = T // 512

    for h in range(nh):
        kb1 = reg.get("gwR", (128, nt, 128), F32R, off=0)
        kb2 = reg.get("gwR", (128, nt, 128), F32R, off=nt * 128)
        vb1 = reg.get("gwR", (128, nt, 128), F32R, off=2 * nt * 128)
        vb2 = reg.get("gwR", (128, nt, 128), F32R, off=3 * nt * 128)
        P.dma("sp", kb1.full(),
              TV(k1.base[:, :, h, :].rearrange("t p w -> p t w"), k1.key,
                 0, 1, 0, k1.free_total * nt * 4), tag=f"{tag} kld1.{h}")
        P.dma("sp", kb2.full(),
              TV(k2.base[:, :, h, :].rearrange("t p w -> p t w"), k2.key,
                 0, 1, 0, k2.free_total * nt * 4), tag=f"{tag} kld2.{h}")
        P.dma("sp", vb1.full(),
              TV(v1tm.base.rearrange("(t p) (c w) -> c p t w", p=128, w=128)
                 [h], v1tm.key, 0, 1, 0, v1tm.free_total * T * 4),
              tag=f"{tag} vld1.{h}")
        P.dma("sp", vb2.full(),
              TV(v2tm.base.rearrange("(t p) (c w) -> c p t w", p=128, w=128)
                 [h], v2tm.key, 0, 1, 0, v2tm.free_total * T * 4),
              tag=f"{tag} vld2.{h}")

        vs_ps = ctx.pb[6][0:1, 0:128]
        pairs = []
        for tk in range(nt):
            pairs.append((onesc.full(), vb1[:, tk, :]))
            pairs.append((onesc.full(), vb2[:, tk, :]))
        P.mm_group(pairs, vs_ps, tag=f"{tag} vsum.{h}")
        vs_row = reg.get("msc1", (1, 128), F32, off=0)
        P.copy("dve", vs_row.full(), vs_ps, tag=f"{tag} vsumc.{h}")

        for ts in range(nts):
            qb1 = reg.get("gwR", (128, 4, 128), F32R, off=8192)
            qb2 = reg.get("gwR", (128, 4, 128), F32R, off=8704)
            P.dma("sp", qb1.full(),
                  TV(q1.base[4 * ts:4 * ts + 4, :, h, :]
                     .rearrange("t p w -> p t w"), q1.key, 0, 1, 0,
                     q1.free_total * nt * 4), tag=f"{tag} qld1.{h}.{ts}")
            P.dma("sp", qb2.full(),
                  TV(q2.base[4 * ts:4 * ts + 4, :, h, :]
                     .rearrange("t p w -> p t w"), q2.key, 0, 1, 0,
                     q2.free_total * nt * 4), tag=f"{tag} qld2.{h}.{ts}")
            q1s = TV(qb1.full().ap.rearrange("p t w -> p (t w)"), "rg_gwR",
                     0, 128, 8192 * 4, 8192 * 4 + 2048)
            q2s = TV(qb2.full().ap.rearrange("p t w -> p (t w)"), "rg_gwR",
                     0, 128, 8704 * 4, 8704 * 4 + 2048)
            nps = ctx.pb[2][:, 0:512]
            dps = ctx.pb[3][0:1, 0:512]
            ntk = 4 * ts + 4

            def emit_scores(tk):
                sps = ctx.pb[4 + tk % 3][:, 0:512]
                P.mm_group([(kb1[:, tk, :], q1s),
                            (kb2[:, tk, :], q1s),
                            (kb1[:, tk, :], q2s)], sps,
                           tag=f"{tag} sc.{h}.{ts}.{tk}")
                dcol = tk - 4 * ts
                if dcol >= 0:
                    dtv = TV(sps.ap[:, dcol * 128:(dcol + 1) * 128],
                             sps.key, 0, 128, sps.b0 + dcol * 512,
                             sps.b0 + (dcol + 1) * 512)
                    P.tt("dve", dtv, dtv, mtri.full(), ALU.mult,
                         tag=f"{tag} msk")
                    if dcol > 0:
                        P.memset("dve", TV(sps.ap[:, 0:dcol * 128], sps.key,
                                           0, 128, sps.b0,
                                           sps.b0 + dcol * 512), 0.0,
                                 tag=f"{tag} z")
                base = (tk % 2) * 1024
                eb = reg.get("lnx", (128, 512), F32, off=base)
                ef = reg.get("lnx", (128, 512), F32, off=base + 512)
                e2 = reg.get("gwR", (128, 512), F32R,
                             off=9216 + (tk % 2) * 1024)
                e1 = reg.get("gwR", (128, 512), F32R,
                             off=9216 + (tk % 2) * 1024 + 512)
                P.act(eb.full(), sps, AF.Exp, scale=inv_sqrt_T,
                      tag=f"{tag} exp")
                P.act(ef.full(), eb.full(), AF.Identity, bias=-1.0,
                      tag=f"{tag} em1")
                P.copy("act", e2.full(), ef.full(), tag=f"{tag} es1")
                P.tt("dve", e1.full(), ef.full(), e2.full(), ALU.subtract,
                     tag=f"{tag} es2")
                return e1, e2

            epipe = [emit_scores(0)]
            for tk in range(ntk):
                if tk + 1 < ntk:
                    epipe.append(emit_scores(tk + 1))
                e1, e2 = epipe[tk]
                P.mm_group([(vb1[:, tk, :], e2.full()),
                            (vb2[:, tk, :], e2.full()),
                            (vb1[:, tk, :], e1.full())], nps,
                           start=(tk == 0), stop=False,
                           tag=f"{tag} av.{h}.{ts}.{tk}")
                P.mm_group([(onesc.full(), e2.full()),
                            (onesc.full(), e1.full())], dps,
                           start=(tk == 0), stop=(tk == ntk - 1),
                           tag=f"{tag} dn.{h}.{ts}.{tk}")
            P.mm_group([(vs_row.full(), ones_row.full())], nps,
                       start=False, stop=True, tag=f"{tag} vadd.{h}.{ts}")
            dn = reg.get("msc1", (1, 512), F32, off=256)
            P.tscalar("dve", dn.full(), dps, float(T), None, ALU.add,
                      tag=f"{tag} dT")
            P.recip(dn.full(), dn.full(), tag=f"{tag} rec")
            rb_ps = ctx.pb[7][:, 0:512]
            ones_r1 = reg.get("msc1", (1, 128), F32, off=1024)
            P.memset("pool", ones_r1.full(), 1.0, tag=f"{tag} o1")
            P.mm_group([(ones_r1.full(), dn.full())], rb_ps,
                       tag=f"{tag} rb.{h}.{ts}")
            rb = reg.get("lnx", (128, 512), F32, off=2048)
            P.copy("act", rb.full(), rb_ps, tag=f"{tag} rbc")
            att = reg.get("lnx", (128, 512), F32, off=2560)
            P.tt("dve", att.full(), nps, rb.full(), ALU.mult,
                 tag=f"{tag} norm")
            P.dma("pool",
                  TV(ao.base[4 * ts:4 * ts + 4, :, h, :]
                     .rearrange("t p w -> p t w"), ao.key, 0, 1, 0,
                     ao.free_total * nt * 4),
                  TV(att.full().ap.rearrange("p (t w) -> p t w", w=128),
                     att.full().key, 0, 128, att.full().b0, att.full().b1),
                  tag=f"{tag} aw1")

    emit_gemm(ctx, ao, w["wo"],
              res_add_cb(ctx, xres, out, T, D, nslab=nslab),
              T, D, D, wprep="f32r_split", nslab=nslab, tag=tag + ".o")



def build_l0a(T=2048, D=2048, H=16):
    nc = bass.Bass("TRN2", target_bir_lowering=False)
    P = Prog(nc)
    x_in = P.dram("x", (T, D), F32, kind="ExternalInput")
    y_out = P.dram("y", (T, D), F32, kind="ExternalOutput")
    ns = min(512, D)
    w = {k: P.dram(k, wslab_shape(D, D, ns), F32, kind="ExternalInput")
         for k in ("wq", "wk", "wv", "wo")}
    ctx = Ctx(P, std_consts(T))
    emit_l0_attn(ctx, x_in, y_out, w, T, D, H, nslab=ns)
    P.emit()
    return nc, ctx, P



def _bcast16(tv, groups):
    ap = tv.ap.unsqueeze(2).broadcast_to([tv.ap.shape[0], groups, 16])
    return TV(ap, tv.key, tv.p0, tv.p1, tv.b0, tv.b1)


def emit_e4m3fn_round(ctx, eng, out_tv, in_tv, tmp1, tmp2, tag):
    P = ctx.P
    P.tscalar(eng, out_tv, in_tv, 448.0, None, ALU.min, tag=tag + " clip")
    P.tscalar(eng, tmp1.i32(), out_tv.i32(), EXPMASK, None, ALU.bitwise_and,
              tag=tag + " eb")
    P.tscalar(eng, tmp1, tmp1, 0.125, float(2.0 ** -9), ALU.mult, ALU.max,
              tag=tag + " st")
    P.recip(tmp2, tmp1, tag=tag + " rc")
    P.tt(eng, tmp2, out_tv, tmp2, ALU.mult, tag=tag + " dv")
    P.tscalar(eng, tmp2, tmp2, MAGIC, MAGIC, ALU.add, ALU.subtract,
              tag=tag + " rn")
    P.tt(eng, out_tv, tmp2, tmp1, ALU.mult, tag=tag + " ml")


def emit_e2m1_round(ctx, out_tv, in_tv, t, tag):
    P = ctx.P
    a, rlo, rmid, c2 = t[0], t[1], t[2], t[3]
    sgn = out_tv
    P.tscalar("dve", a.i32(), in_tv.i32(), 0x7FFFFFFF, None,
              ALU.bitwise_and, tag=tag + " ab")
    P.tscalar("dve", sgn.i32(), in_tv.i32(), -2147483648, None,
              ALU.bitwise_and, tag=tag + " sg")
    P.tscalar("pool", a, a, 6.0, None, ALU.min, tag=tag + " cl")
    P.tscalar("dve", rlo, a, 2.0, MAGIC, ALU.mult, ALU.add, tag=tag + " l1")
    P.tscalar("dve", rlo, rlo, MAGIC, 0.5, ALU.subtract, ALU.mult,
              tag=tag + " l2")
    P.tscalar("pool", rmid, a, MAGIC, MAGIC, ALU.add, ALU.subtract,
              tag=tag + " m1")
    P.tscalar("dve", c2, a, 2.0, None, ALU.is_ge, tag=tag + " c2")
    P.tt("dve", rmid, rmid, rlo, ALU.subtract, tag=tag + " d1")
    P.tt("dve", rmid, rmid, c2, ALU.mult, tag=tag + " d2")
    P.tt("dve", rlo, rlo, rmid, ALU.add, tag=tag + " d3")
    P.tscalar("pool", rmid, a, 0.5, MAGIC, ALU.mult, ALU.add, tag=tag + " h1")
    P.tscalar("pool", rmid, rmid, MAGIC, 2.0, ALU.subtract, ALU.mult,
              tag=tag + " h2")
    P.tscalar("dve", c2, a, 4.0, None, ALU.is_ge, tag=tag + " c4")
    P.tt("dve", rmid, rmid, rlo, ALU.subtract, tag=tag + " e1")
    P.tt("dve", rmid, rmid, c2, ALU.mult, tag=tag + " e2")
    P.tt("dve", rlo, rlo, rmid, ALU.add, tag=tag + " e3")
    P.tt("dve", out_tv.i32(), rlo.i32(), sgn.i32(), ALU.bitwise_or,
         tag=tag + " or")


def emit_nvfp4_quant_chunk(ctx, dq_out_bf16, src_tv, K, recip6ts_col,
                           recip_ts_col, tag):
    P, reg = ctx.P, ctx.reg
    G = K // 16
    am = reg.get("msc0", (128, G), F32, off=0)
    t1 = reg.get("msc0", (128, G), F32, off=G)
    t2 = reg.get("msc0", (128, G), F32, off=2 * G)
    P.reduce("dve", am.full(), TV(src_tv.ap.rearrange("p (g k) -> p g k",
                                                      k=16), src_tv.key,
                                  src_tv.p0, src_tv.p1, src_tv.b0,
                                  src_tv.b1),
             ALU.max, absval=True, tag=tag + " am")
    P.tscalar("dve", am.full(), am.full(), recip6ts_col, None, ALU.mult,
              tag=tag + " bs")
    emit_e4m3fn_round(ctx, "dve", am.full(), am.full(), t1.full(), t2.full(),
                      tag + " e4")
    P.tscalar("dve", t1.full(), am.full(), 0.0, None, ALU.is_equal,
              tag=tag + " eq")
    P.tt("dve", t1.full(), t1.full(), am.full(), ALU.add, tag=tag + " de")
    P.recip(t1.full(), t1.full(), tag=tag + " rd")
    P.tscalar("dve", t1.full(), t1.full(), recip_ts_col, None, ALU.mult,
              tag=tag + " rt")
    y = reg.get("msc1", (128, K), F32, off=0)
    yv = TV(y.full().ap.rearrange("p (g k) -> p g k", k=16), y.full().key,
            0, 128, y.full().b0, y.full().b1)
    sv = TV(src_tv.ap.rearrange("p (g k) -> p g k", k=16), src_tv.key,
            src_tv.p0, src_tv.p1, src_tv.b0, src_tv.b1)
    P.tt("dve", yv, sv, _bcast16(t1.full(), G), ALU.mult, tag=tag + " sc")
    scr = [TV(ctx.reg.get("gxB", (128, K), F32, off=i * K).full().ap,
              "rg_gxB", 0, 128, i * K * 4, (i + 1) * K * 4)
           for i in range(3)]
    scr.append(TV(ctx.reg.get("lnx", (128, K), F32, off=0).full().ap,
                  "rg_lnx", 0, 128, 0, K * 4))
    emit_e2m1_round(ctx, y.full(), y.full(), scr, tag + " e2")
    ov = TV(dq_out_bf16.ap.rearrange("p (g k) -> p g k", k=16),
            dq_out_bf16.key, dq_out_bf16.p0, dq_out_bf16.p1,
            dq_out_bf16.b0, dq_out_bf16.b1)
    P.tt("dve", ov, yv, _bcast16(am.full(), G), ALU.mult, tag=tag + " dq")

def emit_l0_ffn(ctx, xres, out, wdram, bias_dram, xts_dram, T=2048, D=2048,
                nslab=512, tag="l0f"):
    P, reg = ctx.P, ctx.reg
    N = K = D
    nt, nk = T // 128, K // 128
    sc = ctx.scratch
    xln_tm = sc("s_t1", (T, D), F32)
    emit_ln(ctx, xres, T, D, dst_tm=xln_tm, tag=tag + ".ln")

    S = ctx.S
    wmax = S[5][:, 0:1]
    P.memset("dve", wmax, 0.0, tag=f"{tag} wm0")
    for c in range(N // 128):
        wt = reg.get("msc1", (128, K), F32, off=0)
        P.dma("sp", wt.full(), wdram[c * 128:(c + 1) * 128, :],
              tag=f"{tag} WA{c}")
        cm = S[6][:, 0:1]
        P.reduce("dve", cm, wt.full(), ALU.max, absval=True,
                 tag=f"{tag} wmr{c}")
        P.tt("dve", wmax, wmax, cm, ALU.max, tag=f"{tag} wmm{c}")
    ident = ctx.sconst("ident", 128, 128)
    pad = reg.get("lnx", (128, 128), F32, off=0)
    P.memset("dve", pad.full(), 0.0, tag=f"{tag} pz")
    P.copy("dve", pad[:, 0:1], wmax, tag=f"{tag} pc")
    tps = ctx.pb[0][:, 0:128]
    P.transpose(tps, pad.full(), ident.full(), tag=f"{tag} pt")
    row = ctx.R[0][0:1, 0:128]
    P.copy("dve", row, TV(tps.ap[0:1, :], tps.key, 0, 1, tps.b0,
                          tps.b0 + 512), tag=f"{tag} rc")
    wts = S[5][0:1, 1:2]
    P.reduce("dve", wts, row, ALU.max, tag=f"{tag} wts")
    P.tscalar("dve", wts, wts, 1.0 / 2688.0, None, ALU.mult,
              tag=f"{tag} ts")
    onesr = ctx.R[1][0:1, 0:128]
    P.memset("dve", onesr, 1.0, tag=f"{tag} o")
    s6 = S[5][0:1, 2:3]
    P.tscalar("dve", s6, wts, 6.0, None, ALU.mult, tag=f"{tag} s6")
    P.recip(s6, s6, tag=f"{tag} s6r")
    rts = S[5][0:1, 3:4]
    P.recip(rts, wts, tag=f"{tag} rts")
    P.dma("sp", S[6][0:1, 0:1], xts_dram[0:1, 0:1], tag=f"{tag} xts")
    x6 = S[6][0:1, 1:2]
    P.tscalar("dve", x6, S[6][0:1, 0:1], 6.0, None, ALU.mult,
              tag=f"{tag} x6")
    P.recip(x6, x6, tag=f"{tag} x6r")
    xrts = S[6][0:1, 2:3]
    P.recip(xrts, S[6][0:1, 0:1], tag=f"{tag} xrts")
    osc = S[6][0:1, 3:4]
    P.tt("dve", osc, wts, S[6][0:1, 0:1], ALU.mult, tag=f"{tag} osc")

    def bcast_col(dst_col, src11, bank, tg):
        ps = ctx.pb[bank][:, 0:1]
        P.mm_group([(onesr, src11)], ps, tag=tg)
        P.copy("dve", dst_col, ps, tag=tg + "c")
    rw6c = S[7][:, 0:1]
    rwtc = S[7][:, 1:2]
    rx6c = S[7][:, 2:3]
    rxtc = S[7][:, 3:4]
    oscc = S[7][:, 4:5]
    bcast_col(rw6c, s6, 1, f"{tag} b1")
    bcast_col(rwtc, rts, 1, f"{tag} b2")
    bcast_col(rx6c, x6, 1, f"{tag} b3")
    bcast_col(rxtc, xrts, 1, f"{tag} b4")
    bcast_col(oscc, osc, 1, f"{tag} b5")

    bias_tvs = []
    for i in range(N // nslab):
        if i < 2:
            tvb = ctx.R[2 + i][0:1, 0:nslab]
        else:
            tvb = reg.get("msc0", (1, nslab), F32,
                          off=512 * (i - 1)).full()
        P.dma("sp", tvb, bias_dram[i * nslab:(i + 1) * nslab],
              tag=f"{tag} bld{i}")
        bias_tvs.append(tvb)

    def bias_row_tv(ns):
        return bias_tvs[ns]

    identb = ctx.sconst("ident_bf", 128, 128, dtype=BF16)
    wdqT = sc("s_wq", wslab_shape(K, N, nslab), BF16)
    for c in range(N // 128):
        wt = reg.get("msc1", (128, K), F32, off=0)
        P.dma("sp", wt.full(), wdram[c * 128:(c + 1) * 128, :],
              tag=f"{tag} WB{c}")
        dq = reg.get("gwB", (128, K), BF16, off=0)
        emit_nvfp4_quant_chunk(ctx, dq.full(), wt.full(), K, rw6c, rwtc,
                               f"{tag} wq{c}")
        for j0 in range(0, nk, 4):
            jn = min(4, nk - j0)
            psb = ctx.pb[2 + (j0 // 4) % 2].cast(BF16)
            for j in range(j0, j0 + jn):
                P.transpose(psb[:, (j - j0) * 128:(j - j0 + 1) * 128],
                            TV(dq.full().ap[:, j * 128:(j + 1) * 128],
                               "rg_gwB", 0, 128, j * 256, (j + 1) * 256),
                            identb.full(), tag=f"{tag} wT{c}.{j}")
            ob = reg.get("gwB", (128, jn * 128), BF16, off=4096)
            P.copy("dve", ob.full(), psb[:, 0:jn * 128], tag=f"{tag} wE")
            dst = TV(wdqT.base[c // 4, :, j0:j0 + jn,
                               (c % 4) * 128:(c % 4 + 1) * 128],
                     wdqT.key, 0, 1, 0, wdqT.free_total * 8)
            src = TV(ob.full().ap.rearrange("p (c t) -> p c t", t=128),
                     ob.full().key, 0, 128, ob.full().b0, ob.full().b1)
            P.dma("pool", dst, src, tag=f"{tag} wW{c}.{j0}")

    xdqT = sc("s_xq", kt4_shape(T, K), BF16)
    for tt in range(nt):
        xt = reg.get("msc1", (128, D), F32, off=0)
        P.dma("sp", xt.full(), xln_tm[tt * 128:(tt + 1) * 128, :],
              tag=f"{tag} xr{tt}")
        dq = reg.get("gwB", (128, K), BF16, off=0)
        emit_nvfp4_quant_chunk(ctx, dq.full(), xt.full(), K, rx6c, rxtc,
                               f"{tag} xq{tt}")
        for j0 in range(0, nk, 4):
            jn = min(4, nk - j0)
            psb = ctx.pb[2 + (j0 // 4) % 2].cast(BF16)
            for j in range(j0, j0 + jn):
                P.transpose(psb[:, (j - j0) * 128:(j - j0 + 1) * 128],
                            TV(dq.full().ap[:, j * 128:(j + 1) * 128],
                               "rg_gwB", 0, 128, j * 256, (j + 1) * 256),
                            identb.full(), tag=f"{tag} xT{tt}.{j}")
            ob = reg.get("gwB", (128, jn * 128), BF16, off=4096)
            P.copy("dve", ob.full(), psb[:, 0:jn * 128], tag=f"{tag} xE")
            P.dma("pool", kt4_block(xdqT, tt, j0, jn), ob.full(),
                  tag=f"{tag} xW{tt}.{j0}")

    def post(ps, tt, ns):
        o = reg.get("tsa", (128, nslab), F32)
        bb = ctx.pb[(tt % 2)][:, 0:nslab]
        P.mm_group([(onesr, bias_row_tv(ns))], bb, tag=f"{tag} bb{tt}.{ns}")
        bs = reg.get("tsb", (128, nslab), F32)
        P.copy("act", bs.full(), bb, tag=f"{tag} bbc")
        P.stt("dve", o.full(), ps, oscc, bs.full(), ALU.mult, ALU.add,
              tag=f"{tag} ep")
        return o.full()

    emit_gemm(ctx, xdqT, wdqT,
              res_add_cb(ctx, xres, out, T, D, nslab=nslab, post=post),
              T, K, N, wprep="raw", nslab=nslab, tag=tag + ".mm")



def emit_l1_attn(ctx, xres, out, w, T=2048, D=2048, H=16, nslab=512,
                 tag="l1a"):
    P, reg = ctx.P, ctx.reg
    nt, nh = T // 128, H
    D_ = D
    eps = 1e-6
    sc = ctx.scratch
    xln = sc("s_x1", kt4_shape(T, D), F32)
    emit_ln(ctx, xres, T, D, dst_f32=xln, tag=tag + ".ln")

    dec = reg.get("msc0", (128, nt), F32, off=1024)
    P.dma("sp", dec.full(), ctx.consts["decay"].full(),
          tag=f"{tag} dec")
    onesr = ctx.R[1][0:1, 0:128]
    P.memset("dve", onesr, 1.0, tag=f"{tag} o")

    bias_tv = {}

    def load_bias(nm):
        tvs = []
        for i in range(D // nslab):
            rb = reg.get("msc1", (1, nslab), F32, off=i * nslab)
            P.dma("sp", rb.full(), w[nm][i * nslab:(i + 1) * nslab],
                  tag=f"{tag} b{nm}{i}")
            tvs.append(rb.full())
        bias_tv[nm] = tvs

    qtm = sc("s_t1", (T, D), F32)
    k1tm = sc("s_t3", (T, D), F32R)
    k2tm = sc("s_t4", (T, D), F32R)
    v1tm = sc("s_t5", (T, D), F32R)
    v2tm = sc("s_t6", (T, D), F32R)

    def bias_pairs(nm):
        def ep(tt, ns):
            return [(onesr, bias_tv[nm][ns])]
        return ep

    def q_cb(ps, tt, ns):
        o = reg.get(f"reo{tt % 2}", (128, nslab), F32)
        P.act(o.full(), ps, AF.Relu, tag=f"{tag} qrelu")
        P.tscalar("dve", o.full(), o.full(), dec[:, tt:tt + 1], None,
                  ALU.mult, tag=f"{tag} qdec")
        P.dma("pool", qtm[tt * 128:(tt + 1) * 128,
                          ns * nslab:(ns + 1) * nslab], o.full(),
              tag=f"{tag} qw")

    def kv_cb(d1tm, d2tm, relu):
        def cb(ps, tt, ns):
            o = reg.get("tsc", (128, nslab), F32)
            if relu:
                P.act(o.full(), ps, AF.Relu, tag=f"{tag} krelu")
                P.tscalar("dve", o.full(), o.full(), dec[:, tt:tt + 1],
                          None, ALU.mult, tag=f"{tag} kdec")
            else:
                P.copy("act", o.full(), ps, tag=f"{tag} vev")
            o1 = reg.get("reo0", (128, nslab), F32R)
            o2 = reg.get("reo1", (128, nslab), F32R)
            P.copy("dve", o1.full(), o.full(), tag=f"{tag} s1")
            P.tt("dve", o2.full(), o.full(), o1.full(), ALU.subtract,
                 tag=f"{tag} s2")
            P.dma("pool", d1tm[tt * 128:(tt + 1) * 128,
                               ns * nslab:(ns + 1) * nslab], o1.full(),
                  tag=f"{tag} w1")
            P.dma("pool", d2tm[tt * 128:(tt + 1) * 128,
                               ns * nslab:(ns + 1) * nslab], o2.full(),
                  tag=f"{tag} w2")
        return cb

    g3 = dict(wprep="f32r_split", nslab=nslab)
    load_bias("bq")
    emit_gemm(ctx, xln, w["wq"], q_cb, T, D, D, tag=tag + ".q",
              extra_pairs=bias_pairs("bq"), **g3)
    load_bias("bk")
    emit_gemm(ctx, xln, w["wk"], kv_cb(k1tm, k2tm, True), T, D, D,
              tag=tag + ".k", extra_pairs=bias_pairs("bk"), **g3)
    load_bias("bv")
    emit_gemm(ctx, xln, w["wv"], kv_cb(v1tm, v2tm, False), T, D, D,
              tag=tag + ".v", extra_pairs=bias_pairs("bv"), **g3)

    q1 = sc("s_q1", kt4_shape(T, D), F32R)
    q2 = sc("s_q2", kt4_shape(T, D), F32R)
    emit_transpose_split(ctx, qtm, T, D, dst_split=(q1, q2), tag=tag + ".tq")

    ao = sc("s_t2", (T, D), F32)
    onesc = reg.get("gwR", (128, 2), F32R, off=9100)
    _o1f = ctx.S[6][:, 8:10]
    P.memset("dve", _o1f, 1.0, tag=f"{tag} oc0")
    P.copy("dve", onesc.full(), _o1f, tag=f"{tag} oc")
    for h in range(nh):
        kb1 = reg.get("gwR", (128, nt, 128), F32R, off=0)
        kb2 = reg.get("gwR", (128, nt, 128), F32R, off=nt * 128)
        vb1 = reg.get("gwR", (128, nt, 128), F32R, off=2 * nt * 128)
        vb2 = reg.get("gwR", (128, nt, 128), F32R, off=3 * nt * 128)
        for (b, tm) in ((kb1, k1tm), (kb2, k2tm), (vb1, v1tm), (vb2, v2tm)):
            P.dma("sp", b.full(),
                  TV(tm.base.rearrange("(t p) (c w) -> c p t w", p=128,
                                       w=128)[h], tm.key, 0, 1, 0,
                     tm.free_total * T * 4), tag=f"{tag} ld{h}")
        kv_p = ctx.pb[2][:, 0:128]
        ks_p = ctx.pb[3][:, 0:2]
        pairs_kv = []
        pairs_ks = []
        for tk in range(nt):
            pairs_kv += [(kb1[:, tk, :], vb1[:, tk, :]),
                         (kb2[:, tk, :], vb1[:, tk, :]),
                         (kb1[:, tk, :], vb2[:, tk, :])]
            pairs_ks += [(kb1[:, tk, :], onesc.full()),
                         (kb2[:, tk, :], onesc.full())]
        P.mm_group(pairs_kv, kv_p, tag=f"{tag} kv.{h}")
        P.mm_group(pairs_ks, ks_p, tag=f"{tag} ks.{h}")
        r1 = reg.get("gwR", (128, 130), F32R, off=4 * nt * 128)
        r2 = reg.get("gwR", (128, 130), F32R, off=4 * nt * 128 + 130)
        t0 = reg.get("lnx", (128, 130), F32, off=0)
        P.memset("dve", t0[:, 128:130], 0.0, tag=f"{tag} kvz")
        P.copy("dve", t0[:, 0:128], kv_p, tag=f"{tag} kve")
        P.copy("dve", t0[:, 128:129],
               TV(ks_p.ap[:, 0:1], ks_p.key, 0, 128, ks_p.b0, ks_p.b0 + 4),
               tag=f"{tag} kse")
        P.copy("dve", r1.full(), t0.full(), tag=f"{tag} kvs1")
        P.tt("dve", r2.full(), t0.full(), r1.full(), ALU.subtract,
             tag=f"{tag} kvs2")
        for tt in range(nt):
            qb1 = reg.get("gwR", (128, 128), F32R,
                          off=4 * nt * 128 + 300 + (tt % 2) * 256)
            qb2 = reg.get("gwR", (128, 128), F32R,
                          off=4 * nt * 128 + 300 + (tt % 2) * 256 + 128)
            P.dma("sp", qb1.full(), kt4_block(q1, tt, h, 1),
                  tag=f"{tag} q1.{h}.{tt}")
            P.dma("sp", qb2.full(), kt4_block(q2, tt, h, 1),
                  tag=f"{tag} q2.{h}.{tt}")
            ops_ = ctx.pb[4 + tt % 2][:, 0:130]
            P.mm_group([(qb1.full(), r1.full()),
                        (qb2.full(), r1.full()),
                        (qb1.full(), r2.full())], ops_,
                       tag=f"{tag} out.{h}.{tt}")
            z = ctx.S[6][:, 9:10]
            P.tscalar("dve", z, TV(ops_.ap[:, 128:129], ops_.key, 0, 128,
                                   ops_.b0 + 512, ops_.b0 + 516),
                      eps, None, ALU.add, tag=f"{tag} z")
            P.recip(z, z, tag=f"{tag} zr")
            o = reg.get("lnx", (128, 128), F32, off=256 + (tt % 2) * 128)
            P.tscalar("dve", o.full(), TV(ops_.ap[:, 0:128], ops_.key, 0,
                                          128, ops_.b0, ops_.b0 + 512),
                      z, None, ALU.mult, tag=f"{tag} on")
            P.dma("pool", ao[tt * 128:(tt + 1) * 128,
                            h * 128:(h + 1) * 128], o.full(),
                  tag=f"{tag} ow.{h}.{tt}")

    aoT = sc("s_a1", kt4_shape(T, D), F32)
    emit_transpose_split(ctx, ao, T, D, dst_f32=aoT, tag=tag + ".ta")
    load_bias("bo")
    emit_gemm(ctx, aoT, w["wo"],
              res_add_cb(ctx, xres, out, T, D, nslab=nslab),
              T, D, D, wprep="f32r_split",
              nslab=nslab, extra_pairs=bias_pairs("bo"), tag=tag + ".o")



def emit_l1_ffn(ctx, xres, out, w, T=2048, D=2048, E=8, FFN=1024,
                nslab=512, tag="l1f"):
    P, reg = ctx.P, ctx.reg
    nt, nk = T // 128, D // 128
    EF = E * FFN
    sc = ctx.scratch
    xln = sc("s_x1", kt4_shape(T, D), F32)
    emit_ln(ctx, xres, T, D, dst_f32=xln, tag=tag + ".ln")

    onesr = ctx.R[1][0:1, 0:128]
    P.memset("dve", onesr, 1.0, tag=f"{tag} o")
    brow = ctx.R[2][0:1, 0:E]
    P.dma("sp", brow, w["bias"][0:E], tag=f"{tag} bias")
    bps = ctx.pb[0][:, 0:E]
    P.mm_group([(onesr, brow)], bps, tag=f"{tag} bb")
    bbc = ctx.S[6][:, 0:E]
    P.copy("dve", bbc, bps, tag=f"{tag} bbc")

    wn = reg.get("msc0", (128, nt * E), F32, off=1024)

    def r_cb(ps, tt, ns):
        s_ = ctx.S
        r = s_[5][:, 0:E]
        P.act(r, ps, AF.Sigmoid, tag=f"{tag} sg")
        sco = s_[5][:, 8:8 + E]
        P.tt("dve", sco, r, bbc, ALU.add, tag=f"{tag} sb")
        m1 = s_[6][:, 8:9]
        P.reduce("dve", m1, sco, ALU.max, tag=f"{tag} m1")
        eq = s_[7][:, 0:E]
        P.tscalar("dve", eq, sco, m1, None, ALU.is_ge, tag=f"{tag} eq")
        s2 = s_[7][:, 8:8 + E]
        P.stt("dve", s2, eq, -1e30, sco, ALU.mult, ALU.add, tag=f"{tag} s2")
        m2 = s_[6][:, 9:10]
        P.reduce("dve", m2, s2, ALU.max, tag=f"{tag} m2")
        mm_ = s_[7][:, 0:E]
        P.tscalar("dve", mm_, sco, m2, None, ALU.is_ge, tag=f"{tag} mtop")
        rm = s_[5][:, 8:8 + E]
        P.tt("dve", rm, r, mm_, ALU.mult, tag=f"{tag} rm")
        ds = s_[6][:, 8:9]
        P.reduce("dve", ds, rm, ALU.add, tag=f"{tag} ds")
        P.recip(ds, ds, tag=f"{tag} dr")
        P.tscalar("dve", wn[:, tt * E:(tt + 1) * E], rm, ds, None, ALU.mult,
                  tag=f"{tag} wn")

    emit_gemm(ctx, xln, w["wg"], r_cb, T, D, E,
              wprep="f32r_split", nslab=E, tag=tag + ".r")

    s1tm = sc("s_t1", (T, EF), F32)
    ident = ctx.sconst("ident", 128, 128)

    def h1_cb(ps, tt, ns):
        o = reg.get(f"reo{tt % 2}", (128, nslab), F32)
        P.act(o.full(), ps, AF.Silu, tag=f"{tag} silu")
        P.dma("pool", s1tm[tt * 128:(tt + 1) * 128,
                           ns * nslab:(ns + 1) * nslab], o.full(),
              tag=f"{tag} s1w")

    emit_gemm(ctx, xln, w["w1"], h1_cb, T, D, EF,
              wprep="f32r_split", nslab=nslab, tag=tag + ".h1")

    gkt = sc("s_g1", kt4_shape(T, EF), F32)
    nsl_per_e = FFN // nslab

    def h3_cb(ps, tt, ns):
        e = ns // nsl_per_e
        sl = reg.get(f"rex{tt % 2}", (128, nslab), F32)
        P.dma("act", sl.full(), s1tm[tt * 128:(tt + 1) * 128,
                                     ns * nslab:(ns + 1) * nslab],
              tag=f"{tag} s1r")
        g = reg.get("tsc", (128, nslab), F32)
        P.stt("dve", g.full(), ps, wn[:, tt * E + e:tt * E + e + 1],
              sl.full(), ALU.mult, ALU.mult, tag=f"{tag} g")
        psT = ctx.pb[(tt % 2)]
        for j in range(nslab // 128):
            P.transpose(psT[:, j * 128:(j + 1) * 128],
                        TV(g.full().ap[:, j * 128:(j + 1) * 128],
                           "rg_tsc", 0, 128, j * 512, (j + 1) * 512),
                        ident.full(), tag=f"{tag} gT")
        o1 = reg.get("tsa", (128, nslab), F32)
        pv = psT[:, 0:nslab]
        P.copy("dve", o1.full(), pv, tag=f"{tag} gs1")
        P.dma("pool", kt4_block(gkt, tt, ns * (nslab // 128), nslab // 128),
              o1.full(), tag=f"{tag} gw1")

    emit_gemm(ctx, xln, w["w3"], h3_cb, T, D, EF,
              wprep="f32r_split", nslab=nslab, tag=tag + ".h3")

    acc = sc("s_t2", (T, D), F32)
    nkg = EF // 2048
    for kg in range(nkg):
        first = (kg == 0)
        last = (kg == nkg - 1)

        if last:
            cb = res_add_acc_cb(ctx, acc, xres, out, T, D, nslab, tag)
        elif first:
            cb = store_cb(ctx, acc, T, D, nslab, tag)
        else:
            cb = accum_cb(ctx, acc, T, D, nslab, tag)
        emit_gemm(ctx, gkt, w["w2"], cb, T, 2048, D,
                  wprep="f32r_split", nslab=nslab, kgroup=(kg * 16, 16),
                  tag=tag + f".d{kg}")


def store_cb(ctx, dst, T, D, nslab, tag):
    P = ctx.P

    def cb(ps, tt, ns):
        o = ctx.reg.get(f"reo{tt % 2}", (128, nslab), F32)
        P.copy("dve", o.full(), ps, tag=f"{tag} st")
        P.dma("pool", dst[tt * 128:(tt + 1) * 128,
                          ns * nslab:(ns + 1) * nslab], o.full(),
              tag=f"{tag} stw")
    return cb


def accum_cb(ctx, acc, T, D, nslab, tag):
    P = ctx.P

    def cb(ps, tt, ns):
        a = ctx.reg.get(f"rex{tt % 2}", (128, nslab), F32)
        P.dma("act", a.full(), acc[tt * 128:(tt + 1) * 128,
                                   ns * nslab:(ns + 1) * nslab],
              tag=f"{tag} accr")
        o = ctx.reg.get(f"reo{tt % 2}", (128, nslab), F32)
        P.tt("dve", o.full(), ps, a.full(), ALU.add, tag=f"{tag} acca")
        P.dma("pool", acc[tt * 128:(tt + 1) * 128,
                          ns * nslab:(ns + 1) * nslab], o.full(),
              tag=f"{tag} accw")
    return cb


def res_add_acc_cb(ctx, acc, xres, out, T, D, nslab, tag):
    P = ctx.P

    def cb(ps, tt, ns):
        a = ctx.reg.get(f"rex{tt % 2}", (128, nslab), F32)
        P.dma("act", a.full(), acc[tt * 128:(tt + 1) * 128,
                                   ns * nslab:(ns + 1) * nslab],
              tag=f"{tag} fr1")
        xv = ctx.reg.get("tsd", (128, nslab), F32)
        P.dma("act", xv.full(), xres[tt * 128:(tt + 1) * 128,
                                     ns * nslab:(ns + 1) * nslab],
              tag=f"{tag} fr2")
        o = ctx.reg.get(f"reo{tt % 2}", (128, nslab), F32)
        P.tt("dve", o.full(), ps, a.full(), ALU.add, tag=f"{tag} fa1")
        P.tt("dve", o.full(), o.full(), xv.full(), ALU.add, tag=f"{tag} fa2")
        P.dma("pool", out[tt * 128:(tt + 1) * 128,
                          ns * nslab:(ns + 1) * nslab], o.full(),
              tag=f"{tag} fw")
    return cb



def emit_rope_tile(ctx, o_buf, nheads, cosb, sinb, tag):
    P = ctx.P
    W = nheads * 64

    def half(i):
        ap = o_buf.full().ap.rearrange("p (h two d) -> p h two d", two=2,
                                       d=64)[:, :, i, :]
        return TV(ap, o_buf.full().key, 0, 128, o_buf.full().b0,
                  o_buf.full().b1)

    def cbc(ctv):
        ap = ctv.ap.unsqueeze(1).broadcast_to([128, nheads, 64])
        return TV(ap, ctv.key, ctv.p0, ctv.p1, ctv.b0, ctv.b1)

    q1, q2 = half(0), half(1)
    a = ctx.reg.get("tsc", (128, W), F32)
    d = ctx.reg.get("tsd", (128, W), F32)

    def v3(buf):
        ap = buf.full().ap.rearrange("p (h d) -> p h d", d=64)
        return TV(ap, buf.full().key, 0, 128, buf.full().b0, buf.full().b1)

    P.tt("dve", v3(a), q1, cbc(cosb), ALU.mult, tag=tag + " r1")
    P.tt("pool", v3(d), q1, cbc(sinb), ALU.mult, tag=tag + " r2")
    t2 = ctx.reg.get("rex0", (128, W), F32)
    P.tt("dve", v3(t2), q2, cbc(sinb), ALU.mult, tag=tag + " r3")
    P.tt("dve", v3(a), v3(a), v3(t2), ALU.subtract, tag=tag + " r4")
    P.tt("pool", v3(t2), q2, cbc(cosb), ALU.mult, tag=tag + " r5")
    P.tt("pool", v3(d), v3(t2), v3(d), ALU.add, tag=tag + " r6")
    P.copy("dve", q1, v3(a), tag=tag + " r7")
    P.copy("pool", q2, v3(d), tag=tag + " r8")


def emit_l2_attn(ctx, xres, out, w, T=2048, D=2048, H=16, KV=4, WIN=512,
                 nslab=512, tag="l2a"):
    P, reg = ctx.P, ctx.reg
    nt, nh = T // 128, H
    dh = 128
    Dkv = KV * dh
    nrep = H // KV
    sc = ctx.scratch
    xln = sc("s_x1", kt4_shape(T, D), F32)
    emit_ln(ctx, xres, T, D, dst_f32=xln, tag=tag + ".ln")

    qtm = sc("s_t1", (T, D), F32)
    ktm = sc("s_t2", (T, Dkv), F32)
    v1tm = sc("s_t3", (T, Dkv), F32R)
    v2tm = sc("s_t4", (T, Dkv), F32R)

    def rope_cb(dst, nhd, nsl):
        def cb(ps, tt, ns):
            o = reg.get(f"reo{tt % 2}", (128, nsl), F32)
            P.copy("dve", o.full(), ps, tag=f"{tag} ev")
            cosb = reg.get("msc0", (128, 64), F32, off=0)
            sinb = reg.get("msc0", (128, 64), F32, off=64)
            P.dma("sp", cosb.full(),
                  ctx.consts["ropec"][tt * 128:(tt + 1) * 128, :],
                  tag=f"{tag} cs")
            P.dma("sp", sinb.full(),
                  ctx.consts["ropes"][tt * 128:(tt + 1) * 128, :],
                  tag=f"{tag} sn")
            emit_rope_tile(ctx, o, nsl // 128, cosb.full(),
                           sinb.full(), f"{tag} rp{tt}.{ns}")
            P.dma("pool", dst[tt * 128:(tt + 1) * 128,
                              ns * nsl:(ns + 1) * nsl], o.full(),
                  tag=f"{tag} qw")
        return cb

    emit_gemm(ctx, xln, w["wq"], rope_cb(qtm, H, nslab), T, D, D,
              tag=tag + ".q", wprep="f32r_split", nslab=nslab)
    emit_gemm(ctx, xln, w["wk"], rope_cb(ktm, KV, min(nslab, Dkv)),
              T, D, Dkv, tag=tag + ".k", nslab=min(nslab, Dkv),
              wprep="f32r_split")

    def vsplit_cb(ps, tt, ns):
        o1 = reg.get("reo0", (128, min(nslab, Dkv)), F32R)
        o2 = reg.get("reo1", (128, min(nslab, Dkv)), F32R)
        P.copy("dve", o1.full(), ps, tag=f"{tag} v s1")
        P.tt("dve", o2.full(), ps, o1.full(), ALU.subtract,
             tag=f"{tag} v s2")
        for d_, o_ in ((v1tm, o1), (v2tm, o2)):
            P.dma("pool", d_[tt * 128:(tt + 1) * 128,
                            ns * min(nslab, Dkv):(ns + 1) * min(nslab, Dkv)],
                  o_.full(), tag=f"{tag} vw")

    emit_gemm(ctx, xln, w["wv"], vsplit_cb, T, D, Dkv,
              nslab=min(nslab, Dkv), wprep="f32r_split", tag=tag + ".v")

    q1 = sc("s_q1", kt4_shape(T, D), F32R)
    q2 = sc("s_q2", kt4_shape(T, D), F32R)
    k1 = sc("s_k1", kt4_shape(T, Dkv), F32R)
    k2 = sc("s_k2", kt4_shape(T, Dkv), F32R)
    emit_transpose_split(ctx, qtm, T, D, dst_split=(q1, q2), tag=tag + ".tq")
    emit_transpose_split(ctx, ktm, T, Dkv, dst_split=(k1, k2),
                         tag=tag + ".tk")

    ao = sc("s_a1", kt4_shape(T, D), F32)
    mtri = ctx.sconst("mask_triu", 128, 128)
    msl = ctx.sconst("mask_sl", 128, 128)
    ones_row = ctx.sconst("ones_row", 1, 512)
    onesc = reg.get("gwR", (128, 2), F32R, off=11264)
    _o1f = ctx.S[6][:, 8:10]
    P.memset("dve", _o1f, 1.0, tag=f"{tag} oc0")
    P.copy("dve", onesc.full(), _o1f, tag=f"{tag} oc")
    scale = 1.0 / float(np.sqrt(dh))
    nts = T // 512
    wtiles = WIN // 128

    for h in range(nh):
        g = h // nrep
        kb1 = reg.get("gwR", (128, nt, 128), F32R, off=0)
        kb2 = reg.get("gwR", (128, nt, 128), F32R, off=nt * 128)
        vb1 = reg.get("gwR", (128, nt, 128), F32R, off=2 * nt * 128)
        vb2 = reg.get("gwR", (128, nt, 128), F32R, off=3 * nt * 128)
        if h % nrep == 0:
            for (b, src, iskt4) in ((kb1, k1, True), (kb2, k2, True),
                                    (vb1, v1tm, False), (vb2, v2tm, False)):
                if iskt4:
                    P.dma("sp", b.full(),
                          TV(src.base[:, :, g, :].rearrange(
                              "t p w -> p t w"), src.key, 0, 1, 0,
                             src.free_total * nt * 4),
                          tag=f"{tag} kld.{h}")
                else:
                    P.dma("sp", b.full(),
                          TV(src.base.rearrange(
                              "(t p) (c w) -> c p t w", p=128, w=128)[g],
                             src.key, 0, 1, 0, src.free_total * T * 4),
                          tag=f"{tag} vld.{h}")
        for ts in range(nts):
            qb1 = reg.get("gwR", (128, 4, 128), F32R, off=8192)
            qb2 = reg.get("gwR", (128, 4, 128), F32R, off=8704)
            P.dma("sp", qb1.full(),
                  TV(q1.base[4 * ts:4 * ts + 4, :, h, :]
                     .rearrange("t p w -> p t w"), q1.key, 0, 1, 0,
                     q1.free_total * nt * 4), tag=f"{tag} qld1.{h}.{ts}")
            P.dma("sp", qb2.full(),
                  TV(q2.base[4 * ts:4 * ts + 4, :, h, :]
                     .rearrange("t p w -> p t w"), q2.key, 0, 1, 0,
                     q2.free_total * nt * 4), tag=f"{tag} qld2.{h}.{ts}")
            q1s = TV(qb1.full().ap.rearrange("p t w -> p (t w)"), "rg_gwR",
                     0, 128, 8192 * 4, 8192 * 4 + 2048)
            q2s = TV(qb2.full().ap.rearrange("p t w -> p (t w)"), "rg_gwR",
                     0, 128, 8704 * 4, 8704 * 4 + 2048)
            nps = ctx.pb[2][:, 0:512]
            dps = ctx.pb[3][0:2, 0:512]
            tks = [tk for tk in range(max(0, 4 * ts - wtiles),
                                      4 * ts + 4)]

            def emit_sc(tk):
                sps = ctx.pb[4 + tk % 3][:, 0:512]
                P.mm_group([(kb1[:, tk, :], q1s),
                            (kb2[:, tk, :], q1s),
                            (kb1[:, tk, :], q2s)], sps,
                           tag=f"{tag} sc.{h}.{ts}.{tk}")
                base = (tk % 2) * 1024
                eb = reg.get("lnx", (128, 512), F32, off=base)
                e2 = reg.get("gwR", (128, 512), F32R,
                             off=9216 + (tk % 2) * 1024)
                e1 = reg.get("gwR", (128, 512), F32R,
                             off=9216 + (tk % 2) * 1024 + 512)
                P.act(eb.full(), sps, AF.Exp, scale=scale,
                      tag=f"{tag} exp")
                for qq in range(4):
                    tq = 4 * ts + qq
                    dcol = tq - tk
                    sl = TV(eb.full().ap[:, qq * 128:(qq + 1) * 128],
                            "rg_lnx", 0, 128, eb.full().b0 + qq * 512,
                            eb.full().b0 + (qq + 1) * 512)
                    if dcol == 0:
                        P.tt("dve", sl, sl, mtri.full(), ALU.mult,
                             tag=f"{tag} mk")
                    elif dcol < 0:
                        P.memset("dve", sl, 0.0, tag=f"{tag} z")
                    elif dcol == wtiles:
                        P.tt("dve", sl, sl, msl.full(), ALU.mult,
                             tag=f"{tag} mw")
                    elif dcol > wtiles:
                        P.memset("dve", sl, 0.0, tag=f"{tag} z2")
                P.copy("act", e2.full(), eb.full(), tag=f"{tag} es1")
                P.tt("dve", e1.full(), eb.full(), e2.full(), ALU.subtract,
                     tag=f"{tag} es2")
                return e1, e2

            epipe = {tks[0]: emit_sc(tks[0])}
            for i, tk in enumerate(tks):
                if i + 1 < len(tks):
                    epipe[tks[i + 1]] = emit_sc(tks[i + 1])
                e1, e2 = epipe[tk]
                P.mm_group([(vb1[:, tk, :], e2.full()),
                            (vb2[:, tk, :], e2.full()),
                            (vb1[:, tk, :], e1.full())], nps,
                           start=(i == 0), stop=(i == len(tks) - 1),
                           tag=f"{tag} av.{h}.{ts}.{tk}")
                P.mm_group([(onesc.full(), e2.full()),
                            (onesc.full(), e1.full())], dps,
                           start=(i == 0), stop=(i == len(tks) - 1),
                           tag=f"{tag} dn.{h}.{ts}.{tk}")
            dn = reg.get("msc1", (1, 512), F32, off=256)
            P.copy("dve", dn.full(), TV(dps.ap[0:1, :], dps.key, 0, 1,
                                        dps.b0, dps.b0 + 2048),
                   tag=f"{tag} dnc")
            P.recip(dn.full(), dn.full(), tag=f"{tag} rec")
            rb_ps = ctx.pb[7][:, 0:512]
            ones_r1 = reg.get("msc1", (1, 128), F32, off=1024)
            P.memset("pool", ones_r1.full(), 1.0, tag=f"{tag} o1")
            P.mm_group([(ones_r1.full(), dn.full())], rb_ps,
                       tag=f"{tag} rb.{h}.{ts}")
            rb = reg.get("lnx", (128, 512), F32, off=2048)
            P.copy("act", rb.full(), rb_ps, tag=f"{tag} rbc")
            att = reg.get("lnx", (128, 512), F32, off=2560)
            P.tt("dve", att.full(), nps, rb.full(), ALU.mult,
                 tag=f"{tag} norm")
            P.dma("pool",
                  TV(ao.base[4 * ts:4 * ts + 4, :, h, :]
                     .rearrange("t p w -> p t w"), ao.key, 0, 1, 0,
                     ao.free_total * nt * 4),
                  TV(att.full().ap.rearrange("p (t w) -> p t w", w=128),
                     att.full().key, 0, 128, att.full().b0,
                     att.full().b1),
                  tag=f"{tag} aw")

    emit_gemm(ctx, ao, w["wo"],
              res_add_cb(ctx, xres, out, T, D, nslab=nslab),
              T, D, D, wprep="f32r_split",
              nslab=nslab, tag=tag + ".o")



def emit_l2_ffn(ctx, xres, out, w, T=2048, D=2048, I=8192, nslab=512,
                tag="l2f"):
    P, reg = ctx.P, ctx.reg
    nt = T // 128
    sc = ctx.scratch
    xln = sc("s_x1", kt4_shape(T, D), F32)
    emit_ln(ctx, xres, T, D, dst_f32=xln, tag=tag + ".ln")

    onesr = ctx.R[1][0:1, 0:128]
    P.memset("dve", onesr, 1.0, tag=f"{tag} o")
    ident = ctx.sconst("ident", 128, 128)
    sgtm = sc("s_t1", (T, I), F32)
    hkt = sc("s_g1", kt4_shape(T, I), F32)

    def bcast_scale(slot, src_dram, ns, nm):
        bt = reg.get("msc0", (128, nslab), F32, off=slot * 512)
        stv = src_dram[ns * nslab:(ns + 1) * nslab]
        bsrc = TV(stv.ap.partition_broadcast(128), stv.key, stv.p0,
                  stv.p1, stv.b0, stv.b1)
        P.dma("sp", bt.full(), bsrc, tag=f"{tag} {nm}r{ns}")
        return bt

    gb_ref = {}

    def gate_cb(ps, tt, ns):
        if tt % 6 == 0:
            gb_ref[ns] = bcast_scale(0, w["gs"], ns, "gs")
        gb = gb_ref[ns]
        o = reg.get(f"reo{tt % 2}", (128, nslab), F32)
        P.tt("dve", o.full(), ps, gb.full(), ALU.mult, tag=f"{tag} gsc")
        P.act(o.full(), o.full(), AF.Silu, tag=f"{tag} silu")
        P.dma("pool", sgtm[tt * 128:(tt + 1) * 128,
                           ns * nslab:(ns + 1) * nslab], o.full(),
              tag=f"{tag} sgw")

    emit_gemm(ctx, xln, w["wg"], gate_cb, T, D, I,
              wprep="int_f32r", nslab=nslab, tag=tag + ".g")

    ub_ref = {}

    def up_cb(ps, tt, ns):
        if tt % 6 == 0:
            ub_ref[ns] = bcast_scale(1, w["us"], ns, "us")
        ub = ub_ref[ns]
        t_ = reg.get(f"reo{tt % 2}", (128, nslab), F32)
        P.tt("dve", t_.full(), ps, ub.full(), ALU.mult, tag=f"{tag} usc")
        sl = reg.get(f"rex{tt % 2}", (128, nslab), F32)
        P.dma("act", sl.full(), sgtm[tt * 128:(tt + 1) * 128,
                                     ns * nslab:(ns + 1) * nslab],
              tag=f"{tag} sgr")
        P.tt("dve", t_.full(), t_.full(), sl.full(), ALU.mult,
             tag=f"{tag} h")
        psT = ctx.pb[(tt % 2)]
        for j in range(nslab // 128):
            P.transpose(psT[:, j * 128:(j + 1) * 128],
                        TV(t_.full().ap[:, j * 128:(j + 1) * 128],
                           t_.full().key, 0, 128,
                           t_.full().b0 + j * 512,
                           t_.full().b0 + (j + 1) * 512),
                        ident.full(), tag=f"{tag} hT")
        o1 = reg.get("tsa", (128, nslab), F32)
        pv = psT[:, 0:nslab]
        P.copy("dve", o1.full(), pv, tag=f"{tag} hs1")
        P.dma("pool", kt4_block(hkt, tt, ns * (nslab // 128), nslab // 128),
              o1.full(), tag=f"{tag} hw1")

    emit_gemm(ctx, xln, w["wu"], up_cb, T, D, I,
              wprep="int_f32r", nslab=nslab, tag=tag + ".u")

    acc = sc("s_t2", (T, D), F32)
    kgc = min(16, I // 128)
    nkg = (I // 128) // kgc
    db_ref = {}

    def final_cb(ps, tt, ns):
        if tt % 6 == 0:
            db_ref[ns] = bcast_scale(2, w["ds"], ns, "ds")
        db = db_ref[ns]
        o = reg.get(f"reo{tt % 2}", (128, nslab), F32)
        if nkg > 1:
            a = reg.get(f"rex{tt % 2}", (128, nslab), F32)
            P.dma("act", a.full(), acc[tt * 128:(tt + 1) * 128,
                                       ns * nslab:(ns + 1) * nslab],
                  tag=f"{tag} far")
            P.tt("dve", o.full(), ps, a.full(), ALU.add, tag=f"{tag} fa")
            P.tt("dve", o.full(), o.full(), db.full(), ALU.mult,
                 tag=f"{tag} fs")
        else:
            P.tt("dve", o.full(), ps, db.full(), ALU.mult, tag=f"{tag} fs")
        xv = reg.get("tsd", (128, nslab), F32)
        P.dma("act", xv.full(), xres[tt * 128:(tt + 1) * 128,
                                     ns * nslab:(ns + 1) * nslab],
              tag=f"{tag} fx")
        P.tt("dve", o.full(), o.full(), xv.full(), ALU.add,
             tag=f"{tag} fr")
        P.dma("pool", out[tt * 128:(tt + 1) * 128,
                          ns * nslab:(ns + 1) * nslab], o.full(),
              tag=f"{tag} fw")

    for kg in range(nkg):
        last = (kg == nkg - 1)
        if last:
            cb = final_cb
        elif kg == 0:
            cb = store_cb(ctx, acc, T, D, nslab, tag)
        else:
            cb = accum_cb(ctx, acc, T, D, nslab, tag)
        emit_gemm(ctx, hkt, w["wd"], cb, T, kgc * 128, D,
                  wprep="int_f32r", nslab=nslab,
                  kgroup=(kg * kgc, kgc), tag=tag + f".d{kg}")



def emit_l3_attn(ctx, xres, out, w, T=2048, D=2048, NST=16, RNK=128,
                 nslab=512, tag="l3a"):
    P, reg = ctx.P, ctx.reg
    nt, nk = T // 128, D // 128
    NE = RNK + 2 * NST
    sc = ctx.scratch
    ident = ctx.sconst("ident", 128, 128)
    u_fm = sc("s_q1", kt4_shape(T, D), F32)
    emit_ln(ctx, xres, T, D, dst_f32=u_fm, tag=tag + ".ln")

    sdbl = sc("s_t1", (T, NE), F32)

    def dbl_cb(ps, tt, ns):
        o = reg.get(f"reo{tt % 2}", (128, NE), F32)
        P.copy("dve", o.full(), ps, tag=f"{tag} dble")
        P.dma("pool", sdbl[tt * 128:(tt + 1) * 128, :], o.full(),
              tag=f"{tag} dblw")

    emit_gemm(ctx, u_fm, w["xproj"], dbl_cb, T, D, NE,
              wprep="f32r_split", nslab=NE, tag=tag + ".dbl")

    drk = sc("s_t3", kt4_shape(T, RNK), F32)
    bct_d = sc("s_bct", (2 * NST, T), F32)
    for tt in range(nt):
        xt = reg.get("lnx", (128, NE), F32, off=(tt % 2) * 256)
        P.dma("sp", xt.full(), sdbl[tt * 128:(tt + 1) * 128, :],
              tag=f"{tag} dblr{tt}")
        ps = ctx.pb[tt % 2]
        P.transpose(ps[:, 0:128],
                    TV(xt.full().ap[:, 0:128], xt.full().key, 0, 128,
                       xt.full().b0, xt.full().b0 + 512),
                    ident.full(), tag=f"{tag} drT{tt}")
        o1 = reg.get("tsa", (128, 128), F32)
        pv = ps[:, 0:128]
        P.copy("dve", o1.full(), pv, tag=f"{tag} drs1")
        P.dma("pool", kt4_block(drk, tt, 0, 1), o1.full(),
              tag=f"{tag} drw1")
        ps2 = ctx.pb[2 + tt % 2]
        P.transpose(ps2[0:2 * NST, 0:128],
                    TV(xt.full().ap[:, RNK:NE], xt.full().key, 0, 128,
                       xt.full().b0 + RNK * 4, xt.full().b0 + NE * 4),
                    ident.full(), tag=f"{tag} bcT{tt}")
        ob = reg.get("tsc", (2 * NST, 128), F32)
        P.copy("dve", ob.full(), ps2[0:2 * NST, 0:128], tag=f"{tag} bce")
        P.dma("pool", bct_d[0:2 * NST, tt * 128:(tt + 1) * 128], ob.full(),
              tag=f"{tag} bcw{tt}")

    delta_tm = sc("s_t2", (T, D), F32)
    onesr = ctx.R[1][0:1, 0:128]
    P.memset("dve", onesr, 1.0, tag=f"{tag} o")
    btvs = []
    for i in range(D // nslab):
        rb = reg.get("msc1", (1, nslab), F32, off=i * nslab)
        P.dma("sp", rb.full(), w["dtb"][i * nslab:(i + 1) * nslab],
              tag=f"{tag} dtb{i}")
        btvs.append(rb.full())

    def d_cb(ps, tt, ns):
        t1 = reg.get("tsa", (128, nslab), F32)
        t2 = reg.get("tsb", (128, nslab), F32)
        o = reg.get(f"reo{tt % 2}", (128, nslab), F32)
        P.tscalar("dve", t2.full(), ps, 0.0, None, ALU.max,
                  tag=f"{tag} sp4")
        P.act(t1.full(), ps, AF.Abs, tag=f"{tag} sp1",
              extra=[t2.full()])
        P.act(t1.full(), t1.full(), AF.Exp, scale=-1.0, tag=f"{tag} sp2")
        P.act(t1.full(), t1.full(), AF.Ln, bias=1.0, tag=f"{tag} sp3")
        P.tt("dve", o.full(), t1.full(), t2.full(), ALU.add,
             tag=f"{tag} sp5")
        P.dma("pool", delta_tm[tt * 128:(tt + 1) * 128,
                               ns * nslab:(ns + 1) * nslab], o.full(),
              tag=f"{tag} dw")

    emit_gemm(ctx, drk, w["dtw"], d_cb, T, RNK, D,
              wprep="f32r_split", nslab=nslab,
              extra_pairs=lambda tt, ns: [(onesr, btvs[ns])],
              tag=tag + ".dt")

    df = sc("s_t5", kt4_shape(T, D), F32)
    emit_transpose_split(ctx, delta_tm, T, D, dst_f32=df, tag=tag + ".tdf")

    ao_tm = sc("s_t6", (T, D), F32)
    dcol = reg.get("st0", (128, nk, NST), F32, off=0)
    dvec = reg.get("st0", (128, nk), F32, off=nk * NST)
    hstate = reg.get("st1", (128, 2, NST), F32, off=0)
    P.dma("sp", dvec.full(), w["dvec"].full(), tag=f"{tag} dv")
    for c in range(nk):
        P.dma("sp", dcol[:, c, :], w["alog"][c * 128:(c + 1) * 128, :],
              tag=f"{tag} al{c}")
    P.act(dcol.full(), dcol.full(), AF.Exp, tag=f"{tag} ae")
    P.tscalar("dve", dcol.full(), dcol.full(), -1.0, None, ALU.mult,
              tag=f"{tag} an")

    CG = 2
    TH = T // 2
    nth = nt // 2
    for cg in range(nk // CG):
        cs = [cg * CG + i for i in range(CG)]
        for th in range(2):
            dl, du, yy = {}, {}, {}
            for i, c in enumerate(cs):
                dl[c] = reg.get("gwB", (128, TH), F32, off=i * TH)
                du[c] = reg.get("gwB", (128, TH), F32, off=(CG + i) * TH)
                yy[c] = reg.get("msc0", (128, TH), F32, off=i * TH)
                P.dma("sp", dl[c].full(),
                      TV(df.base[th * nth:(th + 1) * nth, :, c, :]
                         .rearrange("t p w -> p t w"), df.key, 0, 1, 0,
                         df.free_total * nt * 4), tag=f"{tag} dld{c}")
                ut = reg.get("gxB", (128, TH), F32, off=4 * TH)
                P.dma("sp", ut.full(),
                      TV(u_fm.base[th * nth:(th + 1) * nth, :, c, :]
                         .rearrange("t p w -> p t w"), u_fm.key, 0, 1, 0,
                         u_fm.free_total * nt * 4), tag=f"{tag} uld{c}")
                P.tt("dve", du[c].full(), dl[c].full(), ut.full(), ALU.mult,
                     tag=f"{tag} du{c}")
                P.memset("pool", yy[c].full(), 0.0, tag=f"{tag} y0{c}")
            for n in range(NST):
                bb = reg.get("gxB", (128, TH), F32, off=(n % 2) * TH)
                cbf = reg.get("gxB", (128, TH), F32, off=(2 + n % 2) * TH)
                P.dma("sp", bb.full(),
                      TV(bct_d.base[n:n + 1, th * TH:(th + 1) * TH]
                         .partition_broadcast(128), bct_d.key, 0, 1,
                         th * TH * 4, (th + 1) * TH * 4),
                      tag=f"{tag} bb{n}")
                P.dma("sp", cbf.full(),
                      TV(bct_d.base[NST + n:NST + n + 1,
                                    th * TH:(th + 1) * TH]
                         .partition_broadcast(128), bct_d.key, 0, 1,
                         (NST + n) * T * 4 + th * TH * 4,
                         (NST + n) * T * 4 + (th + 1) * TH * 4),
                      tag=f"{tag} cb{n}")
                for ci, c in enumerate(cs):
                    par = (n * CG + ci) % 2
                    dA = reg.get("lnx", (128, TH), F32, off=par * 2 * TH)
                    dBu = reg.get("lnx", (128, TH), F32,
                                  off=par * 2 * TH + TH)
                    hh = reg.get("stg", (128, TH), F32, off=par * TH)
                    P.act(dA.full(), dl[c].full(), AF.Exp,
                          scale=dcol[:, c, n:n + 1], tag=f"{tag} dA{c}.{n}")
                    P.tt("pool", dBu.full(), du[c].full(), bb.full(),
                         ALU.mult, tag=f"{tag} dBu{c}.{n}")
                    if th == 1:
                        tmp = reg.get("st1", (128, 1), F32, off=40)
                        P.tt("dve", tmp.full(), dA[:, 0:1],
                             hstate[:, ci, n:n + 1], ALU.mult,
                             tag=f"{tag} hc{c}.{n}")
                        P.tt("dve", dBu[:, 0:1], dBu[:, 0:1], tmp.full(),
                             ALU.add, tag=f"{tag} hc2{c}.{n}")
                    P.scan("dve", hh.full(), dA.full(), dBu.full(), 0.0,
                           ALU.mult, ALU.add, tag=f"{tag} sc{c}.{n}")
                    if th == 0:
                        P.copy("dve", hstate[:, ci, n:n + 1],
                               hh[:, TH - 1:TH], tag=f"{tag} hs{c}.{n}")
                    P.tt("dve", dA.full(), hh.full(), cbf.full(), ALU.mult,
                         tag=f"{tag} yc{c}.{n}")
                    P.tt("dve", yy[c].full(), yy[c].full(), dA.full(),
                         ALU.add, tag=f"{tag} ya{c}.{n}")
            for i, c in enumerate(cs):
                ut = reg.get("gxB", (128, TH), F32, off=4 * TH)
                P.dma("sp", ut.full(),
                      TV(u_fm.base[th * nth:(th + 1) * nth, :, c, :]
                         .rearrange("t p w -> p t w"), u_fm.key, 0, 1, 0,
                         u_fm.free_total * nt * 4), tag=f"{tag} uld2{c}")
                P.stt("dve", yy[c].full(), ut.full(), dvec[:, c:c + 1],
                      yy[c].full(), ALU.mult, ALU.add, tag=f"{tag} yf{c}")
                for q in range(nth):
                    tt = th * nth + q
                    ps = ctx.pb[4 + q % 2]
                    P.transpose(ps[:, 0:128],
                                TV(yy[c].full().ap[:, q * 128:(q + 1) * 128],
                                   yy[c].full().key, 0, 128,
                                   yy[c].full().b0 + q * 512,
                                   yy[c].full().b0 + (q + 1) * 512),
                                ident.full(), tag=f"{tag} yT{c}.{tt}")
                    o = reg.get("tsd", (128, 128), F32)
                    P.copy("dve", o.full(), ps[:, 0:128], tag=f"{tag} yE")
                    P.dma("pool", ao_tm[tt * 128:(tt + 1) * 128,
                                        c * 128:(c + 1) * 128], o.full(),
                          tag=f"{tag} yW{c}.{tt}")

    for tt in range(nt):
        a = reg.get("lnx", (128, D), F32, off=(tt % 2) * 2048)
        b = reg.get("stg", (128, D), F32, off=0)
        P.dma("sp", a.full(), ao_tm[tt * 128:(tt + 1) * 128, :],
              tag=f"{tag} ra")
        P.dma("act", b.full(), xres[tt * 128:(tt + 1) * 128, :],
              tag=f"{tag} rx")
        P.tt("dve", a.full(), a.full(), b.full(), ALU.add, tag=f"{tag} rr")
        P.dma("pool", out[tt * 128:(tt + 1) * 128, :], a.full(),
              tag=f"{tag} rw")



def emit_l3_ffn(ctx, xres, out, w, T=2048, D=2048, E=8, FF=512,
                nslab=512, tag="l3f"):
    P, reg = ctx.P, ctx.reg
    nt, nk = T // 128, D // 128
    EF = E * FF
    nkf = FF // 128
    sc = ctx.scratch
    xlnf = sc("s_x1", kt4_shape(T, D), F32)
    xb = sc("s_xb", kt4_shape(T, D), BF16)
    emit_ln(ctx, xres, T, D, dst_f32=xlnf, dst_bf16=xb,
            tag=tag + ".ln")

    wn = reg.get("msc0", (128, nt * E), F32, off=1024)

    def r_cb(ps, tt, ns):
        s_ = ctx.S
        sco = s_[5][:, 0:E]
        P.copy("dve", sco, ps, tag=f"{tag} le")
        v1 = s_[6][:, 8:9]
        P.reduce("dve", v1, sco, ALU.max, tag=f"{tag} v1")
        eq1 = s_[7][:, 0:E]
        P.tscalar("dve", eq1, sco, v1, None, ALU.is_ge, tag=f"{tag} eq1")
        s2 = s_[5][:, 8:8 + E]
        P.stt("dve", s2, eq1, -1e30, sco, ALU.mult, ALU.add,
              tag=f"{tag} s2")
        v2 = s_[6][:, 9:10]
        P.reduce("dve", v2, s2, ALU.max, tag=f"{tag} v2")
        bb = s_[6][:, 10:11]
        P.tscalar("dve", bb, v2, v1, None, ALU.subtract, tag=f"{tag} d21")
        P.act(bb, bb, AF.Exp, tag=f"{tag} eb")
        w1c = s_[6][:, 11:12]
        P.tscalar("dve", w1c, bb, 1.0, None, ALU.add, tag=f"{tag} 1b")
        P.recip(w1c, w1c, tag=f"{tag} w1")
        w2c = s_[6][:, 12:13]
        P.tt("dve", w2c, bb, w1c, ALU.mult, tag=f"{tag} w2")
        m2 = s_[7][:, 8:8 + E]
        P.tscalar("dve", m2, sco, v2, None, ALU.is_ge, tag=f"{tag} m2")
        P.tt("dve", m2, m2, eq1, ALU.subtract, tag=f"{tag} msec")
        P.tscalar("dve", eq1, eq1, w1c, None, ALU.mult, tag=f"{tag} a1")
        P.tscalar("dve", m2, m2, w2c, None, ALU.mult, tag=f"{tag} a2")
        P.tt("dve", wn[:, tt * E:(tt + 1) * E], eq1, m2, ALU.add,
             tag=f"{tag} wm")

    emit_gemm(ctx, xlnf, w["rw"], r_cb, T, D, E,
              wprep="f32r_split", nslab=E, tag=tag + ".r")

    s1tm = sc("s_b1", (T, EF), BF16)
    identb = ctx.sconst("ident_bf", 128, 128, dtype=BF16)

    def h1_cb(ps, tt, ns):
        c1 = reg.get("tsa", (128, FF), BF16)
        P.copy("dve", c1.full(), ps, tag=f"{tag} c1")
        o = reg.get("tsb", (128, FF), BF16)
        P.act(o.full(), c1.full(), AF.Silu, tag=f"{tag} silu")
        P.dma("pool", s1tm[tt * 128:(tt + 1) * 128,
                           ns * FF:(ns + 1) * FF], o.full(),
              tag=f"{tag} s1w")

    emit_gemm(ctx, xb, w["w1"], h1_cb, T, D, EF,
              wprep="bf16", nslab=FF, tag=tag + ".h1")

    hT = sc("s_b2", kt4_shape(T, EF), BF16)

    def h3_cb(ps, tt, ns):
        c3 = reg.get("tsa", (128, FF), BF16)
        P.copy("dve", c3.full(), ps, tag=f"{tag} c3")
        sl = reg.get("tsb", (128, FF), BF16)
        P.dma("act", sl.full(), s1tm[tt * 128:(tt + 1) * 128,
                                     ns * FF:(ns + 1) * FF],
              tag=f"{tag} s1r")
        h = reg.get("tsc", (128, FF), BF16)
        P.tt("dve", h.full(), c3.full(), sl.full(), ALU.mult,
             tag=f"{tag} h")
        psT = ctx.pb[(tt % 2)].cast(BF16)
        for j in range(nkf):
            P.transpose(psT[:, j * 128:(j + 1) * 128],
                        TV(h.full().ap[:, j * 128:(j + 1) * 128],
                           h.full().key, 0, 128,
                           h.full().b0 + j * 256,
                           h.full().b0 + (j + 1) * 256),
                        identb.full(), tag=f"{tag} hT")
        ob = reg.get("tsd", (128, FF), BF16)
        P.copy("dve", ob.full(), psT[:, 0:FF], tag=f"{tag} hE")
        P.dma("pool", kt4_block(hT, tt, ns * nkf, nkf), ob.full(),
              tag=f"{tag} hw")

    emit_gemm(ctx, xb, w["w3"], h3_cb, T, D, EF,
              wprep="bf16", nslab=FF, tag=tag + ".h3")

    nsl = D // nslab
    EG = 4
    acc = sc("s_t2", (T, D), F32)
    for ns in range(nsl):
        for eg in range(E // EG):
            w2c = reg.get("gwB", (128, EG * nkf, nslab), BF16, off=0)
            for j in range(EG):
                e = eg * EG + j
                st = reg.get("stg", (128, nkf * nslab), F32, off=0)
                P.dma("sp", st.full(), wslab_block(w["w2"], ns, e * nkf,
                                                   nkf),
                      tag=f"{tag} w2l{e}")
                st3 = reg.get("stg", (128, nkf, nslab), F32, off=0)
                P.copy("dve", TV(w2c[:, j * nkf:(j + 1) * nkf, :]
                                 .ap.rearrange("p c s -> p (c s)"),
                                 "rg_gwB", 0, 128,
                                 w2c[:, j * nkf:(j + 1) * nkf, :].b0,
                                 w2c[:, j * nkf:(j + 1) * nkf, :].b1),
                       st.full(), tag=f"{tag} w2c{e}")
            for ttg in range(nt // 2):
                tts = (2 * ttg, 2 * ttg + 1)
                hb = {}
                for i, tt in enumerate(tts):
                    hb[tt] = reg.get("gxB", (128, EG * nkf, 128), BF16,
                                     off=i * EG * nkf * 128)
                    P.dma("sp", TV(hb[tt].full().ap.rearrange(
                        "p c t -> p (c t)"),
                        "rg_gxB", 0, 128, hb[tt].full().b0,
                        hb[tt].full().b1),
                        kt4_block(hT, tt, eg * EG * nkf, EG * nkf),
                        tag=f"{tag} hld{tt}")
                    osum = reg.get("lnx", (128, nslab), F32, off=i * nslab)
                    if eg == 0:
                        P.memset("dve", osum.full(), 0.0,
                                 tag=f"{tag} os0")
                    else:
                        P.dma("act", osum.full(),
                              acc[tt * 128:(tt + 1) * 128,
                                  ns * nslab:(ns + 1) * nslab],
                              tag=f"{tag} osr")
                for j in range(EG):
                    e = eg * EG + j
                    for i, tt in enumerate(tts):
                        ps = ctx.pb[4 + (i + e * 2) % 2][:, 0:nslab]
                        pairs = [(hb[tt][:, j * nkf + c, :],
                                  w2c[:, j * nkf + c, :])
                                 for c in range(nkf)]
                        P.mm_group(pairs, ps, tag=f"{tag} dmm{e}.{tt}.{ns}")
                        rb = reg.get("tsa", (128, nslab), BF16,
                                     off=(i % 2) * nslab)
                        P.copy("dve", rb.full(), ps, tag=f"{tag} rb")
                        osum = reg.get("lnx", (128, nslab), F32,
                                       off=i * nslab)
                        P.stt("dve", osum.full(), rb.full(),
                              wn[:, tt * E + e:tt * E + e + 1], osum.full(),
                              ALU.mult, ALU.add, tag=f"{tag} osa")
                for i, tt in enumerate(tts):
                    osum = reg.get("lnx", (128, nslab), F32, off=i * nslab)
                    if eg == 0:
                        P.dma("pool", acc[tt * 128:(tt + 1) * 128,
                                          ns * nslab:(ns + 1) * nslab],
                              osum.full(), tag=f"{tag} osw")
                        continue
                    xv = reg.get("lnx", (128, nslab), F32, off=2 * nslab +
                                 (i % 2) * nslab)
                    P.dma("act", xv.full(), xres[tt * 128:(tt + 1) * 128,
                                                 ns * nslab:
                                                 (ns + 1) * nslab],
                          tag=f"{tag} fx")
                    P.tt("dve", osum.full(), osum.full(), xv.full(),
                         ALU.add, tag=f"{tag} fr")
                    P.dma("pool", out[tt * 128:(tt + 1) * 128,
                                      ns * nslab:(ns + 1) * nslab],
                          osum.full(), tag=f"{tag} fw")



T_, D_ = 2048, 2048


def _host_prep(inputs):
    g = lambda k: np.asarray(inputs[k])
    w = {}
    for k in ("Wq", "Wk", "Wv", "Wo"):
        w["l0a_" + k.lower()] = host_wslab(
            np.ascontiguousarray(g("L0_attn_" + k)), 512)
    w["l0f_w"] = np.ascontiguousarray(g("L0_ffn_weight"))
    w["l0f_b"] = np.ascontiguousarray(g("L0_ffn_bias"))
    w["l0f_ts"] = np.asarray(g("L0_ffn_x_tensor_scale"),
                             np.float32).reshape(1, 1)
    for k in ("Wq", "Wk", "Wv", "Wo"):
        w["l1a_" + k.lower()] = host_wslab(
            np.ascontiguousarray(g("L1_attn_" + k).T), 512)
    for k in ("bq", "bk", "bv", "bo"):
        w["l1a_" + k] = np.ascontiguousarray(g("L1_attn_" + k))
    w["l1f_wg"] = host_wslab(np.ascontiguousarray(g("L1_ffn_Wg").T), 8)
    w["l1f_w1"] = host_wslab(np.ascontiguousarray(
        g("L1_ffn_W1").reshape(8192, 2048).T), 512)
    w["l1f_w3"] = host_wslab(np.ascontiguousarray(
        g("L1_ffn_W3").reshape(8192, 2048).T), 512)
    w["l1f_w2"] = host_wslab(np.ascontiguousarray(
        g("L1_ffn_W2").transpose(0, 2, 1).reshape(8192, 2048)), 512)
    w["l1f_bias"] = np.ascontiguousarray(g("L1_ffn_bias"))
    w["l2a_wq"] = host_wslab(np.ascontiguousarray(g("L2_attn_Wq").T), 512)
    w["l2a_wk"] = host_wslab(np.ascontiguousarray(g("L2_attn_Wk").T), 512)
    w["l2a_wv"] = host_wslab(np.ascontiguousarray(g("L2_attn_Wv").T), 512)
    w["l2a_wo"] = host_wslab(np.ascontiguousarray(g("L2_attn_Wo").T), 512)
    w["l2f_wg"] = host_wslab(np.ascontiguousarray(g("L2_ffn_gate_w").T), 512)
    w["l2f_wu"] = host_wslab(np.ascontiguousarray(g("L2_ffn_up_w").T), 512)
    w["l2f_wd"] = host_wslab(np.ascontiguousarray(g("L2_ffn_down_w").T), 512)
    w["l2f_gs"] = np.ascontiguousarray(g("L2_ffn_gate_scale"))
    w["l2f_us"] = np.ascontiguousarray(g("L2_ffn_up_scale"))
    w["l2f_ds"] = np.ascontiguousarray(g("L2_ffn_down_scale"))
    w["l3a_xproj"] = host_wslab(np.ascontiguousarray(
        g("L3_attn_x_proj_w").T), 160)
    w["l3a_dtw"] = host_wslab(np.ascontiguousarray(
        g("L3_attn_dt_proj_w").T), 512)
    w["l3a_dtb"] = np.ascontiguousarray(g("L3_attn_dt_proj_b"))
    w["l3a_alog"] = np.ascontiguousarray(g("L3_attn_A_log"))
    w["l3a_dvec"] = np.ascontiguousarray(
        g("L3_attn_D").reshape(16, 128).T)
    w["l3f_rw"] = host_wslab(np.ascontiguousarray(g("L3_ffn_router_w").T), 8)
    w["l3f_w1"] = host_wslab(np.ascontiguousarray(
        g("L3_ffn_w1").transpose(1, 0, 2).reshape(2048, 4096)), 512)
    w["l3f_w3"] = host_wslab(np.ascontiguousarray(
        g("L3_ffn_w3").transpose(1, 0, 2).reshape(2048, 4096)), 512)
    w["l3f_w2"] = host_wslab(np.ascontiguousarray(
        g("L3_ffn_w2").reshape(4096, 2048)), 512)
    return w


_BLOCK_WDEFS = {
    "l0a": [("l0a_wq", None, F32), ("l0a_wk", None, F32),
            ("l0a_wv", None, F32), ("l0a_wo", None, F32)],
    "l0f": [("l0f_w", (2048, 2048), F32), ("l0f_b", (2048,), F32),
            ("l0f_ts", (1, 1), F32)],
    "l1a": [("l1a_wq", None, F32), ("l1a_wk", None, F32),
            ("l1a_wv", None, F32), ("l1a_wo", None, F32),
            ("l1a_bq", (2048,), F32), ("l1a_bk", (2048,), F32),
            ("l1a_bv", (2048,), F32), ("l1a_bo", (2048,), F32)],
    "l1f": [("l1f_wg", wslab_shape(2048, 8, 8), F32),
            ("l1f_w1", wslab_shape(2048, 8192), F32),
            ("l1f_w3", wslab_shape(2048, 8192), F32),
            ("l1f_w2", wslab_shape(8192, 2048), F32),
            ("l1f_bias", (8,), F32)],
    "l2a": [("l2a_wq", None, F32),
            ("l2a_wk", wslab_shape(2048, 512, 512), F32),
            ("l2a_wv", wslab_shape(2048, 512, 512), F32),
            ("l2a_wo", None, F32)],
    "l2f": [("l2f_wg", wslab_shape(2048, 8192), I32),
            ("l2f_wu", wslab_shape(2048, 8192), I32),
            ("l2f_wd", wslab_shape(8192, 2048), I32),
            ("l2f_gs", (8192,), F32), ("l2f_us", (8192,), F32),
            ("l2f_ds", (2048,), F32)],
    "l3a": [("l3a_xproj", wslab_shape(2048, 160, 160), F32),
            ("l3a_dtw", wslab_shape(128, 2048), F32),
            ("l3a_dtb", (2048,), F32), ("l3a_alog", (2048, 16), F32),
            ("l3a_dvec", (128, 16), F32)],
    "l3f": [("l3f_rw", wslab_shape(2048, 8, 8), F32),
            ("l3f_w1", wslab_shape(2048, 4096, 512), F32),
            ("l3f_w3", wslab_shape(2048, 4096, 512), F32),
            ("l3f_w2", wslab_shape(4096, 2048), F32)],
}


def _declare_block_inputs(P, block):
    w = {}
    for name, shape, dt in _BLOCK_WDEFS[block]:
        if shape is None:
            shape = wslab_shape(2048, 2048, 512)
        short = name.split("_", 1)[1]
        w[short] = P.dram(name, shape, dt, kind="ExternalInput")
    return w


def _emit_block(ctx, block, xin, xout, w):
    if block == "l0a":
        emit_l0_attn(ctx, xin, xout, w, T_, D_, 16)
    elif block == "l0f":
        emit_l0_ffn(ctx, xin, xout, w["w"], w["b"], w["ts"], T_, D_)
    elif block == "l1a":
        emit_l1_attn(ctx, xin, xout, w, T_, D_, 16)
    elif block == "l1f":
        emit_l1_ffn(ctx, xin, xout, w, T_, D_, 8, 1024)
    elif block == "l2a":
        emit_l2_attn(ctx, xin, xout, w, T_, D_, 16, 4, 512)
    elif block == "l2f":
        emit_l2_ffn(ctx, xin, xout, w, T_, D_, 8192)
    elif block == "l3a":
        emit_l3_attn(ctx, xin, xout, w, T_, D_, 16, 128)
    elif block == "l3f":
        emit_l3_ffn(ctx, xin, xout, w, T_, D_, 8, 512)


BLOCKS = ("l0a", "l0f", "l1a", "l1f", "l2a", "l2f", "l3a", "l3f")

_prog_cache = {}


def _build_block_prog(block):
    if block in _prog_cache:
        return _prog_cache[block]
    nc = bass.Bass("TRN2", target_bir_lowering=False)
    P = Prog(nc)
    x_in = P.dram("x", (T_, D_), F32, kind="ExternalInput")
    y_out = P.dram("y", (T_, D_), F32, kind="ExternalOutput")
    w = _declare_block_inputs(P, block)
    ctx = Ctx(P, std_consts(T_))
    _emit_block(ctx, block, x_in, y_out, w)
    P.emit()
    _prog_cache[block] = (nc, ctx)
    return nc, ctx


def _build_full_prog(nblocks=8):
    key = ("full", nblocks)
    if key in _prog_cache:
        return _prog_cache[key]
    nc = bass.Bass("TRN2", target_bir_lowering=False)
    P = Prog(nc)
    x_in = P.dram("x", (T_, D_), F32, kind="ExternalInput")
    y_out = P.dram("y", (T_, D_), F32, kind="ExternalOutput")
    ctx = Ctx(P, std_consts(T_))
    resA = ctx.scratch("s_resA", (T_, D_), F32)
    resB = ctx.scratch("s_resB", (T_, D_), F32)
    chain = [x_in, resA, resB, resA, resB, resA, resB, resA]
    chain = chain[:nblocks] + [y_out]
    for i, block in enumerate(BLOCKS[:nblocks]):
        w = _declare_block_inputs(P, block)
        _emit_block(ctx, block, chain[i], chain[i + 1], w)
    P.emit()
    _prog_cache[key] = (nc, ctx)
    return nc, ctx


FUSED = True


def kernel(**inputs):
    x = np.ascontiguousarray(np.asarray(inputs["L0_attn_x"], np.float32))
    B = x.shape[0]
    w = _host_prep(inputs)
    shards = [np.ascontiguousarray(x[c]) for c in range(B)]
    if FUSED:
        nc, ctx = _build_full_prog()
        in_maps = []
        for c in range(B):
            im = {"x": shards[c]}
            for block in BLOCKS:
                for name, shape, dt in _BLOCK_WDEFS[block]:
                    im[name] = w[name]
            for n, a in ctx.const_arrays.items():
                im["c_" + n] = a
            in_maps.append(im)
        res = run_bass_kernel_spmd(nc, in_maps, core_ids=list(range(B)))
        return np.stack([res.results[c]["y"] for c in range(B)], 0)
    for block in BLOCKS:
        nc, ctx = _build_block_prog(block)
        in_maps = []
        for c in range(B):
            im = {"x": shards[c]}
            for name, shape, dt in _BLOCK_WDEFS[block]:
                im[name] = w[name]
            for n, a in ctx.const_arrays.items():
                im["c_" + n] = a
            in_maps.append(im)
        res = run_bass_kernel_spmd(nc, in_maps, core_ids=list(range(B)))
        shards = [res.results[c]["y"] for c in range(B)]
    return np.stack(shards, 0)
